# version 07
# baseline (speedup 1.0000x reference)
import math
import numpy as np
import ml_dtypes

import concourse.bass as bass
import concourse.mybir as mybir
from concourse.bass_utils import run_bass_kernel_spmd

F32 = mybir.dt.float32
BF16 = mybir.dt.bfloat16
I32 = mybir.dt.int32
AF = mybir.ActivationFunctionType
ALU = mybir.AluOpType

LN_EPS = 1e-5
B_MARGIN = 2.0
MAIN_SLOTS = [(0, jj) for jj in range(3)] + [(1, jj) for jj in range(7)]


class Cfg:
    def __init__(self, D, I, KV, T, n_fgroups, debug=False):
        self.D = D
        self.I = I
        self.KV = KV
        self.QH = 4 * KV
        self.HD = 128
        self.T = T
        self.QB = T // 2
        self.NH = T // 2
        assert self.NH <= 512 and self.QB % 128 == 0
        self.ND = D // 128
        self.NI = I // 128
        self.CD = 64 * KV
        self.NC = self.CD // 128
        self.TT = T // 128
        self.KT_C = self.QB // 128
        self.n_fgroups = n_fgroups
        base, rem = self.NI // n_fgroups, self.NI % n_fgroups
        sizes = [base + (1 if g_ < rem else 0) for g_ in range(n_fgroups)]
        self.fgroups = []
        s = 0
        for sz in sizes:
            self.fgroups.append(list(range(s, s + sz)))
            s += sz
        self.OG = min(4, self.ND)
        self.n_og = self.ND // self.OG
        self.CROWS = 2 * KV * 128 + KV
        self.GR = 4 * self.CROWS
        self.debug = debug
        self.wgu_cols = 2 * self.ND * 128


REAL = Cfg(D=4096, I=11008, KV=8, T=1024, n_fgroups=5)
SMALL = Cfg(D=512, I=512, KV=2, T=256, n_fgroups=2, debug=True)
MID = Cfg(D=512, I=512, KV=2, T=1024, n_fgroups=2, debug=True)
MID2 = Cfg(D=512, I=512, KV=8, T=1024, n_fgroups=2, debug=True)
MID3 = Cfg(D=4096, I=512, KV=8, T=1024, n_fgroups=2, debug=True)



def sigma_perm():
    return np.concatenate([np.arange(0, 32), np.arange(64, 96),
                           np.arange(32, 64), np.arange(96, 128)])


def host_tile_lhsT(W):
    O, K = W.shape
    no, nk = O // 128, K // 128
    Wb = W.reshape(no, 128, nk, 128)
    Wt = Wb.transpose(0, 3, 2, 1)
    return np.ascontiguousarray(Wt.reshape(O, K))


def host_tile_wd(Wd, cfg):
    Wb = Wd.reshape(cfg.n_og, cfg.OG, 128, cfg.NI, 128)
    Wt = Wb.transpose(0, 4, 3, 1, 2)
    return np.ascontiguousarray(Wt.reshape(cfg.n_og * 128, cfg.NI * cfg.OG * 128))


def host_rope_tables(positions, cfg):
    HD = cfg.HD
    inv = 1.0 / (10000.0 ** (np.arange(0, HD, 2, dtype=np.float64) / HD))
    fr = np.outer(inv, positions.astype(np.float64))
    cos = np.cos(fr).astype(np.float32)
    sin = np.sin(fr).astype(np.float32)
    rows_q = np.concatenate([np.arange(32), np.arange(32),
                             np.arange(32, 64), np.arange(32, 64)])
    rows_k = np.concatenate([np.arange(32), np.arange(32)] * 2)
    s = 1.0 / math.sqrt(HD)
    return (cos[rows_q] * s, sin[rows_q] * s, cos[rows_k], sin[rows_k])


def host_R_matrix():
    R = np.zeros((128, 128), np.float32)
    for b in (0, 64):
        for i in range(32):
            R[b + i, b + i + 32] = -1.0
            R[b + 32 + i, b + i] = 1.0
    return np.ascontiguousarray(R.T)


def host_masks(cfg):
    QB = cfg.QB
    out = np.zeros((128, 4 * QB), np.float32)
    q = np.arange(QB)
    k = np.arange(128)
    for kt in range(4):
        kk = min(kt, cfg.KT_C - 1)
        out[:, kt * QB:(kt + 1) * QB] = (q[None, :] >= (kk * 128 + k)[:, None])
    return out.astype(ml_dtypes.bfloat16)


def host_valid(core, cfg):
    c = core % 4
    own = {0: c, 1: 7 - c}
    v = np.zeros((1, 10 * cfg.QH), np.float32)
    for s, (qb, jj) in enumerate(MAIN_SLOTS):
        if jj < own[qb]:
            v[0, s * cfg.QH:(s + 1) * cfg.QH] = 1.0
    return v


def host_prepare(inputs, cfg):
    x = np.asarray(inputs["L0_attn_hidden_states"], np.float32)
    Wq = np.asarray(inputs["L0_attn_Wq"], np.float32)
    Wkr = np.asarray(inputs["L0_attn_Wkr"], np.float32)
    Wdk = np.asarray(inputs["L0_attn_Wdk"], np.float32)
    Wupk = np.asarray(inputs["L0_attn_Wupk"], np.float32)
    Wupv = np.asarray(inputs["L0_attn_Wupv"], np.float32)
    Wo = np.asarray(inputs["L0_attn_Wo"], np.float32)
    Wg = np.asarray(inputs["L0_ffn_w_gate"], np.float32)
    Wu = np.asarray(inputs["L0_ffn_w_up"], np.float32)
    Wd = np.asarray(inputs["L0_ffn_w_down"], np.float32)

    QB = cfg.QB
    sp = sigma_perm()
    qrows = np.concatenate([qh * 128 + sp for qh in range(cfg.QH)])
    wq_t = host_tile_lhsT(Wq[qrows])
    wo_t = host_tile_lhsT(Wo)
    wkr_t = host_tile_lhsT(Wkr)
    wdk_t = host_tile_lhsT(Wdk)
    wupk_t = host_tile_lhsT(Wupk)
    wupv_t = np.ascontiguousarray(Wupv.T)

    wg_t = host_tile_lhsT(Wg).reshape(cfg.NI, 128, cfg.ND * 128)
    wu_t = host_tile_lhsT(Wu).reshape(cfg.NI, 128, cfg.ND * 128)
    wgu_t = np.ascontiguousarray(
        np.concatenate([wg_t, wu_t], axis=2)).reshape(cfg.NI * 128, -1)
    wd_t = host_tile_wd(Wd, cfg)

    masks = host_masks(cfg)
    RT = host_R_matrix().astype(ml_dtypes.bfloat16)
    ones_seg = np.ones((128, 130), ml_dtypes.bfloat16)
    zrow = np.zeros((1, 128), ml_dtypes.bfloat16)

    in_maps = []
    for core in range(8):
        b, c = core // 4, core % 4
        toks = np.concatenate([np.arange(c * QB, (c + 1) * QB),
                               np.arange((7 - c) * QB, (8 - c) * QB)])
        xc = np.ascontiguousarray(x[b][toks].T)
        cosq, sinq, cosk, sink = host_rope_tables(toks, cfg)
        in_maps.append({
            "x_in": xc,
            "wq": wq_t, "wo": wo_t, "wkr": wkr_t, "wdk": wdk_t,
            "wupk": wupk_t, "wupv": wupv_t, "wgu": wgu_t, "wd": wd_t,
            "cosq": np.ascontiguousarray(cosq), "sinq": np.ascontiguousarray(sinq),
            "cosk": np.ascontiguousarray(cosk), "sink": np.ascontiguousarray(sink),
            "masks_in": masks, "rt_in": RT, "ones_in": ones_seg, "zrow_in": zrow,
            "valid_in": host_valid(core, cfg),
        })
    return in_maps


def host_assemble(results, cfg):
    out = np.zeros((2, 8 * cfg.QB, cfg.D), np.float32)
    for core in range(8):
        b, c = core // 4, core % 4
        o = results[core]["x_out"]
        out[b, c * cfg.QB:(c + 1) * cfg.QB] = o[:, 0:cfg.QB].T
        out[b, (7 - c) * cfg.QB:(8 - c) * cfg.QB] = o[:, cfg.QB:].T
    return out



ENGS = ("tensor", "scalar", "vector", "gpsimd", "sync")
DMA_RINGS = {"sync": ("sSY", 8), "scalar": ("sSC", 4), "gpsimd": ("sWD", 8)}
SEMS = (["sPE", "sACT", "sDVE", "sPOOL", "sCC"]
        + [f"{nm}{i}" for nm, k in DMA_RINGS.values() for i in range(k)])


class G:
    def __init__(self, nc, cfg, tn, sems):
        self.nc, self.cfg, self.t, self.sems = nc, cfg, tn, sems

    def start_walk(self, kind, eng, regctx):
        self.kind, self.eng = kind, eng
        self.cnt = {k: 0 for k in self.sems}
        self.regs = {}
        self.regctx = regctx
        self.dma_rr = {q: 0 for q in DMA_RINGS}

    def bump(self, sem, n):
        self.cnt[sem] += n
        return self.cnt[sem]

    def mark(self, sem):
        return (sem, self.cnt[sem])

    def _ap(self, instr, incs):
        for sem, n in incs:
            self.bump(sem, n)
            if instr is not None:
                instr.then_inc(self.sems[sem], n)

    def wait(self, kind, ev):
        sem, val = ev
        if val > 0 and self.kind == kind:
            self.eng.wait_ge(self.sems[sem], val)

    def on(self, kind):
        return self.kind == kind

    def mm(self, out, lhsT, rhs, start, stop, incs=(), skip=False):
        i = self.eng.matmul(out, lhsT, rhs, start=start, stop=stop,
                            skip_group_check=skip) if self.on("tensor") else None
        self._ap(i, incs)

    def act(self, out, in_, func, bias=0.0, scale=1.0, incs=()):
        i = self.eng.activation(out, in_, func, bias=bias, scale=scale) \
            if self.on("scalar") else None
        self._ap(i, incs)

    def acopy(self, out, in_, incs=()):
        i = self.eng.copy(out, in_) if self.on("scalar") else None
        self._ap(i, incs)

    def tt(self, out, a, b, op, incs=()):
        i = self.eng.tensor_tensor(out, a, b, op) if self.on("vector") else None
        self._ap(i, incs)

    def stt(self, out, in0, scalar, in1, op0, op1, incs=()):
        i = self.eng.scalar_tensor_tensor(out, in0, scalar, in1, op0, op1) \
            if self.on("vector") else None
        self._ap(i, incs)

    def ts(self, out, in0, s1, s2, op0, op1=ALU.mult, incs=()):
        if self.on("vector"):
            if s2 is None:
                i = self.eng.tensor_single_scalar(out, in0, s1, op0)
            else:
                i = self.eng.tensor_scalar(out, in0, s1, s2, op0, op1)
        else:
            i = None
        self._ap(i, incs)

    def ptt(self, out, a, b, op, incs=()):
        i = self.eng.tensor_tensor(out, a, b, op) if self.on("gpsimd") else None
        self._ap(i, incs)

    def memset(self, ap, v, incs=()):
        i = self.eng.memset(ap, v) if self.on("gpsimd") else None
        self._ap(i, incs)

    def vcopy(self, out, in_, incs=()):
        i = self.eng.tensor_copy(out, in_) if self.on("vector") else None
        self._ap(i, incs)

    def vrecip(self, out, in_, incs=()):
        i = self.eng.reciprocal(out, in_) if self.on("vector") else None
        self._ap(i, incs)

    def vreduce(self, out, in_, op, incs=()):
        i = self.eng.tensor_reduce(out, in_, mybir.AxisListType.X, op) \
            if self.on("vector") else None
        self._ap(i, incs)

    def dma(self, q, out, in_):
        nm, k = DMA_RINGS[q]
        phys = f"{nm}{self.dma_rr[q] % k}"
        self.dma_rr[q] += 1
        if self.cnt[phys] > 0:
            self.wait(q, (phys, self.cnt[phys]))
        i = self.eng.dma_start(out=out, in_=in_) if self.on(q) else None
        self._ap(i, [(phys, 16)])
        return (phys, self.cnt[phys])

    def qmarks(self, q):
        nm, k = DMA_RINGS[q]
        return [(f"{nm}{i}", self.cnt[f"{nm}{i}"]) for i in range(k)]

    def collective(self, kind, op, groups, ins, outs, incs=()):
        i = self.eng.collective_compute(kind, op, replica_groups=groups,
                                        ins=ins, outs=outs) \
            if self.on("gpsimd") else None
        self._ap(i, incs)

    def reg(self, kind, name):
        if self.kind != kind:
            return None
        if name not in self.regs:
            self.regs[name] = self.regctx.enter_context(self.eng.register(name))
        return self.regs[name]

    def reg_load(self, kind, r, src_ap):
        if self.kind == kind:
            self.eng.reg_load(r, src_ap)

    def rap(self, kind, tensor, reg, pattern):
        if self.kind == kind:
            return bass.AP(tensor, reg, pattern)
        return None

    def barrier(self, kinds=ENGS):
        sems = [s for s in self.sems if s != "sCC"]
        evs = [self.mark(s) for s in sems]
        for kind in kinds:
            for ev in evs:
                self.wait(kind, ev)



def emit_program(g: G):
    cfg, t, nc = g.cfg, g.t, g.nc
    D, T, ND, NI, KV, QH, QB, NC, NH = (cfg.D, cfg.T, cfg.ND, cfg.NI, cfg.KV,
                                        cfg.QH, cfg.QB, cfg.NC, cfg.NH)
    PS, AB, WB, CK, FA = t["PS"], t["AB"], t["WB"], t["CK"], t["F32A"]
    ABW = AB.shape[1]
    TAIL0 = max(ND, QH) * T

    def acc_flags(first_it, last_it, hh, nhh=2, step=None):
        st = step if step is not None else NH
        start = bool(first_it and (hh * st) % 512 == 0)
        stop = bool(last_it and (((hh + 1) * st) % 512 == 0 or hh == nhh - 1))
        return start, stop

    def xn_tile(c):
        return AB[:, c * T:(c + 1) * T]

    def inter_tile(i):
        return AB[:, TAIL0 + i * T:TAIL0 + (i + 1) * T]

    def qav_tile(qh):
        return AB[:, qh * T:(qh + 1) * T]

    def kt_tile(h):
        return AB[:, TAIL0 + h * T:TAIL0 + (h + 1) * T]

    def vt_tile(h):
        return AB[:, TAIL0 + (KV + h) * T:TAIL0 + (KV + h + 1) * T]

    fa_x0 = FA[:, 0:T]
    fa_x1 = FA[:, T:2 * T]
    fa_tmp = FA[:, 2 * T:3 * T]
    fa_xw = FA[:, 3 * T:4 * T]
    cos_q, sin_q = FA[:, 0:T], FA[:, T:2 * T]
    cos_k, sin_k = FA[:, 2 * T:3 * T], FA[:, 3 * T:4 * T]
    fa_rope = FA[:, 4 * T:5 * T]
    mbrb = CK[:, 0:2 * T]
    knorm = t["masks"][0:1, 0:2 * T]

    ones_col = t["ones"][:, 0:1]
    ones_row = t["ones"][0:1, 0:128]
    zrow = t["zrow"][0:1, 0:128]
    zcol = t["zrow"][0:1, 0:1]
    ones_qb_row = t["masks"][0:1, 0:QB]

    for nm, dst in (("rt_in", t["rt"][:, :]),
                    ("ones_in", t["ones"][:, :]), ("zrow_in", t["zrow"][:, :]),
                    ("valid_in", t["valid"][:, :])):
        g.dma("sync", dst, t[nm][:, :])
    for k in ENGS:
        for ev in g.qmarks("sync"):
            g.wait(k, ev)

    def emit_ln(x_dram, resident, chunk_evs=None, first_write_guard=None):
        act_evs = []
        pe_evs = []
        for c in range(ND):
            buf = fa_x0 if (c % 2 == 0) else fa_x1
            sqv = t["sq"][:, (c % 2) * T:(c % 2) * T + T]
            xbf = xn_tile(c)
            lq = "sync" if (c % 2 == 0) else "scalar"
            if chunk_evs is not None:
                g.wait(lq, chunk_evs[c])
            if c >= 2:
                g.wait(lq, act_evs[c - 2])
                g.wait("scalar", pe_evs[c - 2])
            ev_x = g.dma(lq, buf, x_dram[c * 128:(c + 1) * 128, :])
            g.wait("scalar", ev_x)
            if c == 0 and first_write_guard is not None:
                g.wait("scalar", first_write_guard)
            g.act(sqv, buf, AF.Square)
            g.acopy(xbf, buf, incs=[("sACT", 1)])
            ev_sq = g.mark("sACT")
            act_evs.append(ev_sq)
            g.wait("tensor", ev_sq)
            for hh in range(2):
                sl = slice(hh * NH, (hh + 1) * NH)
                fst, lst = acc_flags(c == 0, c == ND - 1, hh)
                g.mm(PS[0:1, hh * NH:(hh + 1) * NH], ones_col, xbf[:, sl],
                     start=fst, stop=lst)
                g.mm(PS[0:1, 1024 + hh * NH:1024 + (hh + 1) * NH], ones_col,
                     sqv[:, sl], start=fst, stop=lst,
                     incs=([("sPE", 1)] if hh == 1 else []))
            pe_evs.append(g.mark("sPE"))
        ev_stats = g.mark("sPE")
        g.wait("scalar", ev_stats)
        g.acopy(t["r_sum"][:, :], PS[0:1, 0:T])
        g.acopy(t["r_sumsq"][:, :], PS[0:1, 1024:1024 + T], incs=[("sACT", 1)])
        ev_rows = g.mark("sACT")
        g.wait("vector", ev_rows)
        g.ts(t["r_mean"][:, :], t["r_sum"][:, :], 1.0 / D, None, ALU.mult,
             incs=[("sDVE", 1)])
        g.wait("vector", g.mark("sDVE"))
        g.tt(t["r_rstd"][:, :], t["r_mean"][:, :], t["r_mean"][:, :], ALU.mult,
             incs=[("sDVE", 1)])
        g.wait("vector", g.mark("sDVE"))
        g.stt(t["r_var"][:, :], t["r_sumsq"][:, :], 1.0 / D, t["r_rstd"][:, :],
              ALU.mult, ALU.subtract, incs=[("sDVE", 1)])
        g.wait("vector", g.mark("sDVE"))
        g.ts(t["r_var"][:, :], t["r_var"][:, :], LN_EPS, None, ALU.add,
             incs=[("sDVE", 1)])
        ev_var = g.mark("sDVE")
        g.wait("scalar", ev_var)
        g.act(t["r_rstd"][:, :], t["r_var"][:, :], AF.Ln,
              incs=[("sACT", 1)])
        g.wait("scalar", g.mark("sACT"))
        g.act(t["r_rstd"][:, :], t["r_rstd"][:, :], AF.Exp, scale=-0.5,
              incs=[("sACT", 1)])
        g.wait("scalar", g.mark("sACT"))
        g.acopy(t["r_meanbf"][:, :], t["r_mean"][:, :])
        g.acopy(t["r_rstdbf"][:, :], t["r_rstd"][:, :], incs=[("sACT", 1)])
        ev_rbf = g.mark("sACT")
        g.wait("tensor", ev_rbf)
        for hh in range(2):
            sl = slice(hh * NH, (hh + 1) * NH)
            fst, lst = acc_flags(True, True, hh)
            g.mm(PS[:, 2048 + hh * NH:2048 + (hh + 1) * NH], ones_row,
                 t["r_meanbf"][0:1, sl], start=fst, stop=lst)
            g.mm(PS[:, 3072 + hh * NH:3072 + (hh + 1) * NH], ones_row,
                 t["r_rstdbf"][0:1, sl], start=fst, stop=lst,
                 incs=([("sPE", 1)] if hh == 1 else []))
        ev_bc = g.mark("sPE")
        g.wait("scalar", ev_bc)
        g.acopy(mbrb[:, 0:T], PS[:, 2048:2048 + T])
        g.acopy(mbrb[:, T:2 * T], PS[:, 3072:3072 + T], incs=[("sACT", 1)])
        ev_mb = g.mark("sACT")
        assert resident
        xn_evs = [None] * ND
        g.wait("vector", ev_mb)
        g.wait("gpsimd", ev_mb)
        for c in range(ND):
            eng = "vector" if (c % 2 == 0) else "gpsimd"
            sem = "sDVE" if (c % 2 == 0) else "sPOOL"
            op = g.tt if (c % 2 == 0) else g.ptt
            scratch = fa_tmp if (c % 2 == 0) else fa_xw
            g.wait(eng, pe_evs[c])
            if c >= 2:
                g.wait(eng, xn_evs[c - 2])
            op(scratch, xn_tile(c), mbrb[:, 0:T], ALU.subtract,
               incs=[(sem, 1)])
            g.wait(eng, g.mark(sem))
            op(xn_tile(c), scratch, mbrb[:, T:2 * T], ALU.mult,
               incs=[(sem, 1)])
            xn_evs[c] = g.mark(sem)
        return xn_evs

    STG_CAP = max(64, (NC * T) // 4)
    stgv = [CK[:, 0:2 * STG_CAP].bitcast(F32),
            CK[:, 2 * STG_CAP:4 * STG_CAP].bitcast(F32)]
    stg_war = {0: None, 1: None}
    stg_rr = [0]

    def hwload(dst_bf, dram_ap, cols, ev_free=None):
        if ev_free is not None:
            g.wait("scalar", ev_free)
        done = None
        off = 0
        while off < cols:
            n = min(STG_CAP, cols - off)
            b = stg_rr[0] % 2
            q = "sync" if b == 0 else "scalar"
            stg_rr[0] += 1
            if stg_war[b] is not None:
                g.wait(q, stg_war[b])
            ev_l = g.dma(q, stgv[b][:, 0:n], dram_ap[:, off:off + n])
            g.wait("scalar", ev_l)
            g.acopy(dst_bf[:, off:off + n], stgv[b][:, 0:n],
                    incs=[("sACT", 1)])
            done = g.mark("sACT")
            stg_war[b] = done
            off += n
        return done

    def emit_ffn(x_src, x_dst, x_src_evs=None):
        WSL = ND * 128
        silu_evs = {}
        up_evs = {}
        inter_done = {}
        chunk_write_evs = [None] * ND
        last_stt = {0: None, 1: None}
        last_wr2 = {0: None, 1: None}
        xw_bufs = (fa_xw, fa_tmp)
        prev_evac = None
        wrr = [0]
        slot_pe = {}

        def wload(dram_ap, cols):
            s = wrr[0] % 3
            hw = (wrr[0] % 2 == 1)
            wrr[0] += 1
            ev_free = slot_pe.get(s)
            if hw:
                ev = hwload(WB[:, s * WSL:s * WSL + cols], dram_ap, cols,
                            ev_free=ev_free)
            else:
                if ev_free is not None:
                    g.wait("gpsimd", ev_free)
                ev = g.dma("gpsimd", WB[:, s * WSL:s * WSL + cols], dram_ap)
            return s, ev

        for gi, group in enumerate(cfg.fgroups):
            ng = len(group)
            for idx, i in enumerate(group):
                sg, evg = wload(t["wgu"][i * 128:(i + 1) * 128, 0:WSL], WSL)
                su, evu = wload(t["wgu"][i * 128:(i + 1) * 128, WSL:2 * WSL], WSL)
                p = idx % 2
                g.wait("tensor", evg)
                g.wait("tensor", evu)
                if idx >= 2:
                    g.wait("tensor", silu_evs[(gi, idx - 2)])
                    g.wait("tensor", inter_done[(gi, idx - 2)])
                elif prev_evac is not None:
                    g.wait("tensor", prev_evac)
                gcol, ucol = p * 1024, 2048 + p * 1024
                gate_ev = None
                for (col, s) in ((gcol, sg), (ucol, su)):
                    for c in range(ND):
                        for hh in range(2):
                            sl = slice(hh * NH, (hh + 1) * NH)
                            last = (c == ND - 1)
                            fst, lst = acc_flags(c == 0, last, hh)
                            g.mm(PS[:, col + hh * NH:col + (hh + 1) * NH],
                                 WB[:, s * WSL + c * 128:s * WSL + (c + 1) * 128],
                                 xn_tile(c)[:, sl], start=fst, stop=lst,
                                 incs=([("sPE", 1)] if (last and hh == 1) else []))
                    if col == gcol:
                        gate_ev = g.mark("sPE")
                        slot_pe[sg] = gate_ev
                    else:
                        up_evs[(gi, idx)] = g.mark("sPE")
                        slot_pe[su] = up_evs[(gi, idx)]
                g.wait("scalar", gate_ev)
                sbuf = t["silu"][:, p * T:(p + 1) * T]
                g.act(sbuf, PS[:, gcol:gcol + T], AF.Silu, incs=[("sACT", 1)])
                silu_evs[(gi, idx)] = g.mark("sACT")
                g.wait("vector", silu_evs[(gi, idx)])
                g.wait("vector", up_evs[(gi, idx)])
                g.tt(inter_tile(idx), sbuf, PS[:, ucol:ucol + T], ALU.mult,
                     incs=[("sDVE", 1)])
                inter_done[(gi, idx)] = g.mark("sDVE")
            ev_ginter = g.mark("sDVE")

            nt = min(3, ng)
            bounds = []
            b0 = 0
            for tpart in range(nt):
                sz = (ng - b0 + (nt - tpart - 1)) // (nt - tpart)
                bounds.append((b0, b0 + sz))
                b0 += sz
            evac_by_o = {}
            xread_evs = {}
            for og in range(cfg.n_og):
                for o in range(cfg.OG):
                    ch = og * cfg.OG + o
                    par = ch % 2
                    buf = fa_x0 if par == 0 else fa_x1
                    src = x_dst if gi > 0 else x_src
                    if gi == 0 and x_src_evs is not None:
                        g.wait("scalar", x_src_evs[ch])
                    if chunk_write_evs[ch] is not None:
                        g.wait("scalar", chunk_write_evs[ch])
                    if last_stt[par] is not None:
                        g.wait("scalar", last_stt[par])
                    if o < 2:
                        xread_evs[(og, o)] = g.dma(
                            "scalar", buf, src[ch * 128:(ch + 1) * 128, :])
                third_ev = None
                for ti, (i0, i1) in enumerate(bounds):
                    w = (i1 - i0) * cfg.OG * 128
                    src0 = (group[0] + i0) * cfg.OG * 128
                    sd, evd = wload(t["wd"][og * 128:(og + 1) * 128,
                                            src0:src0 + w], w)
                    g.wait("tensor", evd)
                    if og == 0 and ti == 0:
                        g.wait("tensor", ev_ginter)
                        if prev_evac is not None:
                            g.wait("tensor", prev_evac)
                    nmm, tot = 0, (i1 - i0) * cfg.OG * 2
                    for ii in range(i0, i1):
                        for o in range(cfg.OG):
                            if ti == 0 and ii == i0 and og >= 1:
                                g.wait("tensor", evac_by_o[(og - 1, o)])
                            ocol = o * 1024
                            wcol = sd * WSL + ((ii - i0) * cfg.OG + o) * 128
                            for hh in range(2):
                                sl = slice(hh * NH, (hh + 1) * NH)
                                nmm += 1
                                fst, lst = acc_flags(ii == 0 and ti == 0,
                                                     ii == ng - 1, hh)
                                g.mm(PS[:, ocol + hh * NH:ocol + (hh + 1) * NH],
                                     WB[:, wcol:wcol + 128], inter_tile(ii)[:, sl],
                                     start=fst, stop=lst,
                                     incs=([("sPE", 1)] if nmm == tot else []))
                    third_ev = g.mark("sPE")
                    slot_pe[sd] = third_ev
                for o in range(cfg.OG):
                    ch = og * cfg.OG + o
                    par = ch % 2
                    buf = fa_x0 if par == 0 else fa_x1
                    src = x_dst if gi > 0 else x_src
                    if o >= 2:
                        if chunk_write_evs[ch] is not None:
                            g.wait("scalar", chunk_write_evs[ch])
                        if last_stt[par] is not None:
                            g.wait("scalar", last_stt[par])
                        xread_evs[(og, o)] = g.dma(
                            "scalar", buf, src[ch * 128:(ch + 1) * 128, :])
                    wpar = o % 2
                    xw = xw_bufs[wpar]
                    g.wait("vector", xread_evs[(og, o)])
                    g.wait("vector", third_ev)
                    if last_wr2[wpar] is not None:
                        g.wait("vector", last_wr2[wpar])
                    g.stt(xw, PS[:, o * 1024:o * 1024 + T], 0.5, buf,
                          ALU.mult, ALU.add, incs=[("sDVE", 1)])
                    ev_e = g.mark("sDVE")
                    evac_by_o[(og, o)] = ev_e
                    last_stt[par] = ev_e
                    g.wait("sync", ev_e)
                    chunk_write_evs[ch] = g.dma(
                        "sync", x_dst[ch * 128:(ch + 1) * 128, :], xw)
                    last_wr2[wpar] = chunk_write_evs[ch]
            prev_evac = evac_by_o[(cfg.n_og - 1, cfg.OG - 1)]
        return chunk_write_evs

    def emit_proj(wdram, oc_n, ct_n, rhs_tile, evac, slots=4, wcols=None,
                  dual=False):
        wcols = wcols if wcols is not None else ct_n * 128
        pe_done = {}
        war = {}
        for oc in range(oc_n):
            p = oc % 2
            ev_free = pe_done.get(oc - 2)
            wsrc = wdram[oc * 128:(oc + 1) * 128, 0:wcols]
            if dual and p == 1:
                ev_w = hwload(WB[:, p * wcols:(p + 1) * wcols], wsrc, wcols,
                              ev_free=ev_free)
            else:
                if ev_free is not None:
                    g.wait("gpsimd", ev_free)
                ev_w = g.dma("gpsimd", WB[:, p * wcols:(p + 1) * wcols], wsrc)
            g.wait("tensor", ev_w)
            slot = oc % slots
            if war.get(oc - slots) is not None:
                g.wait("tensor", war[oc - slots])
            for c in range(ct_n):
                for hh in range(2):
                    sl = slice(hh * NH, (hh + 1) * NH)
                    last = (c == ct_n - 1)
                    fst, lst = acc_flags(c == 0, last, hh)
                    g.mm(PS[:, slot * 1024 + hh * NH:slot * 1024 + (hh + 1) * NH],
                         WB[:, p * wcols + c * 128:p * wcols + (c + 1) * 128],
                         rhs_tile(c)[:, sl], start=fst, stop=lst,
                         incs=([("sPE", 1)] if (last and hh == 1) else []))
            pe_done[oc] = g.mark("sPE")
            war[oc] = evac(oc, slot, pe_done[oc])
        return g.mark("sDVE"), g.mark("sACT"), g.mark("sPE")

    xn_evs = emit_ln(t["x_in"], resident=True)
    g.wait("tensor", xn_evs[-1])
    g.wait("tensor", xn_evs[-2])
    if cfg.debug:
        for c in range(ND):
            g.wait("sync", xn_evs[c])
            g.dma("sync", t["dbg_xn1"][c * 128:(c + 1) * 128, :], xn_tile(c))
    g.barrier()
    ffn1_evs = emit_ffn(t["x_in"], t["x_out"], x_src_evs=None)
    if cfg.debug:
        for c in range(ND):
            g.wait("sync", ffn1_evs[c])
            g.dma("sync", t["dbg_x1"][c * 128:(c + 1) * 128, :],
                  t["x_out"][c * 128:(c + 1) * 128, :])

    g.barrier()
    xn2_evs = emit_ln(t["x_out"], resident=True, chunk_evs=ffn1_evs)
    g.barrier()
    for nm, dst in (("cosq", cos_q), ("sinq", sin_q), ("cosk", cos_k),
                    ("sink", sin_k)):
        g.dma("sync", dst, t[nm][:, :])
    for ev in g.qmarks("sync"):
        g.wait("vector", ev)
        g.wait("tensor", ev)

    def ckv_tile(cc):
        return CK[:, cc * T:(cc + 1) * T]

    stage1 = CK[:, NC * T:NC * T + T]
    stage2 = CK[:, NC * T + T:NC * T + 2 * T]

    def evac_copy_to(dst_fn):
        def f(oc, slot, ev_pe):
            g.wait("scalar", ev_pe)
            g.acopy(dst_fn(oc), PS[:, slot * 1024:slot * 1024 + T],
                    incs=[("sACT", 1)])
            return g.mark("sACT")
        return f

    emit_proj(t["wdk"], NC, ND, xn_tile, evac_copy_to(ckv_tile))
    g.barrier()

    def make_rope_evac(cos_t, sin_t, post):
        st = {"rot_war": {}, "stage_pe": {}, "stage_dve": {}, "ro": {}}

        def evac(oc, slot, ev_pe):
            g.wait("scalar", ev_pe)
            if oc >= 1:
                g.wait("scalar", st["stage_pe"][oc - 1])
                g.wait("scalar", st["stage_dve"][oc - 1])
            g.acopy(stage1, PS[:, slot * 1024:slot * 1024 + T], incs=[("sACT", 1)])
            ev_k = g.mark("sACT")
            srot = 2 + (oc % 2)
            g.wait("tensor", ev_k)
            if oc >= 2:
                g.wait("tensor", st["rot_war"][oc - 2])
            for hh in range(2):
                sl = slice(hh * NH, (hh + 1) * NH)
                fst, lst = acc_flags(True, True, hh)
                g.mm(PS[:, srot * 1024 + hh * NH:srot * 1024 + (hh + 1) * NH],
                     t["rt"][:, :], stage1[:, sl], start=fst, stop=lst,
                     incs=([("sPE", 1)] if hh == 1 else []))
            ev_rot = g.mark("sPE")
            st["stage_pe"][oc] = ev_rot
            g.wait("vector", ev_k)
            if oc >= 1:
                for e in st["ro"][oc - 1]:
                    g.wait("vector", e)
            g.tt(fa_rope, stage1, cos_t, ALU.mult)
            g.wait("vector", ev_rot)
            g.tt(stage2, PS[:, srot * 1024:srot * 1024 + T], sin_t, ALU.mult,
                 incs=[("sDVE", 1)])
            g.wait("vector", g.mark("sDVE"))
            g.tt(stage2, fa_rope, stage2, ALU.add, incs=[("sDVE", 1)])
            ev_ro = g.mark("sDVE")
            st["stage_dve"][oc] = ev_ro
            ev_fins = post(oc, ev_ro)
            st["rot_war"][oc] = ev_fins[0]
            st["ro"][oc] = ev_fins
            return ev_k
        return evac

    def post_kr(oc, ev_ro):
        g.wait("vector", ev_ro)
        for hh in range(2):
            h = oc * 2 + hh
            g.vcopy(kt_tile(h)[0:64, :], stage2[hh * 64:(hh + 1) * 64, :],
                    incs=([("sDVE", 1)] if hh == 1 else []))
        return [g.mark("sDVE")]

    emit_proj(t["wkr"], NC, ND, xn_tile, make_rope_evac(cos_k, sin_k, post_kr),
              slots=2)
    g.barrier()

    kc_state = {}

    def evac_kc(oc, slot, ev_pe):
        g.wait("scalar", ev_pe)
        if oc >= 1:
            g.wait("scalar", kc_state[oc - 1])
        g.acopy(stage1, PS[:, slot * 1024:slot * 1024 + T], incs=[("sACT", 1)])
        ev_k = g.mark("sACT")
        g.wait("vector", ev_k)
        for hh in range(2):
            h = oc * 2 + hh
            g.vcopy(kt_tile(h)[64:128, :], stage1[hh * 64:(hh + 1) * 64, :],
                    incs=([("sDVE", 1)] if hh == 1 else []))
        ev = g.mark("sDVE")
        kc_state[oc] = ev
        return ev

    emit_proj(t["wupk"], NC, NC, ckv_tile, evac_kc, wcols=NC * 128)
    ev_kt_done = g.mark("sDVE")
    g.barrier()

    KVHD = KV * 128
    wupv_sb = WB[:, 0:NC * KVHD]
    ev_wupv = g.dma(
        "gpsimd", wupv_sb.rearrange("p (c d) -> p c d", c=NC),
        t["wupv"].ap().rearrange("(c p) d -> p c d", p=128))
    g.wait("tensor", ev_wupv)
    step = min(NH, KVHD)
    vt_evs = {}
    for tt in range(cfg.TT):
        slot = tt % 4
        if tt >= 4:
            g.wait("tensor", vt_evs[tt - 4])
        nmm, tot = 0, NC * (KVHD // step)
        for cc in range(NC):
            for hh in range(KVHD // step):
                sl = slice(hh * step, (hh + 1) * step)
                nmm += 1
                fst, lst = acc_flags(cc == 0, cc == NC - 1, hh,
                                     nhh=KVHD // step, step=step)
                g.mm(PS[:, slot * 1024 + hh * step:slot * 1024 + (hh + 1) * step],
                     ckv_tile(cc)[:, tt * 128:(tt + 1) * 128],
                     wupv_sb[:, cc * KVHD + hh * step:cc * KVHD + (hh + 1) * step],
                     start=fst, stop=lst,
                     incs=([("sPE", 1)] if nmm == tot else []))
        ev_pe = g.mark("sPE")
        g.wait("scalar", ev_pe)
        for h in range(KV):
            g.acopy(vt_tile(h)[:, tt * 128:(tt + 1) * 128],
                    PS[:, slot * 1024 + h * 128:slot * 1024 + (h + 1) * 128],
                    incs=([("sACT", 1)] if h == KV - 1 else []))
        vt_evs[tt] = g.mark("sACT")
    ev_vt_done = g.mark("sACT")

    ev_kn = {}
    knorm_ship = {}
    for h in range(KV):
        sqv = t["sq"][:, (h % 2) * T:(h % 2) * T + T]
        g.wait("scalar", ev_kt_done)
        if h >= 2:
            g.wait("scalar", ev_kn[h - 2])
        g.act(sqv, kt_tile(h), AF.Square, incs=[("sACT", 1)])
        ev_sq = g.mark("sACT")
        g.wait("tensor", ev_sq)
        if h == 0:
            g.wait("tensor", ev_vt_done)
        for hh in range(2):
            sl = slice(hh * NH, (hh + 1) * NH)
            fst, lst = acc_flags(True, True, hh)
            g.mm(PS[0:1, (h % 4) * 1024 + hh * NH:(h % 4) * 1024 + (hh + 1) * NH],
                 ones_col, sqv[:, sl], start=fst, stop=lst,
                 incs=([("sPE", 1)] if hh == 1 else []))
        ev_mm = g.mark("sPE")
        ev_kn[h] = ev_mm
        g.wait("scalar", ev_mm)
        if h >= 2:
            g.wait("scalar", knorm_ship[h - 2])
        g.acopy(knorm[0:1, (h % 2) * T:(h % 2) * T + T],
                PS[0:1, (h % 4) * 1024:(h % 4) * 1024 + T], incs=[("sACT", 1)])
        ev_k = g.mark("sACT")
        g.wait("sync", ev_k)
        knorm_ship[h] = g.dma(
            "sync", t["contrib"][2 * KV * 128 + h:2 * KV * 128 + h + 1, :],
            knorm[0:1, (h % 2) * T:(h % 2) * T + T])

    g.wait("sync", ev_kt_done)
    for h in range(KV):
        g.dma("sync", t["contrib"][h * 128:(h + 1) * 128, :], kt_tile(h))
    g.wait("sync", ev_vt_done)
    for h in range(KV):
        g.dma("sync",
              t["contrib"][KV * 128 + h * 128:KV * 128 + (h + 1) * 128, :],
              vt_tile(h))
    for ev in g.qmarks("sync"):
        g.wait("gpsimd", ev)
    g.collective("AllGather", ALU.bypass, [[0, 1, 2, 3], [4, 5, 6, 7]],
                 [t["contrib"].ap().opt()], [t["gout"].ap().opt()],
                 incs=[("sCC", 1)])
    ev_cc = g.mark("sCC")
    for ev in g.qmarks("sync"):
        g.wait("sync", ev)
    ev_masks = g.dma("sync", t["masks"][:, :], t["masks_in"][:, :])
    g.wait("tensor", ev_masks)
    g.wait("vector", ev_masks)

    if cfg.debug:
        g.wait("sync", ev_kt_done)
        g.wait("sync", ev_vt_done)
        for h in range(KV):
            g.dma("sync", t["dbg_kt"][h * 128:(h + 1) * 128, :], kt_tile(h))
            g.dma("sync", t["dbg_vt"][h * 128:(h + 1) * 128, :], vt_tile(h))

    q_state = {}

    def post_q(oc, ev_ro):
        g.wait("sync", ev_ro)
        ev_store = g.dma("sync", t["qtild"][oc * 128:(oc + 1) * 128, :], stage2)
        g.wait("scalar", ev_ro)
        if oc >= 2:
            g.wait("scalar", q_state[oc - 2])
        sqv = t["sq"][:, (oc % 2) * T:(oc % 2) * T + T]
        g.act(sqv, stage2, AF.Square, incs=[("sACT", 1)])
        ev_sq = g.mark("sACT")
        srot = 2 + (oc % 2)
        g.wait("tensor", ev_sq)
        for hh in range(2):
            sl = slice(hh * NH, (hh + 1) * NH)
            fst, lst = acc_flags(True, True, hh)
            g.mm(PS[0:1, srot * 1024 + hh * NH:srot * 1024 + (hh + 1) * NH],
                 ones_col, sqv[:, sl], start=fst, stop=lst,
                 incs=([("sPE", 1)] if hh == 1 else []))
        ev_qn = g.mark("sPE")
        q_state[oc] = ev_qn
        g.wait("scalar", ev_qn)
        g.acopy(t["r_qn2"][:, :], PS[0:1, srot * 1024:srot * 1024 + T],
                incs=[("sACT", 1)])
        ev_r = g.mark("sACT")
        g.wait("vector", ev_r)
        for qb in range(2):
            g.vreduce(t["r_qmax2"][0:1, oc * 2 + qb:oc * 2 + qb + 1],
                      t["r_qn2"][0:1, qb * QB:(qb + 1) * QB], ALU.max,
                      incs=([("sDVE", 1)] if qb == 1 else []))
        return [g.mark("sDVE"), ev_store]

    emit_proj(t["wq"], QH, ND, xn_tile,
              make_rope_evac(cos_q, sin_q, post_q), slots=2, dual=True)
    ev_qproj = g.mark("sDVE")
    g.barrier()

    g.wait("sync", ev_cc)
    ev_ks = g.dma("sync", t["ksg2"][:, :],
                  bass.AP(t["gout"], 2 * KV * 128 * T,
                          [[cfg.CROWS * T, 4], [T, KV], [1, T]]))
    g.wait("vector", ev_ks)
    g.vreduce(t["r_kpm"][:, :], t["ksg2"][:, :], ALU.max, incs=[("sDVE", 1)])
    ev_kpm = g.mark("sDVE")
    g.wait("sync", ev_kpm)
    ev_kb = g.dma("sync", t["r_t1"][0:1, 24:24 + 4 * KV], t["r_kpm"][:, 0:1])
    g.wait("vector", ev_kb)
    g.tt(t["r_t1"][0:1, 56:56 + KV], t["r_t1"][0:1, 24:24 + KV],
         t["r_t1"][0:1, 24 + KV:24 + 2 * KV], ALU.max, incs=[("sDVE", 1)])
    g.wait("vector", g.mark("sDVE"))
    g.tt(t["r_t1"][0:1, 64:64 + KV], t["r_t1"][0:1, 24 + 2 * KV:24 + 3 * KV],
         t["r_t1"][0:1, 24 + 3 * KV:24 + 4 * KV], ALU.max, incs=[("sDVE", 1)])
    g.wait("vector", g.mark("sDVE"))
    g.tt(t["r_t1"][0:1, 8:8 + KV], t["r_t1"][0:1, 56:56 + KV],
         t["r_t1"][0:1, 64:64 + KV], ALU.max, incs=[("sDVE", 1)])
    ev_km = g.mark("sDVE")
    g.wait("scalar", ev_km)
    g.act(t["r_t1"][0:1, 0:KV], t["r_t1"][0:1, 8:8 + KV], AF.Ln,
          incs=[("sACT", 1)])
    ev_lnk0 = g.mark("sACT")
    g.wait("vector", ev_lnk0)
    g.ts(t["r_t1"][0:1, 16:16 + KV], t["r_t1"][0:1, 0:KV], 0.5, None, ALU.mult,
         incs=[("sDVE", 1)])
    ev_lnk = g.mark("sDVE")

    g.wait("scalar", ev_qproj)
    g.act(t["r_lnq"][:, :], t["r_qmax2"][:, :], AF.Ln, incs=[("sACT", 1)])
    ev_lnq = g.mark("sACT")
    g.wait("scalar", ev_lnq)
    g.wait("scalar", ev_lnk)
    for h in range(KV):
        g.act(t["r_Bv"][0:1, h * 8:(h + 1) * 8],
              t["r_lnq"][0:1, h * 8:(h + 1) * 8], AF.Exp, scale=0.5,
              bias=t["r_t1"][0:1, 16 + h:17 + h],
              incs=([("sACT", 1)] if h == KV - 1 else []))
    ev_b = g.mark("sACT")
    g.wait("vector", ev_b)
    g.ts(t["r_lnq"][:, :], t["r_Bv"][:, :], -1.0, -B_MARGIN, ALU.mult, ALU.add,
         incs=[("sDVE", 1)])
    ev_nb = g.mark("sDVE")
    g.wait("scalar", ev_nb)
    g.acopy(t["r_nbbf"][:, :], t["r_lnq"][:, :], incs=[("sACT", 1)])
    ev_nbf = g.mark("sACT")
    g.wait("tensor", ev_nbf)
    g.mm(PS[:, 2048:2048 + 2 * QH], ones_row, t["r_nbbf"][0:1, :],
         start=True, stop=True, incs=[("sPE", 1)])
    ev_nbc = g.mark("sPE")
    g.wait("scalar", ev_nbc)
    g.acopy(t["negB"][:, :], PS[:, 2048:2048 + 2 * QH], incs=[("sACT", 1)])
    ev_negB = g.mark("sACT")

    NSLOT = 3 + 7
    g.wait("vector", ev_nb)
    for s, (qb, jj) in enumerate(MAIN_SLOTS):
        g.vcopy(t["r_bx"][0:1, s * QH:(s + 1) * QH],
                bass.AP(t["r_lnq"], qb, [[2 * QH, 1], [2, QH]]),
                incs=([("sDVE", 1)] if s == NSLOT - 1 else []))
    g.wait("vector", g.mark("sDVE"))
    g.tt(t["r_bx"][:, :], t["r_bx"][:, :], t["valid"][:, :], ALU.mult,
         incs=[("sDVE", 1)])
    g.ts(t["r_bx2"][:, :], t["valid"][:, :], 1e9, -1e9, ALU.mult, ALU.add,
         incs=[("sDVE", 1)])
    g.wait("vector", g.mark("sDVE"))
    g.tt(t["r_bx"][:, :], t["r_bx"][:, :], t["r_bx2"][:, :], ALU.add,
         incs=[("sDVE", 1)])
    ev_bx = g.mark("sDVE")
    g.wait("scalar", ev_bx)
    g.acopy(t["r_bxbf"][:, :], t["r_bx"][:, :], incs=[("sACT", 1)])
    ev_bxbf = g.mark("sACT")
    g.wait("tensor", ev_bxbf)
    g.mm(PS[:, 2048:2048 + NSLOT * QH], ones_row, t["r_bxbf"][0:1, :],
         start=True, stop=True, incs=[("sPE", 1)])
    ev_bxm = g.mark("sPE")
    g.wait("scalar", ev_bxm)
    g.acopy(t["bias_x"][:, :], PS[:, 2048:2048 + NSLOT * QH], incs=[("sACT", 1)])
    ev_biasx = g.mark("sACT")

    g.barrier()
    for qh in range(QH):
        g.dma("sync", qav_tile(qh), t["qtild"][qh * 128:(qh + 1) * 128, :])
    for ev in g.qmarks("sync"):
        g.wait("tensor", ev)
    if cfg.debug:
        for qh in range(QH):
            g.dma("sync", t["dbg_q"][qh * 128:(qh + 1) * 128, :], qav_tile(qh))

    KT_C = cfg.KT_C
    NU = (3 + 7 + 2) * KT_C
    ev_unit_pe = {}
    ev_exp = {}
    ev_epi_dve = None
    ev_epi_act = None

    HSEG = 10 * T
    KTV = AB[:, TAIL0:TAIL0 + 2 * HSEG]

    def ktall(hp):
        return KTV[:, hp * HSEG: hp * HSEG + 4 * T]

    def vall(hp):
        return KTV[:, hp * HSEG + 4 * T: hp * HSEG + 8 * T]

    def ownkt(hp):
        return KTV[:, hp * HSEG + 8 * T: hp * HSEG + 9 * T]

    def ownvt(hp):
        return KTV[:, hp * HSEG + 9 * T: hp * HSEG + 10 * T]

    def chunk_col(jj, kt):
        rank = jj if jj < 4 else 7 - jj
        half = 0 if jj < 4 else 1
        return rank * T + half * QB + kt * 128

    ev_hload = {}
    ev_h_pe_done = {}

    def load_h(h):
        hp = h % 2
        if h >= 2:
            g.wait("sync", ev_h_pe_done[h - 2])
        else:
            g.wait("sync", ev_cc)
        evs = []
        for r in range(4):
            evs.append(g.dma("sync", ktall(hp)[:, r * T:(r + 1) * T],
                             t["gout"][r * cfg.CROWS + h * 128:
                                       r * cfg.CROWS + (h + 1) * 128, :]))
            evs.append(g.dma("sync", vall(hp)[:, r * T:(r + 1) * T],
                             t["gout"][r * cfg.CROWS + KV * 128 + h * 128:
                                       r * cfg.CROWS + KV * 128 + (h + 1) * 128, :]))
        evs.append(g.dma("sync", ownkt(hp),
                         t["contrib"][h * 128:(h + 1) * 128, :]))
        evs.append(g.dma("sync", ownvt(hp),
                         t["contrib"][KV * 128 + h * 128:
                                      KV * 128 + (h + 1) * 128, :]))
        ev_hload[h] = evs

    g.wait("scalar", ev_negB)
    g.wait("scalar", ev_biasx)
    load_h(0)
    load_h(1)

    esA = fa_rope
    esB = t["sq"][:, 0:2 * T].bitcast(F32)
    E = 2 if KT_C % 2 == 0 else 1
    NP = NU // E

    ev_exppair = {}
    ev_av = {}
    ev_esA = {}
    ev_esB = {}
    ev_epi_dve = None
    ev_epi_act = None
    ev_ese_prev = None

    for qh in range(QH):
        h = qh // 4
        hp = h % 2
        ev_ktls = ev_hload[h]
        if ev_epi_dve is not None:
            g.wait("tensor", ev_epi_dve)
            g.wait("gpsimd", ev_epi_dve)
        g.memset(esA[:, :], 0.0)
        g.memset(esB[:, :], 0.0, incs=[("sPOOL", 1)])
        ev_ms = g.mark("sPOOL")
        g.wait("vector", ev_ms)
        g.wait("gpsimd", ev_ms)
        for col in (2048, 2560):
            g.mm(PS[:, col:col + QB], zrow, ones_qb_row, start=True, stop=False,
                 skip=True, incs=([("sPE", 1)] if col == 2560 else []))
        ev_clear = g.mark("sPE")

        units = []
        for s, (qb, jj) in enumerate(MAIN_SLOTS):
            for kt in range(KT_C):
                units.append(("main", qb, s, chunk_col(jj, kt), kt))
        for qb in range(2):
            for kt in range(KT_C):
                units.append(("diag", qb, None, qb * QB + kt * 128, kt))

        def u_info(u):
            kind_u, qb, s, col, kt = units[u]
            lhs = (ktall(hp)[:, col:col + 128] if kind_u == "main"
                   else ownkt(hp)[:, col:col + 128])
            vlhs = (vall(hp)[:, col:col + 128] if kind_u == "main"
                    else ownvt(hp)[:, col:col + 128])
            if kind_u == "main":
                bias = t["bias_x"][:, s * QH + qh:s * QH + qh + 1]
            else:
                bias = t["negB"][:, qh * 2 + qb:qh * 2 + qb + 1]
            return kind_u, qb, col, kt, lhs, vlhs, bias

        def ebuf_of(u):
            p = u // E
            if E == 2:
                return t["eb"][:, (p % 2) * 2 * QB + (u % 2) * QB:
                               (p % 2) * 2 * QB + (u % 2) * QB + QB]
            return t["eb"][:, (u % 2) * QB:(u % 2) * QB + QB]

        ev_mask_done = {}

        def prev_pair(p, d):
            pp = p - d
            if pp >= 0:
                return (qh, pp)
            if qh > 0:
                return (qh - 1, NP + pp)
            return None

        def emit_av(u):
            kind_u, qb, col, kt, lhs, vlhs, bias = u_info(u)
            eb_u = ebuf_of(u)
            g.wait("tensor", ev_mask_done[(qh, u)])
            avcol = 2048 + qb * 512
            stop_acc = (u == 11 * KT_C - 1) or (u == NU - 1)
            g.mm(PS[:, avcol:avcol + QB], vlhs, eb_u,
                 start=False, stop=stop_acc, skip=True, incs=[("sPE", 1)])
            ev_av[(qh, u)] = g.mark("sPE")

        for p in range(NP):
            us = list(range(p * E, (p + 1) * E))
            for u in us:
                kind_u, qb, col, kt, lhs, vlhs, bias = u_info(u)
                if u == 0 and qh % 4 == 0:
                    for e_ in ev_ktls:
                        g.wait("tensor", e_)
                if u == us[0]:
                    pq = prev_pair(p, 2)
                    if pq is not None:
                        g.wait("tensor", ev_exppair[pq])
                    if p == 0:
                        g.wait("tensor", ev_clear)
                scol = (((p % 2) * 2 + (u % 2)) * 512 if E == 2
                        else (u % 2) * 512)
                g.mm(PS[:, scol:scol + QB], lhs,
                     qav_tile(qh)[:, qb * QB:(qb + 1) * QB],
                     start=True, stop=True,
                     incs=([("sPE", 1)] if u == us[-1] else []))
            ev_s = g.mark("sPE")
            if p > 0:
                for u in range((p - 1) * E, p * E):
                    emit_av(u)
            g.wait("scalar", ev_s)
            pq2 = prev_pair(p, 2)
            if pq2 is not None:
                qh2, pp2 = pq2
                for u2 in range(pp2 * E, (pp2 + 1) * E):
                    g.wait("scalar", ev_av[(qh2, u2)])
                    g.wait("scalar", ev_esA.get((qh2, u2), ("sDVE", 0)))
                    g.wait("scalar", ev_esB.get((qh2, u2), ("sPOOL", 0)))
            kind0, qb0_, col0, kt0, lhs0, vlhs0, bias0 = u_info(us[0])
            if E == 2:
                eb_grp = t["eb"][:, (p % 2) * 2 * QB:(p % 2) * 2 * QB + 2 * QB]
                base = (p % 2) * 1024
                g.act(eb_grp, PS[:, base:base + 512 + QB], AF.Exp, bias=bias0,
                      incs=[("sACT", 1)])
            else:
                scol = (us[0] % 2) * 512
                g.act(ebuf_of(us[0]), PS[:, scol:scol + QB], AF.Exp, bias=bias0,
                      incs=[("sACT", 1)])
            ev_e = g.mark("sACT")
            ev_exppair[(qh, p)] = ev_e
            for u in us:
                kind_u, qb, col, kt, lhs, vlhs, bias = u_info(u)
                eb_u = ebuf_of(u)
                if kind_u == "diag":
                    g.wait("vector", ev_e)
                    g.tt(eb_u, eb_u, t["masks"][:, kt * QB:(kt + 1) * QB],
                         ALU.mult, incs=[("sDVE", 1)])
                    ev_mask_done[(qh, u)] = g.mark("sDVE")
                else:
                    ev_mask_done[(qh, u)] = ev_e
                if u % 2 == 0:
                    g.wait("vector", ev_mask_done[(qh, u)])
                    g.wait("vector", g.mark("sDVE"))
                    g.tt(esA[:, qb * QB:(qb + 1) * QB],
                         esA[:, qb * QB:(qb + 1) * QB], eb_u, ALU.add,
                         incs=[("sDVE", 1)])
                    ev_esA[(qh, u)] = g.mark("sDVE")
                else:
                    g.wait("gpsimd", ev_mask_done[(qh, u)])
                    g.wait("gpsimd", g.mark("sPOOL"))
                    g.ptt(esB[:, qb * QB:(qb + 1) * QB],
                          esB[:, qb * QB:(qb + 1) * QB], eb_u, ALU.add,
                          incs=[("sPOOL", 1)])
                    ev_esB[(qh, u)] = g.mark("sPOOL")
        for u in range(NU - E, NU):
            emit_av(u)
        if qh % 4 == 3:
            ev_h_pe_done[h] = g.mark("sPE")
            if h + 2 < KV:
                load_h(h + 2)

        ev_last = ev_av[(qh, NU - 1)]
        g.wait("vector", ev_esB[(qh, NU - 1)])
        g.wait("vector", ev_esA[(qh, NU - 2)])
        g.tt(esA[:, :], esA[:, :], esB[:, :], ALU.add, incs=[("sDVE", 1)])
        ev_merge = g.mark("sDVE")
        g.wait("scalar", ev_merge)
        g.wait("scalar", ev_last)
        es_bf = t["eb"][:, 0:2 * QB]
        g.acopy(es_bf, esA[:, :], incs=[("sACT", 1)])
        ev_cast = g.mark("sACT")
        g.wait("tensor", ev_cast)
        for qb in range(2):
            g.mm(PS[0:1, 3072 + qb * 512:3072 + qb * 512 + QB], ones_col,
                 es_bf[:, qb * QB:(qb + 1) * QB], start=True, stop=True,
                 incs=([("sPE", 1)] if qb == 1 else []))
        ev_esmm = g.mark("sPE")
        g.wait("scalar", ev_esmm)
        g.acopy(t["r_esum"][0:1, 0:QB], PS[0:1, 3072:3072 + QB])
        g.acopy(t["r_esum"][0:1, QB:2 * QB], PS[0:1, 3584:3584 + QB],
                incs=[("sACT", 1)])
        ev_ese = g.mark("sACT")
        ev_ese_prev = ev_ese
        g.wait("vector", ev_ese)
        g.ts(t["r_esum"][:, :], t["r_esum"][:, :], 1e-30, None, ALU.add,
             incs=[("sDVE", 1)])
        g.wait("vector", g.mark("sDVE"))
        g.vrecip(t["r_recip"][:, :], t["r_esum"][:, :], incs=[("sDVE", 1)])
        ev_rc = g.mark("sDVE")
        g.wait("scalar", ev_rc)
        g.acopy(t["r_recipbf"][:, :], t["r_recip"][:, :], incs=[("sACT", 1)])
        ev_rcb = g.mark("sACT")
        g.wait("tensor", ev_rcb)
        if ev_epi_act is not None:
            g.wait("tensor", ev_epi_act)
        for qb in range(2):
            g.mm(PS[:, 3072 + qb * 512:3072 + qb * 512 + QB], ones_row,
                 t["r_recipbf"][0:1, qb * QB:(qb + 1) * QB], start=True, stop=True,
                 incs=([("sPE", 1)] if qb == 1 else []))
        ev_bcm = g.mark("sPE")
        g.wait("scalar", ev_bcm)
        for qb in range(2):
            g.acopy(t["recb"][:, qb * QB:(qb + 1) * QB],
                    PS[:, 3072 + qb * 512:3072 + qb * 512 + QB],
                    incs=([("sACT", 1)] if qb == 1 else []))
        ev_epi_act = g.mark("sACT")
        g.wait("vector", ev_epi_act)
        g.wait("vector", ev_last)
        for qb in range(2):
            g.tt(qav_tile(qh)[:, qb * QB:(qb + 1) * QB],
                 PS[:, 2048 + qb * 512:2048 + qb * 512 + QB],
                 t["recb"][:, qb * QB:(qb + 1) * QB], ALU.mult,
                 incs=([("sDVE", 1)] if qb == 1 else []))
        ev_epi_dve = g.mark("sDVE")

    if cfg.debug:
        g.wait("sync", ev_epi_dve)
        for qh in range(QH):
            g.dma("sync", t["dbg_av"][qh * 128:(qh + 1) * 128, :], qav_tile(qh))

    g.barrier()
    attn_out_evs = [None] * ND
    wo_last = {0: None, 1: None}
    wo_wr = None

    def evac_wo(oc, slot, ev_pe):
        nonlocal wo_wr
        par = oc % 2
        buf = fa_x0 if par == 0 else fa_x1
        if wo_last[par] is not None:
            g.wait("sync", wo_last[par])
        ev_x = g.dma("sync", buf, t["x_out"][oc * 128:(oc + 1) * 128, :])
        g.wait("vector", ev_x)
        g.wait("vector", ev_pe)
        if wo_wr is not None:
            g.wait("vector", wo_wr)
        g.tt(fa_xw, PS[:, slot * 1024:slot * 1024 + T], buf, ALU.add,
             incs=[("sDVE", 1)])
        ev_e = g.mark("sDVE")
        wo_last[par] = ev_e
        g.wait("sync", ev_e)
        attn_out_evs[oc] = g.dma(
            "sync", t["x_out"][oc * 128:(oc + 1) * 128, :], fa_xw)
        wo_wr = attn_out_evs[oc]
        return ev_e

    emit_proj(t["wo"], ND, QH, qav_tile, evac_wo, dual=True)
    if cfg.debug:
        for c in range(ND):
            g.wait("sync", attn_out_evs[c])
            g.dma("sync", t["dbg_x2"][c * 128:(c + 1) * 128, :],
                  t["x_out"][c * 128:(c + 1) * 128, :])

    g.barrier()
    ln3_guard = g.mark("sPE")
    xn3_evs = emit_ln(t["x_out"], resident=True, chunk_evs=attn_out_evs,
                      first_write_guard=ln3_guard)
    g.wait("tensor", xn3_evs[-1])
    g.wait("tensor", xn3_evs[-2])
    g.barrier(kinds=("sync", "gpsimd"))
    ffn2_evs = emit_ffn(t["x_out"], t["x_out"], x_src_evs=attn_out_evs)

    for ev in ffn2_evs:
        g.wait("sync", ev)
    g.barrier(kinds=("sync",))



def declare(nc, cfg):
    from contextlib import ExitStack
    ctx = ExitStack()
    t = {}
    D, T, ND, NI, KV, QH, QB, NC = (cfg.D, cfg.T, cfg.ND, cfg.NI, cfg.KV,
                                    cfg.QH, cfg.QB, cfg.NC)

    t["x_in"] = nc.dram_tensor("x_in", [D, T], F32, kind="ExternalInput")
    t["wq"] = nc.dram_tensor("wq", [QH * 128, D], F32, kind="ExternalInput")
    t["wo"] = nc.dram_tensor("wo", [D, QH * 128], F32, kind="ExternalInput")
    t["wkr"] = nc.dram_tensor("wkr", [cfg.CD, D], F32, kind="ExternalInput")
    t["wdk"] = nc.dram_tensor("wdk", [cfg.CD, D], F32, kind="ExternalInput")
    t["wupk"] = nc.dram_tensor("wupk", [cfg.CD, cfg.CD], F32, kind="ExternalInput")
    t["wupv"] = nc.dram_tensor("wupv", [cfg.CD, KV * 128], F32, kind="ExternalInput")
    t["wgu"] = nc.dram_tensor("wgu", [NI * 128, 2 * ND * 128], F32,
                              kind="ExternalInput")
    t["wd"] = nc.dram_tensor("wd", [cfg.n_og * 128, NI * cfg.OG * 128], F32,
                             kind="ExternalInput")
    for nm in ("cosq", "sinq", "cosk", "sink"):
        t[nm] = nc.dram_tensor(nm, [128, T], F32, kind="ExternalInput")
    t["masks_in"] = nc.dram_tensor("masks_in", [128, 4 * QB], BF16,
                                   kind="ExternalInput")
    t["rt_in"] = nc.dram_tensor("rt_in", [128, 128], BF16, kind="ExternalInput")
    t["ones_in"] = nc.dram_tensor("ones_in", [128, 130], BF16, kind="ExternalInput")
    t["zrow_in"] = nc.dram_tensor("zrow_in", [1, 128], BF16, kind="ExternalInput")
    t["valid_in"] = nc.dram_tensor("valid_in", [1, 10 * QH], F32,
                                   kind="ExternalInput")
    t["x_out"] = nc.dram_tensor("x_out", [D, T], F32, kind="ExternalOutput")

    t["xn2d"] = nc.dram_tensor("xn2d", [D, T], BF16)
    t["qtild"] = nc.dram_tensor("qtild", [QH * 128, T], BF16)
    t["contrib"] = nc.dram_tensor("contrib", [cfg.CROWS, T], BF16)
    t["gout"] = nc.dram_tensor("gout", [cfg.GR, T], BF16)

    if cfg.debug:
        t["dbg_xn1"] = nc.dram_tensor("dbg_xn1", [D, T], BF16, kind="ExternalOutput")
        t["dbg_x1"] = nc.dram_tensor("dbg_x1", [D, T], F32, kind="ExternalOutput")
        t["dbg_kt"] = nc.dram_tensor("dbg_kt", [KV * 128, T], BF16,
                                     kind="ExternalOutput")
        t["dbg_vt"] = nc.dram_tensor("dbg_vt", [KV * 128, T], BF16,
                                     kind="ExternalOutput")
        t["dbg_q"] = nc.dram_tensor("dbg_q", [QH * 128, T], BF16,
                                    kind="ExternalOutput")
        t["dbg_av"] = nc.dram_tensor("dbg_av", [QH * 128, T], BF16,
                                     kind="ExternalOutput")
        t["dbg_x2"] = nc.dram_tensor("dbg_x2", [D, T], F32, kind="ExternalOutput")

    def sb(name, shape, dtype):
        t[name] = ctx.enter_context(nc.sbuf_tensor(name, shape, dtype))

    max_g = max(len(gg) for gg in cfg.fgroups)
    sb("AB", [128, max(ND, QH) * T + max(max_g, 20) * T], BF16)
    wb_cols = max(3 * ND * 128,
                  2 * ((max_g + 1) // 2) * cfg.OG * 128,
                  2 * ND * 128, 2 * QH * 128, 2 * NC * 128)
    sb("WB", [128, wb_cols], BF16)
    sb("CK", [128, NC * T + 2 * T], BF16)
    sb("F32A", [128, 5 * T], F32)
    sb("masks", [128, 4 * QB], BF16)
    sb("rt", [128, 128], BF16)
    sb("ones", [128, 130], BF16)
    sb("zrow", [1, 128], BF16)
    sb("valid", [1, 10 * QH], F32)
    sb("eb", [128, 4 * QB], BF16)
    sb("recb", [128, 2 * QB], BF16)
    sb("negB", [128, 2 * QH], F32)
    sb("bias_x", [128, 10 * QH], F32)
    sb("silu", [128, 2 * T], BF16)
    sb("sq", [128, 2 * T], BF16)
    sb("ksg2", [4 * KV, T], BF16)
    sb("r_kpm", [4 * KV, 1], F32)
    for nm in ("r_sum", "r_sumsq", "r_rstd"):
        sb(nm, [1, T], F32)
    t["r_mean"] = t["r_sum"]
    t["r_var"] = t["r_sumsq"]
    t["r_qn2"] = t["r_rstd"]
    t["r_esum"] = t["r_sum"]
    t["r_recip"] = t["r_sumsq"]
    sb("r_meanbf", [1, T], BF16)
    sb("r_rstdbf", [1, T], BF16)
    sb("r_recipbf", [1, 2 * QB], BF16)
    sb("r_qmax2", [1, 2 * QH], F32)
    sb("r_lnq", [1, 2 * QH], F32)
    sb("r_nbbf", [1, 2 * QH], BF16)
    sb("r_bx", [1, 10 * QH], F32)
    sb("r_bx2", [1, 10 * QH], F32)
    sb("r_bxbf", [1, 10 * QH], BF16)
    sb("r_t1", [1, 96], F32)
    sb("r_Bv", [1, 2 * QH], F32)

    t["PS"] = ctx.enter_context(nc.psum_tensor("PS", [128, 4096], F32))

    sems = {}
    for nm in SEMS:
        sems[nm] = ctx.enter_context(nc.semaphore(nm))
    return t, sems, ctx


def build_program(cfg):
    from contextlib import ExitStack
    nc = bass.Bass("TRN2", target_bir_lowering=False)
    tn, sems, ctx = declare(nc, cfg)
    g = G(nc, cfg, tn, sems)
    with ctx:
        with nc.Block() as block:
            def run_walk(kind):
                def f(eng):
                    with ExitStack() as regctx:
                        g.start_walk(kind, eng, regctx)
                        emit_program(g)
                return f
            block.tensor(run_walk("tensor"))
            block.scalar(run_walk("scalar"))
            block.vector(run_walk("vector"))
            block.gpsimd(run_walk("gpsimd"))
            block.sync(run_walk("sync"))
    return nc


_CACHED = {}


def kernel(**inputs):
    cfg = REAL
    key = "real"
    if key not in _CACHED:
        _CACHED[key] = build_program(cfg)
    nc = _CACHED[key]
    in_maps = host_prepare(inputs, cfg)
    res = run_bass_kernel_spmd(nc, in_maps, core_ids=list(range(8)))
    return host_assemble(res.results, cfg)


# version 08
# speedup vs baseline: 1.0614x; 1.0614x over previous
import math
import numpy as np
import ml_dtypes

import concourse.bass as bass
import concourse.mybir as mybir
from concourse.bass_utils import run_bass_kernel_spmd

F32 = mybir.dt.float32
BF16 = mybir.dt.bfloat16
I32 = mybir.dt.int32
AF = mybir.ActivationFunctionType
ALU = mybir.AluOpType

LN_EPS = 1e-5
B_MARGIN = 2.0
MAIN_SLOTS = [(0, jj) for jj in range(3)] + [(1, jj) for jj in range(7)]


class Cfg:
    def __init__(self, D, I, KV, T, n_fgroups, debug=False):
        self.D = D
        self.I = I
        self.KV = KV
        self.QH = 4 * KV
        self.HD = 128
        self.T = T
        self.QB = T // 2
        self.NH = T // 2
        assert self.NH <= 512 and self.QB % 128 == 0
        self.ND = D // 128
        self.NI = I // 128
        self.CD = 64 * KV
        self.NC = self.CD // 128
        self.TT = T // 128
        self.KT_C = self.QB // 128
        self.n_fgroups = n_fgroups
        base, rem = self.NI // n_fgroups, self.NI % n_fgroups
        sizes = [base + (1 if g_ < rem else 0) for g_ in range(n_fgroups)]
        self.fgroups = []
        s = 0
        for sz in sizes:
            self.fgroups.append(list(range(s, s + sz)))
            s += sz
        self.OG = min(4, self.ND)
        self.n_og = self.ND // self.OG
        self.CROWS = 2 * KV * 128 + KV
        self.GR = 4 * self.CROWS
        self.debug = debug
        self.wgu_cols = 2 * self.ND * 128


REAL = Cfg(D=4096, I=11008, KV=8, T=1024, n_fgroups=5)
SMALL = Cfg(D=512, I=512, KV=2, T=256, n_fgroups=2, debug=True)
MID = Cfg(D=512, I=512, KV=2, T=1024, n_fgroups=2, debug=True)
MID2 = Cfg(D=512, I=512, KV=8, T=1024, n_fgroups=2, debug=True)
MID3 = Cfg(D=4096, I=512, KV=8, T=1024, n_fgroups=2, debug=True)



def sigma_perm():
    return np.concatenate([np.arange(0, 32), np.arange(64, 96),
                           np.arange(32, 64), np.arange(96, 128)])


def host_tile_lhsT(W):
    O, K = W.shape
    no, nk = O // 128, K // 128
    Wb = W.reshape(no, 128, nk, 128)
    Wt = Wb.transpose(0, 3, 2, 1)
    return np.ascontiguousarray(Wt.reshape(O, K))


def host_tile_wd(Wd, cfg):
    Wb = Wd.reshape(cfg.n_og, cfg.OG, 128, cfg.NI, 128)
    Wt = Wb.transpose(0, 4, 3, 1, 2)
    return np.ascontiguousarray(Wt.reshape(cfg.n_og * 128, cfg.NI * cfg.OG * 128))


def host_rope_tables(positions, cfg):
    HD = cfg.HD
    inv = 1.0 / (10000.0 ** (np.arange(0, HD, 2, dtype=np.float64) / HD))
    fr = np.outer(inv, positions.astype(np.float64))
    cos = np.cos(fr).astype(np.float32)
    sin = np.sin(fr).astype(np.float32)
    rows_q = np.concatenate([np.arange(32), np.arange(32),
                             np.arange(32, 64), np.arange(32, 64)])
    rows_k = np.concatenate([np.arange(32), np.arange(32)] * 2)
    s = 1.0 / math.sqrt(HD)
    return (cos[rows_q] * s, sin[rows_q] * s, cos[rows_k], sin[rows_k])


def host_R_matrix():
    R = np.zeros((128, 128), np.float32)
    for b in (0, 64):
        for i in range(32):
            R[b + i, b + i + 32] = -1.0
            R[b + 32 + i, b + i] = 1.0
    return np.ascontiguousarray(R.T)


def host_masks(cfg):
    QB = cfg.QB
    out = np.zeros((128, 4 * QB), np.float32)
    q = np.arange(QB)
    k = np.arange(128)
    for kt in range(4):
        kk = min(kt, cfg.KT_C - 1)
        out[:, kt * QB:(kt + 1) * QB] = (q[None, :] >= (kk * 128 + k)[:, None])
    return out.astype(ml_dtypes.bfloat16)


def host_valid(core, cfg):
    c = core % 4
    own = {0: c, 1: 7 - c}
    v = np.zeros((1, 10 * cfg.QH), np.float32)
    for s, (qb, jj) in enumerate(MAIN_SLOTS):
        if jj < own[qb]:
            v[0, s * cfg.QH:(s + 1) * cfg.QH] = 1.0
    return v


def host_prepare(inputs, cfg):
    x = np.asarray(inputs["L0_attn_hidden_states"], np.float32)
    Wq = np.asarray(inputs["L0_attn_Wq"], np.float32)
    Wkr = np.asarray(inputs["L0_attn_Wkr"], np.float32)
    Wdk = np.asarray(inputs["L0_attn_Wdk"], np.float32)
    Wupk = np.asarray(inputs["L0_attn_Wupk"], np.float32)
    Wupv = np.asarray(inputs["L0_attn_Wupv"], np.float32)
    Wo = np.asarray(inputs["L0_attn_Wo"], np.float32)
    Wg = np.asarray(inputs["L0_ffn_w_gate"], np.float32)
    Wu = np.asarray(inputs["L0_ffn_w_up"], np.float32)
    Wd = np.asarray(inputs["L0_ffn_w_down"], np.float32)

    QB = cfg.QB
    sp = sigma_perm()
    qrows = np.concatenate([qh * 128 + sp for qh in range(cfg.QH)])
    wq_t = host_tile_lhsT(Wq[qrows])
    wo_t = host_tile_lhsT(Wo)
    wkr_t = host_tile_lhsT(Wkr)
    wdk_t = host_tile_lhsT(Wdk)
    wupk_t = host_tile_lhsT(Wupk)
    wupv_t = np.ascontiguousarray(Wupv.T)

    wg_t = host_tile_lhsT(Wg).reshape(cfg.NI, 128, cfg.ND * 128)
    wu_t = host_tile_lhsT(Wu).reshape(cfg.NI, 128, cfg.ND * 128)
    wgu_t = np.ascontiguousarray(
        np.concatenate([wg_t, wu_t], axis=2)).reshape(cfg.NI * 128, -1)
    wd_t = host_tile_wd(Wd, cfg)

    masks = host_masks(cfg)
    RT = host_R_matrix().astype(ml_dtypes.bfloat16)
    ones_seg = np.ones((128, 130), ml_dtypes.bfloat16)
    zrow = np.zeros((1, 128), ml_dtypes.bfloat16)

    in_maps = []
    for core in range(8):
        b, c = core // 4, core % 4
        toks = np.concatenate([np.arange(c * QB, (c + 1) * QB),
                               np.arange((7 - c) * QB, (8 - c) * QB)])
        xc = np.ascontiguousarray(x[b][toks].T)
        cosq, sinq, cosk, sink = host_rope_tables(toks, cfg)
        in_maps.append({
            "x_in": xc,
            "wq": wq_t, "wo": wo_t, "wkr": wkr_t, "wdk": wdk_t,
            "wupk": wupk_t, "wupv": wupv_t, "wgu": wgu_t, "wd": wd_t,
            "cosq": np.ascontiguousarray(cosq), "sinq": np.ascontiguousarray(sinq),
            "cosk": np.ascontiguousarray(cosk), "sink": np.ascontiguousarray(sink),
            "masks_in": masks, "rt_in": RT, "ones_in": ones_seg, "zrow_in": zrow,
            "valid_in": host_valid(core, cfg),
        })
    return in_maps


def host_assemble(results, cfg):
    out = np.zeros((2, 8 * cfg.QB, cfg.D), np.float32)
    for core in range(8):
        b, c = core // 4, core % 4
        o = results[core]["x_out"]
        out[b, c * cfg.QB:(c + 1) * cfg.QB] = o[:, 0:cfg.QB].T
        out[b, (7 - c) * cfg.QB:(8 - c) * cfg.QB] = o[:, cfg.QB:].T
    return out



ENGS = ("tensor", "scalar", "vector", "gpsimd", "sync")
DMA_RINGS = {"sync": ("sSY", 8), "scalar": ("sSC", 4), "gpsimd": ("sWD", 8)}
SEMS = (["sPE", "sACT", "sDVE", "sPOOL", "sCC"]
        + [f"{nm}{i}" for nm, k in DMA_RINGS.values() for i in range(k)])


class G:
    def __init__(self, nc, cfg, tn, sems):
        self.nc, self.cfg, self.t, self.sems = nc, cfg, tn, sems

    def start_walk(self, kind, eng, regctx):
        self.kind, self.eng = kind, eng
        self.cnt = {k: 0 for k in self.sems}
        self.regs = {}
        self.regctx = regctx
        self.dma_rr = {q: 0 for q in DMA_RINGS}

    def bump(self, sem, n):
        self.cnt[sem] += n
        return self.cnt[sem]

    def mark(self, sem):
        return (sem, self.cnt[sem])

    def _ap(self, instr, incs):
        for sem, n in incs:
            self.bump(sem, n)
            if instr is not None:
                instr.then_inc(self.sems[sem], n)

    def wait(self, kind, ev):
        sem, val = ev
        if val > 0 and self.kind == kind:
            self.eng.wait_ge(self.sems[sem], val)

    def on(self, kind):
        return self.kind == kind

    def mm(self, out, lhsT, rhs, start, stop, incs=(), skip=False):
        i = self.eng.matmul(out, lhsT, rhs, start=start, stop=stop,
                            skip_group_check=skip) if self.on("tensor") else None
        self._ap(i, incs)

    def act(self, out, in_, func, bias=0.0, scale=1.0, incs=()):
        i = self.eng.activation(out, in_, func, bias=bias, scale=scale) \
            if self.on("scalar") else None
        self._ap(i, incs)

    def acopy(self, out, in_, incs=()):
        i = self.eng.copy(out, in_) if self.on("scalar") else None
        self._ap(i, incs)

    def tt(self, out, a, b, op, incs=()):
        i = self.eng.tensor_tensor(out, a, b, op) if self.on("vector") else None
        self._ap(i, incs)

    def stt(self, out, in0, scalar, in1, op0, op1, incs=()):
        i = self.eng.scalar_tensor_tensor(out, in0, scalar, in1, op0, op1) \
            if self.on("vector") else None
        self._ap(i, incs)

    def ts(self, out, in0, s1, s2, op0, op1=ALU.mult, incs=()):
        if self.on("vector"):
            if s2 is None:
                i = self.eng.tensor_single_scalar(out, in0, s1, op0)
            else:
                i = self.eng.tensor_scalar(out, in0, s1, s2, op0, op1)
        else:
            i = None
        self._ap(i, incs)

    def ptt(self, out, a, b, op, incs=()):
        i = self.eng.tensor_tensor(out, a, b, op) if self.on("gpsimd") else None
        self._ap(i, incs)

    def memset(self, ap, v, incs=()):
        i = self.eng.memset(ap, v) if self.on("gpsimd") else None
        self._ap(i, incs)

    def vcopy(self, out, in_, incs=()):
        i = self.eng.tensor_copy(out, in_) if self.on("vector") else None
        self._ap(i, incs)

    def vrecip(self, out, in_, incs=()):
        i = self.eng.reciprocal(out, in_) if self.on("vector") else None
        self._ap(i, incs)

    def vreduce(self, out, in_, op, incs=()):
        i = self.eng.tensor_reduce(out, in_, mybir.AxisListType.X, op) \
            if self.on("vector") else None
        self._ap(i, incs)

    def dma(self, q, out, in_):
        nm, k = DMA_RINGS[q]
        phys = f"{nm}{self.dma_rr[q] % k}"
        self.dma_rr[q] += 1
        if self.cnt[phys] > 0:
            self.wait(q, (phys, self.cnt[phys]))
        i = self.eng.dma_start(out=out, in_=in_) if self.on(q) else None
        self._ap(i, [(phys, 16)])
        return (phys, self.cnt[phys])

    def qmarks(self, q):
        nm, k = DMA_RINGS[q]
        return [(f"{nm}{i}", self.cnt[f"{nm}{i}"]) for i in range(k)]

    def collective(self, kind, op, groups, ins, outs, incs=()):
        i = self.eng.collective_compute(kind, op, replica_groups=groups,
                                        ins=ins, outs=outs) \
            if self.on("gpsimd") else None
        self._ap(i, incs)

    def reg(self, kind, name):
        if self.kind != kind:
            return None
        if name not in self.regs:
            self.regs[name] = self.regctx.enter_context(self.eng.register(name))
        return self.regs[name]

    def reg_load(self, kind, r, src_ap):
        if self.kind == kind:
            self.eng.reg_load(r, src_ap)

    def rap(self, kind, tensor, reg, pattern):
        if self.kind == kind:
            return bass.AP(tensor, reg, pattern)
        return None

    def barrier(self, kinds=ENGS):
        sems = [s for s in self.sems if s != "sCC"]
        evs = [self.mark(s) for s in sems]
        for kind in kinds:
            for ev in evs:
                self.wait(kind, ev)



def emit_program(g: G):
    cfg, t, nc = g.cfg, g.t, g.nc
    D, T, ND, NI, KV, QH, QB, NC, NH = (cfg.D, cfg.T, cfg.ND, cfg.NI, cfg.KV,
                                        cfg.QH, cfg.QB, cfg.NC, cfg.NH)
    PS, AB, WB, CK, FA = t["PS"], t["AB"], t["WB"], t["CK"], t["F32A"]
    ABW = AB.shape[1]
    TAIL0 = max(ND, QH) * T

    def acc_flags(first_it, last_it, hh, nhh=2, step=None):
        st = step if step is not None else NH
        start = bool(first_it and (hh * st) % 512 == 0)
        stop = bool(last_it and (((hh + 1) * st) % 512 == 0 or hh == nhh - 1))
        return start, stop

    def xn_tile(c):
        return AB[:, c * T:(c + 1) * T]

    def inter_tile(i):
        return AB[:, TAIL0 + i * T:TAIL0 + (i + 1) * T]

    def qav_tile(qh):
        return AB[:, qh * T:(qh + 1) * T]

    def kt_tile(h):
        return AB[:, TAIL0 + h * T:TAIL0 + (h + 1) * T]

    def vt_tile(h):
        return AB[:, TAIL0 + (KV + h) * T:TAIL0 + (KV + h + 1) * T]

    fa_x0 = FA[:, 0:T]
    fa_x1 = FA[:, T:2 * T]
    fa_tmp = FA[:, 2 * T:3 * T]
    fa_xw = FA[:, 3 * T:4 * T]
    cos_q, sin_q = FA[:, 0:T], FA[:, T:2 * T]
    cos_k, sin_k = FA[:, 2 * T:3 * T], FA[:, 3 * T:4 * T]
    fa_rope = FA[:, 4 * T:5 * T]
    mbrb = CK[:, 0:2 * T]
    knorm = t["masks"][0:1, 0:2 * T]

    ones_col = t["ones"][:, 0:1]
    ones_row = t["ones"][0:1, 0:128]
    zrow = t["zrow"][0:1, 0:128]
    zcol = t["zrow"][0:1, 0:1]
    ones_qb_row = t["masks"][0:1, 0:QB]

    for nm, dst in (("rt_in", t["rt"][:, :]),
                    ("ones_in", t["ones"][:, :]), ("zrow_in", t["zrow"][:, :]),
                    ("valid_in", t["valid"][:, :])):
        g.dma("sync", dst, t[nm][:, :])
    for k in ENGS:
        for ev in g.qmarks("sync"):
            g.wait(k, ev)

    def emit_ln(x_dram, resident, chunk_evs=None, first_write_guard=None):
        act_evs = []
        pe_evs = []
        for c in range(ND):
            buf = fa_x0 if (c % 2 == 0) else fa_x1
            sqv = t["sq"][:, (c % 2) * T:(c % 2) * T + T]
            xbf = xn_tile(c)
            lq = "sync" if (c % 2 == 0) else "scalar"
            if chunk_evs is not None:
                g.wait(lq, chunk_evs[c])
            if c >= 2:
                g.wait(lq, act_evs[c - 2])
                g.wait("scalar", pe_evs[c - 2])
            ev_x = g.dma(lq, buf, x_dram[c * 128:(c + 1) * 128, :])
            g.wait("scalar", ev_x)
            if c == 0 and first_write_guard is not None:
                g.wait("scalar", first_write_guard)
            g.act(sqv, buf, AF.Square)
            g.acopy(xbf, buf, incs=[("sACT", 1)])
            ev_sq = g.mark("sACT")
            act_evs.append(ev_sq)
            g.wait("tensor", ev_sq)
            for hh in range(2):
                sl = slice(hh * NH, (hh + 1) * NH)
                fst, lst = acc_flags(c == 0, c == ND - 1, hh)
                g.mm(PS[0:1, hh * NH:(hh + 1) * NH], ones_col, xbf[:, sl],
                     start=fst, stop=lst)
                g.mm(PS[0:1, 1024 + hh * NH:1024 + (hh + 1) * NH], ones_col,
                     sqv[:, sl], start=fst, stop=lst,
                     incs=([("sPE", 1)] if hh == 1 else []))
            pe_evs.append(g.mark("sPE"))
        ev_stats = g.mark("sPE")
        g.wait("scalar", ev_stats)
        g.acopy(t["r_sum"][:, :], PS[0:1, 0:T])
        g.acopy(t["r_sumsq"][:, :], PS[0:1, 1024:1024 + T], incs=[("sACT", 1)])
        ev_rows = g.mark("sACT")
        g.wait("vector", ev_rows)
        g.ts(t["r_mean"][:, :], t["r_sum"][:, :], 1.0 / D, None, ALU.mult,
             incs=[("sDVE", 1)])
        g.wait("vector", g.mark("sDVE"))
        g.tt(t["r_rstd"][:, :], t["r_mean"][:, :], t["r_mean"][:, :], ALU.mult,
             incs=[("sDVE", 1)])
        g.wait("vector", g.mark("sDVE"))
        g.stt(t["r_var"][:, :], t["r_sumsq"][:, :], 1.0 / D, t["r_rstd"][:, :],
              ALU.mult, ALU.subtract, incs=[("sDVE", 1)])
        g.wait("vector", g.mark("sDVE"))
        g.ts(t["r_var"][:, :], t["r_var"][:, :], LN_EPS, None, ALU.add,
             incs=[("sDVE", 1)])
        ev_var = g.mark("sDVE")
        g.wait("scalar", ev_var)
        g.act(t["r_rstd"][:, :], t["r_var"][:, :], AF.Ln,
              incs=[("sACT", 1)])
        g.wait("scalar", g.mark("sACT"))
        g.act(t["r_rstd"][:, :], t["r_rstd"][:, :], AF.Exp, scale=-0.5,
              incs=[("sACT", 1)])
        g.wait("scalar", g.mark("sACT"))
        g.acopy(t["r_meanbf"][:, :], t["r_mean"][:, :])
        g.acopy(t["r_rstdbf"][:, :], t["r_rstd"][:, :], incs=[("sACT", 1)])
        ev_rbf = g.mark("sACT")
        g.wait("tensor", ev_rbf)
        for hh in range(2):
            sl = slice(hh * NH, (hh + 1) * NH)
            fst, lst = acc_flags(True, True, hh)
            g.mm(PS[:, 2048 + hh * NH:2048 + (hh + 1) * NH], ones_row,
                 t["r_meanbf"][0:1, sl], start=fst, stop=lst)
            g.mm(PS[:, 3072 + hh * NH:3072 + (hh + 1) * NH], ones_row,
                 t["r_rstdbf"][0:1, sl], start=fst, stop=lst,
                 incs=([("sPE", 1)] if hh == 1 else []))
        ev_bc = g.mark("sPE")
        g.wait("scalar", ev_bc)
        g.acopy(mbrb[:, 0:T], PS[:, 2048:2048 + T])
        g.acopy(mbrb[:, T:2 * T], PS[:, 3072:3072 + T], incs=[("sACT", 1)])
        ev_mb = g.mark("sACT")
        assert resident
        xn_evs = [None] * ND
        g.wait("vector", ev_mb)
        g.wait("gpsimd", ev_mb)
        for c in range(ND):
            eng = "vector" if (c % 2 == 0) else "gpsimd"
            sem = "sDVE" if (c % 2 == 0) else "sPOOL"
            op = g.tt if (c % 2 == 0) else g.ptt
            scratch = fa_tmp if (c % 2 == 0) else fa_xw
            g.wait(eng, pe_evs[c])
            if c >= 2:
                g.wait(eng, xn_evs[c - 2])
            op(scratch, xn_tile(c), mbrb[:, 0:T], ALU.subtract,
               incs=[(sem, 1)])
            g.wait(eng, g.mark(sem))
            op(xn_tile(c), scratch, mbrb[:, T:2 * T], ALU.mult,
               incs=[(sem, 1)])
            xn_evs[c] = g.mark(sem)
        return xn_evs

    STG_CAP = max(64, (NC * T) // 4)
    stgv = [CK[:, 0:2 * STG_CAP].bitcast(F32),
            CK[:, 2 * STG_CAP:4 * STG_CAP].bitcast(F32)]
    stg_war = {0: None, 1: None}
    stg_rr = [0]

    def hwload(dst_bf, dram_ap, cols, ev_free=None):
        if ev_free is not None:
            g.wait("scalar", ev_free)
        done = None
        off = 0
        while off < cols:
            n = min(STG_CAP, cols - off)
            b = stg_rr[0] % 2
            q = "sync" if b == 0 else "scalar"
            stg_rr[0] += 1
            if stg_war[b] is not None:
                g.wait(q, stg_war[b])
            ev_l = g.dma(q, stgv[b][:, 0:n], dram_ap[:, off:off + n])
            g.wait("scalar", ev_l)
            g.acopy(dst_bf[:, off:off + n], stgv[b][:, 0:n],
                    incs=[("sACT", 1)])
            done = g.mark("sACT")
            stg_war[b] = done
            off += n
        return done

    def emit_ffn(x_src, x_dst, x_src_evs=None):
        WSL = ND * 128
        silu_evs = {}
        up_evs = {}
        inter_done = {}
        chunk_write_evs = [None] * ND
        last_stt = {0: None, 1: None}
        last_wr2 = {0: None, 1: None}
        xw_bufs = (fa_xw, fa_tmp)
        prev_evac = None
        wrr = [0]
        slot_pe = {}

        def wload(dram_ap, cols, hw=False):
            s = wrr[0] % 3
            wrr[0] += 1
            ev_free = slot_pe.get(s)
            if hw:
                ev = hwload(WB[:, s * WSL:s * WSL + cols], dram_ap, cols,
                            ev_free=ev_free)
            else:
                if ev_free is not None:
                    g.wait("gpsimd", ev_free)
                ev = g.dma("gpsimd", WB[:, s * WSL:s * WSL + cols], dram_ap)
            return s, ev

        for gi, group in enumerate(cfg.fgroups):
            ng = len(group)
            for idx, i in enumerate(group):
                sg, evg = wload(t["wgu"][i * 128:(i + 1) * 128, 0:WSL], WSL)
                su, evu = wload(t["wgu"][i * 128:(i + 1) * 128, WSL:2 * WSL], WSL)
                p = idx % 2
                g.wait("tensor", evg)
                g.wait("tensor", evu)
                if idx >= 2:
                    g.wait("tensor", silu_evs[(gi, idx - 2)])
                    g.wait("tensor", inter_done[(gi, idx - 2)])
                elif prev_evac is not None:
                    g.wait("tensor", prev_evac)
                gcol, ucol = p * 1024, 2048 + p * 1024
                gate_ev = None
                for (col, s) in ((gcol, sg), (ucol, su)):
                    for c in range(ND):
                        for hh in range(2):
                            sl = slice(hh * NH, (hh + 1) * NH)
                            last = (c == ND - 1)
                            fst, lst = acc_flags(c == 0, last, hh)
                            g.mm(PS[:, col + hh * NH:col + (hh + 1) * NH],
                                 WB[:, s * WSL + c * 128:s * WSL + (c + 1) * 128],
                                 xn_tile(c)[:, sl], start=fst, stop=lst,
                                 incs=([("sPE", 1)] if (last and hh == 1) else []))
                    if col == gcol:
                        gate_ev = g.mark("sPE")
                        slot_pe[sg] = gate_ev
                    else:
                        up_evs[(gi, idx)] = g.mark("sPE")
                        slot_pe[su] = up_evs[(gi, idx)]
                g.wait("scalar", gate_ev)
                sbuf = t["silu"][:, p * T:(p + 1) * T]
                g.act(sbuf, PS[:, gcol:gcol + T], AF.Silu, incs=[("sACT", 1)])
                silu_evs[(gi, idx)] = g.mark("sACT")
                g.wait("vector", silu_evs[(gi, idx)])
                g.wait("vector", up_evs[(gi, idx)])
                g.tt(inter_tile(idx), sbuf, PS[:, ucol:ucol + T], ALU.mult,
                     incs=[("sDVE", 1)])
                inter_done[(gi, idx)] = g.mark("sDVE")
            ev_ginter = g.mark("sDVE")

            nt = min(3, ng)
            bounds = []
            b0 = 0
            for tpart in range(nt):
                sz = (ng - b0 + (nt - tpart - 1)) // (nt - tpart)
                bounds.append((b0, b0 + sz))
                b0 += sz
            evac_by_o = {}
            xread_evs = {}
            for og in range(cfg.n_og):
                for o in range(cfg.OG):
                    ch = og * cfg.OG + o
                    par = ch % 2
                    buf = fa_x0 if par == 0 else fa_x1
                    src = x_dst if gi > 0 else x_src
                    if gi == 0 and x_src_evs is not None:
                        g.wait("scalar", x_src_evs[ch])
                    if chunk_write_evs[ch] is not None:
                        g.wait("scalar", chunk_write_evs[ch])
                    if last_stt[par] is not None:
                        g.wait("scalar", last_stt[par])
                    if o < 2:
                        xread_evs[(og, o)] = g.dma(
                            "scalar", buf, src[ch * 128:(ch + 1) * 128, :])
                third_ev = None
                for ti, (i0, i1) in enumerate(bounds):
                    w = (i1 - i0) * cfg.OG * 128
                    src0 = (group[0] + i0) * cfg.OG * 128
                    sd, evd = wload(t["wd"][og * 128:(og + 1) * 128,
                                            src0:src0 + w], w, hw=(ti == 1))
                    g.wait("tensor", evd)
                    if og == 0 and ti == 0:
                        g.wait("tensor", ev_ginter)
                        if prev_evac is not None:
                            g.wait("tensor", prev_evac)
                    nmm, tot = 0, (i1 - i0) * cfg.OG * 2
                    for ii in range(i0, i1):
                        for o in range(cfg.OG):
                            if ti == 0 and ii == i0 and og >= 1:
                                g.wait("tensor", evac_by_o[(og - 1, o)])
                            ocol = o * 1024
                            wcol = sd * WSL + ((ii - i0) * cfg.OG + o) * 128
                            for hh in range(2):
                                sl = slice(hh * NH, (hh + 1) * NH)
                                nmm += 1
                                fst, lst = acc_flags(ii == 0 and ti == 0,
                                                     ii == ng - 1, hh)
                                g.mm(PS[:, ocol + hh * NH:ocol + (hh + 1) * NH],
                                     WB[:, wcol:wcol + 128], inter_tile(ii)[:, sl],
                                     start=fst, stop=lst,
                                     incs=([("sPE", 1)] if nmm == tot else []))
                    third_ev = g.mark("sPE")
                    slot_pe[sd] = third_ev
                for o in range(cfg.OG):
                    ch = og * cfg.OG + o
                    par = ch % 2
                    buf = fa_x0 if par == 0 else fa_x1
                    src = x_dst if gi > 0 else x_src
                    if o >= 2:
                        if chunk_write_evs[ch] is not None:
                            g.wait("scalar", chunk_write_evs[ch])
                        if last_stt[par] is not None:
                            g.wait("scalar", last_stt[par])
                        xread_evs[(og, o)] = g.dma(
                            "scalar", buf, src[ch * 128:(ch + 1) * 128, :])
                    wpar = o % 2
                    xw = xw_bufs[wpar]
                    g.wait("vector", xread_evs[(og, o)])
                    g.wait("vector", third_ev)
                    if last_wr2[wpar] is not None:
                        g.wait("vector", last_wr2[wpar])
                    g.stt(xw, PS[:, o * 1024:o * 1024 + T], 0.5, buf,
                          ALU.mult, ALU.add, incs=[("sDVE", 1)])
                    ev_e = g.mark("sDVE")
                    evac_by_o[(og, o)] = ev_e
                    last_stt[par] = ev_e
                    g.wait("sync", ev_e)
                    chunk_write_evs[ch] = g.dma(
                        "sync", x_dst[ch * 128:(ch + 1) * 128, :], xw)
                    last_wr2[wpar] = chunk_write_evs[ch]
            prev_evac = evac_by_o[(cfg.n_og - 1, cfg.OG - 1)]
        return chunk_write_evs

    def emit_proj(wdram, oc_n, ct_n, rhs_tile, evac, slots=4, wcols=None,
                  dual=False):
        wcols = wcols if wcols is not None else ct_n * 128
        pe_done = {}
        war = {}
        for oc in range(oc_n):
            p = oc % 2
            ev_free = pe_done.get(oc - 2)
            wsrc = wdram[oc * 128:(oc + 1) * 128, 0:wcols]
            if dual and p == 1:
                ev_w = hwload(WB[:, p * wcols:(p + 1) * wcols], wsrc, wcols,
                              ev_free=ev_free)
            else:
                if ev_free is not None:
                    g.wait("gpsimd", ev_free)
                ev_w = g.dma("gpsimd", WB[:, p * wcols:(p + 1) * wcols], wsrc)
            g.wait("tensor", ev_w)
            slot = oc % slots
            if war.get(oc - slots) is not None:
                g.wait("tensor", war[oc - slots])
            for c in range(ct_n):
                for hh in range(2):
                    sl = slice(hh * NH, (hh + 1) * NH)
                    last = (c == ct_n - 1)
                    fst, lst = acc_flags(c == 0, last, hh)
                    g.mm(PS[:, slot * 1024 + hh * NH:slot * 1024 + (hh + 1) * NH],
                         WB[:, p * wcols + c * 128:p * wcols + (c + 1) * 128],
                         rhs_tile(c)[:, sl], start=fst, stop=lst,
                         incs=([("sPE", 1)] if (last and hh == 1) else []))
            pe_done[oc] = g.mark("sPE")
            war[oc] = evac(oc, slot, pe_done[oc])
        return g.mark("sDVE"), g.mark("sACT"), g.mark("sPE")

    xn_evs = emit_ln(t["x_in"], resident=True)
    g.wait("tensor", xn_evs[-1])
    g.wait("tensor", xn_evs[-2])
    if cfg.debug:
        for c in range(ND):
            g.wait("sync", xn_evs[c])
            g.dma("sync", t["dbg_xn1"][c * 128:(c + 1) * 128, :], xn_tile(c))
    g.barrier()
    ffn1_evs = emit_ffn(t["x_in"], t["x_out"], x_src_evs=None)
    if cfg.debug:
        for c in range(ND):
            g.wait("sync", ffn1_evs[c])
            g.dma("sync", t["dbg_x1"][c * 128:(c + 1) * 128, :],
                  t["x_out"][c * 128:(c + 1) * 128, :])

    g.barrier()
    xn2_evs = emit_ln(t["x_out"], resident=True, chunk_evs=ffn1_evs)
    g.barrier()
    for nm, dst in (("cosq", cos_q), ("sinq", sin_q), ("cosk", cos_k),
                    ("sink", sin_k)):
        g.dma("sync", dst, t[nm][:, :])
    for ev in g.qmarks("sync"):
        g.wait("vector", ev)
        g.wait("tensor", ev)

    def ckv_tile(cc):
        return CK[:, cc * T:(cc + 1) * T]

    stage1 = CK[:, NC * T:NC * T + T]
    stage2 = CK[:, NC * T + T:NC * T + 2 * T]

    def evac_copy_to(dst_fn):
        def f(oc, slot, ev_pe):
            g.wait("scalar", ev_pe)
            g.acopy(dst_fn(oc), PS[:, slot * 1024:slot * 1024 + T],
                    incs=[("sACT", 1)])
            return g.mark("sACT")
        return f

    emit_proj(t["wdk"], NC, ND, xn_tile, evac_copy_to(ckv_tile))
    g.barrier()

    def make_rope_evac(cos_t, sin_t, post):
        st = {"rot_war": {}, "stage_pe": {}, "stage_dve": {}, "ro": {}}

        def evac(oc, slot, ev_pe):
            g.wait("scalar", ev_pe)
            if oc >= 1:
                g.wait("scalar", st["stage_pe"][oc - 1])
                g.wait("scalar", st["stage_dve"][oc - 1])
            g.acopy(stage1, PS[:, slot * 1024:slot * 1024 + T], incs=[("sACT", 1)])
            ev_k = g.mark("sACT")
            srot = 2 + (oc % 2)
            g.wait("tensor", ev_k)
            if oc >= 2:
                g.wait("tensor", st["rot_war"][oc - 2])
            for hh in range(2):
                sl = slice(hh * NH, (hh + 1) * NH)
                fst, lst = acc_flags(True, True, hh)
                g.mm(PS[:, srot * 1024 + hh * NH:srot * 1024 + (hh + 1) * NH],
                     t["rt"][:, :], stage1[:, sl], start=fst, stop=lst,
                     incs=([("sPE", 1)] if hh == 1 else []))
            ev_rot = g.mark("sPE")
            st["stage_pe"][oc] = ev_rot
            g.wait("vector", ev_k)
            if oc >= 1:
                for e in st["ro"][oc - 1]:
                    g.wait("vector", e)
            g.tt(fa_rope, stage1, cos_t, ALU.mult)
            g.wait("vector", ev_rot)
            g.tt(stage2, PS[:, srot * 1024:srot * 1024 + T], sin_t, ALU.mult,
                 incs=[("sDVE", 1)])
            g.wait("vector", g.mark("sDVE"))
            g.tt(stage2, fa_rope, stage2, ALU.add, incs=[("sDVE", 1)])
            ev_ro = g.mark("sDVE")
            st["stage_dve"][oc] = ev_ro
            ev_fins = post(oc, ev_ro)
            st["rot_war"][oc] = ev_fins[0]
            st["ro"][oc] = ev_fins
            return ev_k
        return evac

    def post_kr(oc, ev_ro):
        g.wait("vector", ev_ro)
        for hh in range(2):
            h = oc * 2 + hh
            g.vcopy(kt_tile(h)[0:64, :], stage2[hh * 64:(hh + 1) * 64, :],
                    incs=([("sDVE", 1)] if hh == 1 else []))
        return [g.mark("sDVE")]

    emit_proj(t["wkr"], NC, ND, xn_tile, make_rope_evac(cos_k, sin_k, post_kr),
              slots=2)
    g.barrier()

    kc_state = {}

    def evac_kc(oc, slot, ev_pe):
        g.wait("scalar", ev_pe)
        if oc >= 1:
            g.wait("scalar", kc_state[oc - 1])
        g.acopy(stage1, PS[:, slot * 1024:slot * 1024 + T], incs=[("sACT", 1)])
        ev_k = g.mark("sACT")
        g.wait("vector", ev_k)
        for hh in range(2):
            h = oc * 2 + hh
            g.vcopy(kt_tile(h)[64:128, :], stage1[hh * 64:(hh + 1) * 64, :],
                    incs=([("sDVE", 1)] if hh == 1 else []))
        ev = g.mark("sDVE")
        kc_state[oc] = ev
        return ev

    emit_proj(t["wupk"], NC, NC, ckv_tile, evac_kc, wcols=NC * 128)
    ev_kt_done = g.mark("sDVE")
    g.barrier()

    KVHD = KV * 128
    wupv_sb = WB[:, 0:NC * KVHD]
    ev_wupv = g.dma(
        "gpsimd", wupv_sb.rearrange("p (c d) -> p c d", c=NC),
        t["wupv"].ap().rearrange("(c p) d -> p c d", p=128))
    g.wait("tensor", ev_wupv)
    step = min(NH, KVHD)
    vt_evs = {}
    for tt in range(cfg.TT):
        slot = tt % 4
        if tt >= 4:
            g.wait("tensor", vt_evs[tt - 4])
        nmm, tot = 0, NC * (KVHD // step)
        for cc in range(NC):
            for hh in range(KVHD // step):
                sl = slice(hh * step, (hh + 1) * step)
                nmm += 1
                fst, lst = acc_flags(cc == 0, cc == NC - 1, hh,
                                     nhh=KVHD // step, step=step)
                g.mm(PS[:, slot * 1024 + hh * step:slot * 1024 + (hh + 1) * step],
                     ckv_tile(cc)[:, tt * 128:(tt + 1) * 128],
                     wupv_sb[:, cc * KVHD + hh * step:cc * KVHD + (hh + 1) * step],
                     start=fst, stop=lst,
                     incs=([("sPE", 1)] if nmm == tot else []))
        ev_pe = g.mark("sPE")
        g.wait("scalar", ev_pe)
        for h in range(KV):
            g.acopy(vt_tile(h)[:, tt * 128:(tt + 1) * 128],
                    PS[:, slot * 1024 + h * 128:slot * 1024 + (h + 1) * 128],
                    incs=([("sACT", 1)] if h == KV - 1 else []))
        vt_evs[tt] = g.mark("sACT")
    ev_vt_done = g.mark("sACT")

    ev_kn = {}
    knorm_ship = {}
    for h in range(KV):
        sqv = t["sq"][:, (h % 2) * T:(h % 2) * T + T]
        g.wait("scalar", ev_kt_done)
        if h >= 2:
            g.wait("scalar", ev_kn[h - 2])
        g.act(sqv, kt_tile(h), AF.Square, incs=[("sACT", 1)])
        ev_sq = g.mark("sACT")
        g.wait("tensor", ev_sq)
        if h == 0:
            g.wait("tensor", ev_vt_done)
        for hh in range(2):
            sl = slice(hh * NH, (hh + 1) * NH)
            fst, lst = acc_flags(True, True, hh)
            g.mm(PS[0:1, (h % 4) * 1024 + hh * NH:(h % 4) * 1024 + (hh + 1) * NH],
                 ones_col, sqv[:, sl], start=fst, stop=lst,
                 incs=([("sPE", 1)] if hh == 1 else []))
        ev_mm = g.mark("sPE")
        ev_kn[h] = ev_mm
        g.wait("scalar", ev_mm)
        if h >= 2:
            g.wait("scalar", knorm_ship[h - 2])
        g.acopy(knorm[0:1, (h % 2) * T:(h % 2) * T + T],
                PS[0:1, (h % 4) * 1024:(h % 4) * 1024 + T], incs=[("sACT", 1)])
        ev_k = g.mark("sACT")
        g.wait("sync", ev_k)
        knorm_ship[h] = g.dma(
            "sync", t["contrib"][2 * KV * 128 + h:2 * KV * 128 + h + 1, :],
            knorm[0:1, (h % 2) * T:(h % 2) * T + T])

    g.wait("sync", ev_kt_done)
    for h in range(KV):
        g.dma("sync", t["contrib"][h * 128:(h + 1) * 128, :], kt_tile(h))
    g.wait("sync", ev_vt_done)
    for h in range(KV):
        g.dma("sync",
              t["contrib"][KV * 128 + h * 128:KV * 128 + (h + 1) * 128, :],
              vt_tile(h))
    for ev in g.qmarks("sync"):
        g.wait("gpsimd", ev)
    g.collective("AllGather", ALU.bypass, [[0, 1, 2, 3], [4, 5, 6, 7]],
                 [t["contrib"].ap().opt()], [t["gout"].ap().opt()],
                 incs=[("sCC", 1)])
    ev_cc = g.mark("sCC")
    for ev in g.qmarks("sync"):
        g.wait("sync", ev)
    ev_masks = g.dma("sync", t["masks"][:, :], t["masks_in"][:, :])
    g.wait("tensor", ev_masks)
    g.wait("vector", ev_masks)

    if cfg.debug:
        g.wait("sync", ev_kt_done)
        g.wait("sync", ev_vt_done)
        for h in range(KV):
            g.dma("sync", t["dbg_kt"][h * 128:(h + 1) * 128, :], kt_tile(h))
            g.dma("sync", t["dbg_vt"][h * 128:(h + 1) * 128, :], vt_tile(h))

    q_state = {}

    def post_q(oc, ev_ro):
        g.wait("sync", ev_ro)
        ev_store = g.dma("sync", t["qtild"][oc * 128:(oc + 1) * 128, :], stage2)
        g.wait("scalar", ev_ro)
        if oc >= 2:
            g.wait("scalar", q_state[oc - 2])
        sqv = t["sq"][:, (oc % 2) * T:(oc % 2) * T + T]
        g.act(sqv, stage2, AF.Square, incs=[("sACT", 1)])
        ev_sq = g.mark("sACT")
        srot = 2 + (oc % 2)
        g.wait("tensor", ev_sq)
        for hh in range(2):
            sl = slice(hh * NH, (hh + 1) * NH)
            fst, lst = acc_flags(True, True, hh)
            g.mm(PS[0:1, srot * 1024 + hh * NH:srot * 1024 + (hh + 1) * NH],
                 ones_col, sqv[:, sl], start=fst, stop=lst,
                 incs=([("sPE", 1)] if hh == 1 else []))
        ev_qn = g.mark("sPE")
        q_state[oc] = ev_qn
        g.wait("scalar", ev_qn)
        g.acopy(t["r_qn2"][:, :], PS[0:1, srot * 1024:srot * 1024 + T],
                incs=[("sACT", 1)])
        ev_r = g.mark("sACT")
        g.wait("vector", ev_r)
        for qb in range(2):
            g.vreduce(t["r_qmax2"][0:1, oc * 2 + qb:oc * 2 + qb + 1],
                      t["r_qn2"][0:1, qb * QB:(qb + 1) * QB], ALU.max,
                      incs=([("sDVE", 1)] if qb == 1 else []))
        return [g.mark("sDVE"), ev_store]

    emit_proj(t["wq"], QH, ND, xn_tile,
              make_rope_evac(cos_q, sin_q, post_q), slots=2)
    ev_qproj = g.mark("sDVE")
    g.barrier()

    g.wait("sync", ev_cc)
    ev_ks = g.dma("sync", t["ksg2"][:, :],
                  bass.AP(t["gout"], 2 * KV * 128 * T,
                          [[cfg.CROWS * T, 4], [T, KV], [1, T]]))
    g.wait("vector", ev_ks)
    g.vreduce(t["r_kpm"][:, :], t["ksg2"][:, :], ALU.max, incs=[("sDVE", 1)])
    ev_kpm = g.mark("sDVE")
    g.wait("sync", ev_kpm)
    ev_kb = g.dma("sync", t["r_t1"][0:1, 24:24 + 4 * KV], t["r_kpm"][:, 0:1])
    g.wait("vector", ev_kb)
    g.tt(t["r_t1"][0:1, 56:56 + KV], t["r_t1"][0:1, 24:24 + KV],
         t["r_t1"][0:1, 24 + KV:24 + 2 * KV], ALU.max, incs=[("sDVE", 1)])
    g.wait("vector", g.mark("sDVE"))
    g.tt(t["r_t1"][0:1, 64:64 + KV], t["r_t1"][0:1, 24 + 2 * KV:24 + 3 * KV],
         t["r_t1"][0:1, 24 + 3 * KV:24 + 4 * KV], ALU.max, incs=[("sDVE", 1)])
    g.wait("vector", g.mark("sDVE"))
    g.tt(t["r_t1"][0:1, 8:8 + KV], t["r_t1"][0:1, 56:56 + KV],
         t["r_t1"][0:1, 64:64 + KV], ALU.max, incs=[("sDVE", 1)])
    ev_km = g.mark("sDVE")
    g.wait("scalar", ev_km)
    g.act(t["r_t1"][0:1, 0:KV], t["r_t1"][0:1, 8:8 + KV], AF.Ln,
          incs=[("sACT", 1)])
    ev_lnk0 = g.mark("sACT")
    g.wait("vector", ev_lnk0)
    g.ts(t["r_t1"][0:1, 16:16 + KV], t["r_t1"][0:1, 0:KV], 0.5, None, ALU.mult,
         incs=[("sDVE", 1)])
    ev_lnk = g.mark("sDVE")

    g.wait("scalar", ev_qproj)
    g.act(t["r_lnq"][:, :], t["r_qmax2"][:, :], AF.Ln, incs=[("sACT", 1)])
    ev_lnq = g.mark("sACT")
    g.wait("scalar", ev_lnq)
    g.wait("scalar", ev_lnk)
    for h in range(KV):
        g.act(t["r_Bv"][0:1, h * 8:(h + 1) * 8],
              t["r_lnq"][0:1, h * 8:(h + 1) * 8], AF.Exp, scale=0.5,
              bias=t["r_t1"][0:1, 16 + h:17 + h],
              incs=([("sACT", 1)] if h == KV - 1 else []))
    ev_b = g.mark("sACT")
    g.wait("vector", ev_b)
    g.ts(t["r_lnq"][:, :], t["r_Bv"][:, :], -1.0, -B_MARGIN, ALU.mult, ALU.add,
         incs=[("sDVE", 1)])
    ev_nb = g.mark("sDVE")
    g.wait("scalar", ev_nb)
    g.acopy(t["r_nbbf"][:, :], t["r_lnq"][:, :], incs=[("sACT", 1)])
    ev_nbf = g.mark("sACT")
    g.wait("tensor", ev_nbf)
    g.mm(PS[:, 2048:2048 + 2 * QH], ones_row, t["r_nbbf"][0:1, :],
         start=True, stop=True, incs=[("sPE", 1)])
    ev_nbc = g.mark("sPE")
    g.wait("scalar", ev_nbc)
    g.acopy(t["negB"][:, :], PS[:, 2048:2048 + 2 * QH], incs=[("sACT", 1)])
    ev_negB = g.mark("sACT")

    NSLOT = 3 + 7
    g.wait("vector", ev_nb)
    for s, (qb, jj) in enumerate(MAIN_SLOTS):
        g.vcopy(t["r_bx"][0:1, s * QH:(s + 1) * QH],
                bass.AP(t["r_lnq"], qb, [[2 * QH, 1], [2, QH]]),
                incs=([("sDVE", 1)] if s == NSLOT - 1 else []))
    g.wait("vector", g.mark("sDVE"))
    g.tt(t["r_bx"][:, :], t["r_bx"][:, :], t["valid"][:, :], ALU.mult,
         incs=[("sDVE", 1)])
    g.ts(t["r_bx2"][:, :], t["valid"][:, :], 1e9, -1e9, ALU.mult, ALU.add,
         incs=[("sDVE", 1)])
    g.wait("vector", g.mark("sDVE"))
    g.tt(t["r_bx"][:, :], t["r_bx"][:, :], t["r_bx2"][:, :], ALU.add,
         incs=[("sDVE", 1)])
    ev_bx = g.mark("sDVE")
    g.wait("scalar", ev_bx)
    g.acopy(t["r_bxbf"][:, :], t["r_bx"][:, :], incs=[("sACT", 1)])
    ev_bxbf = g.mark("sACT")
    g.wait("tensor", ev_bxbf)
    g.mm(PS[:, 2048:2048 + NSLOT * QH], ones_row, t["r_bxbf"][0:1, :],
         start=True, stop=True, incs=[("sPE", 1)])
    ev_bxm = g.mark("sPE")
    g.wait("scalar", ev_bxm)
    g.acopy(t["bias_x"][:, :], PS[:, 2048:2048 + NSLOT * QH], incs=[("sACT", 1)])
    ev_biasx = g.mark("sACT")

    g.barrier()
    for qh in range(QH):
        g.dma("sync", qav_tile(qh), t["qtild"][qh * 128:(qh + 1) * 128, :])
    for ev in g.qmarks("sync"):
        g.wait("tensor", ev)
    if cfg.debug:
        for qh in range(QH):
            g.dma("sync", t["dbg_q"][qh * 128:(qh + 1) * 128, :], qav_tile(qh))

    KT_C = cfg.KT_C
    NU = (3 + 7 + 2) * KT_C
    ev_unit_pe = {}
    ev_exp = {}
    ev_epi_dve = None
    ev_epi_act = None

    HSEG = 10 * T
    KTV = AB[:, TAIL0:TAIL0 + 2 * HSEG]

    def ktall(hp):
        return KTV[:, hp * HSEG: hp * HSEG + 4 * T]

    def vall(hp):
        return KTV[:, hp * HSEG + 4 * T: hp * HSEG + 8 * T]

    def ownkt(hp):
        return KTV[:, hp * HSEG + 8 * T: hp * HSEG + 9 * T]

    def ownvt(hp):
        return KTV[:, hp * HSEG + 9 * T: hp * HSEG + 10 * T]

    def chunk_col(jj, kt):
        rank = jj if jj < 4 else 7 - jj
        half = 0 if jj < 4 else 1
        return rank * T + half * QB + kt * 128

    ev_hload = {}
    ev_h_pe_done = {}

    def load_h(h):
        hp = h % 2
        if h >= 2:
            g.wait("sync", ev_h_pe_done[h - 2])
        else:
            g.wait("sync", ev_cc)
        evs = []
        for r in range(4):
            evs.append(g.dma("sync", ktall(hp)[:, r * T:(r + 1) * T],
                             t["gout"][r * cfg.CROWS + h * 128:
                                       r * cfg.CROWS + (h + 1) * 128, :]))
            evs.append(g.dma("sync", vall(hp)[:, r * T:(r + 1) * T],
                             t["gout"][r * cfg.CROWS + KV * 128 + h * 128:
                                       r * cfg.CROWS + KV * 128 + (h + 1) * 128, :]))
        evs.append(g.dma("sync", ownkt(hp),
                         t["contrib"][h * 128:(h + 1) * 128, :]))
        evs.append(g.dma("sync", ownvt(hp),
                         t["contrib"][KV * 128 + h * 128:
                                      KV * 128 + (h + 1) * 128, :]))
        ev_hload[h] = evs

    g.wait("scalar", ev_negB)
    g.wait("scalar", ev_biasx)
    load_h(0)
    load_h(1)

    esA = fa_rope
    esB = t["sq"][:, 0:2 * T].bitcast(F32)
    E = 2 if KT_C % 2 == 0 else 1
    NP = NU // E

    ev_exppair = {}
    ev_av = {}
    ev_esA = {}
    ev_esB = {}
    ev_epi_dve = None
    ev_epi_act = None
    ev_ese_prev = None

    for qh in range(QH):
        h = qh // 4
        hp = h % 2
        ev_ktls = ev_hload[h]
        if ev_epi_dve is not None:
            g.wait("tensor", ev_epi_dve)
            g.wait("gpsimd", ev_epi_dve)
        g.memset(esA[:, :], 0.0)
        g.memset(esB[:, :], 0.0, incs=[("sPOOL", 1)])
        ev_ms = g.mark("sPOOL")
        g.wait("vector", ev_ms)
        g.wait("gpsimd", ev_ms)
        for col in (2048, 2560):
            g.mm(PS[:, col:col + QB], zrow, ones_qb_row, start=True, stop=False,
                 skip=True, incs=([("sPE", 1)] if col == 2560 else []))
        ev_clear = g.mark("sPE")

        units = []
        for s, (qb, jj) in enumerate(MAIN_SLOTS):
            for kt in range(KT_C):
                units.append(("main", qb, s, chunk_col(jj, kt), kt))
        for qb in range(2):
            for kt in range(KT_C):
                units.append(("diag", qb, None, qb * QB + kt * 128, kt))

        def u_info(u):
            kind_u, qb, s, col, kt = units[u]
            lhs = (ktall(hp)[:, col:col + 128] if kind_u == "main"
                   else ownkt(hp)[:, col:col + 128])
            vlhs = (vall(hp)[:, col:col + 128] if kind_u == "main"
                    else ownvt(hp)[:, col:col + 128])
            if kind_u == "main":
                bias = t["bias_x"][:, s * QH + qh:s * QH + qh + 1]
            else:
                bias = t["negB"][:, qh * 2 + qb:qh * 2 + qb + 1]
            return kind_u, qb, col, kt, lhs, vlhs, bias

        def ebuf_of(u):
            p = u // E
            if E == 2:
                return t["eb"][:, (p % 2) * 2 * QB + (u % 2) * QB:
                               (p % 2) * 2 * QB + (u % 2) * QB + QB]
            return t["eb"][:, (u % 2) * QB:(u % 2) * QB + QB]

        ev_mask_done = {}

        def prev_pair(p, d):
            pp = p - d
            if pp >= 0:
                return (qh, pp)
            if qh > 0:
                return (qh - 1, NP + pp)
            return None

        def emit_av(u):
            kind_u, qb, col, kt, lhs, vlhs, bias = u_info(u)
            eb_u = ebuf_of(u)
            g.wait("tensor", ev_mask_done[(qh, u)])
            avcol = 2048 + qb * 512
            stop_acc = (u == 11 * KT_C - 1) or (u == NU - 1)
            g.mm(PS[:, avcol:avcol + QB], vlhs, eb_u,
                 start=False, stop=stop_acc, skip=True, incs=[("sPE", 1)])
            ev_av[(qh, u)] = g.mark("sPE")

        for p in range(NP):
            us = list(range(p * E, (p + 1) * E))
            for u in us:
                kind_u, qb, col, kt, lhs, vlhs, bias = u_info(u)
                if u == 0 and qh % 4 == 0:
                    for e_ in ev_ktls:
                        g.wait("tensor", e_)
                if u == us[0]:
                    pq = prev_pair(p, 2)
                    if pq is not None:
                        g.wait("tensor", ev_exppair[pq])
                    if p == 0:
                        g.wait("tensor", ev_clear)
                scol = (((p % 2) * 2 + (u % 2)) * 512 if E == 2
                        else (u % 2) * 512)
                g.mm(PS[:, scol:scol + QB], lhs,
                     qav_tile(qh)[:, qb * QB:(qb + 1) * QB],
                     start=True, stop=True,
                     incs=([("sPE", 1)] if u == us[-1] else []))
            ev_s = g.mark("sPE")
            if p > 0:
                for u in range((p - 1) * E, p * E):
                    emit_av(u)
            g.wait("scalar", ev_s)
            pq2 = prev_pair(p, 2)
            if pq2 is not None:
                qh2, pp2 = pq2
                for u2 in range(pp2 * E, (pp2 + 1) * E):
                    g.wait("scalar", ev_av[(qh2, u2)])
                    g.wait("scalar", ev_esA.get((qh2, u2), ("sDVE", 0)))
                    g.wait("scalar", ev_esB.get((qh2, u2), ("sPOOL", 0)))
            kind0, qb0_, col0, kt0, lhs0, vlhs0, bias0 = u_info(us[0])
            if E == 2:
                eb_grp = t["eb"][:, (p % 2) * 2 * QB:(p % 2) * 2 * QB + 2 * QB]
                base = (p % 2) * 1024
                g.act(eb_grp, PS[:, base:base + 512 + QB], AF.Exp, bias=bias0,
                      incs=[("sACT", 1)])
            else:
                scol = (us[0] % 2) * 512
                g.act(ebuf_of(us[0]), PS[:, scol:scol + QB], AF.Exp, bias=bias0,
                      incs=[("sACT", 1)])
            ev_e = g.mark("sACT")
            ev_exppair[(qh, p)] = ev_e
            for u in us:
                kind_u, qb, col, kt, lhs, vlhs, bias = u_info(u)
                eb_u = ebuf_of(u)
                if kind_u == "diag":
                    g.wait("vector", ev_e)
                    g.tt(eb_u, eb_u, t["masks"][:, kt * QB:(kt + 1) * QB],
                         ALU.mult, incs=[("sDVE", 1)])
                    ev_mask_done[(qh, u)] = g.mark("sDVE")
                else:
                    ev_mask_done[(qh, u)] = ev_e
                if u % 2 == 0:
                    g.wait("vector", ev_mask_done[(qh, u)])
                    g.wait("vector", g.mark("sDVE"))
                    g.tt(esA[:, qb * QB:(qb + 1) * QB],
                         esA[:, qb * QB:(qb + 1) * QB], eb_u, ALU.add,
                         incs=[("sDVE", 1)])
                    ev_esA[(qh, u)] = g.mark("sDVE")
                else:
                    g.wait("gpsimd", ev_mask_done[(qh, u)])
                    g.wait("gpsimd", g.mark("sPOOL"))
                    g.ptt(esB[:, qb * QB:(qb + 1) * QB],
                          esB[:, qb * QB:(qb + 1) * QB], eb_u, ALU.add,
                          incs=[("sPOOL", 1)])
                    ev_esB[(qh, u)] = g.mark("sPOOL")
        for u in range(NU - E, NU):
            emit_av(u)
        if qh % 4 == 3:
            ev_h_pe_done[h] = g.mark("sPE")
            if h + 2 < KV:
                load_h(h + 2)

        ev_last = ev_av[(qh, NU - 1)]
        g.wait("vector", ev_esB[(qh, NU - 1)])
        g.wait("vector", ev_esA[(qh, NU - 2)])
        g.tt(esA[:, :], esA[:, :], esB[:, :], ALU.add, incs=[("sDVE", 1)])
        ev_merge = g.mark("sDVE")
        g.wait("scalar", ev_merge)
        g.wait("scalar", ev_last)
        es_bf = t["eb"][:, 0:2 * QB]
        g.acopy(es_bf, esA[:, :], incs=[("sACT", 1)])
        ev_cast = g.mark("sACT")
        g.wait("tensor", ev_cast)
        for qb in range(2):
            g.mm(PS[0:1, 3072 + qb * 512:3072 + qb * 512 + QB], ones_col,
                 es_bf[:, qb * QB:(qb + 1) * QB], start=True, stop=True,
                 incs=([("sPE", 1)] if qb == 1 else []))
        ev_esmm = g.mark("sPE")
        g.wait("scalar", ev_esmm)
        g.acopy(t["r_esum"][0:1, 0:QB], PS[0:1, 3072:3072 + QB])
        g.acopy(t["r_esum"][0:1, QB:2 * QB], PS[0:1, 3584:3584 + QB],
                incs=[("sACT", 1)])
        ev_ese = g.mark("sACT")
        ev_ese_prev = ev_ese
        g.wait("vector", ev_ese)
        g.ts(t["r_esum"][:, :], t["r_esum"][:, :], 1e-30, None, ALU.add,
             incs=[("sDVE", 1)])
        g.wait("vector", g.mark("sDVE"))
        g.vrecip(t["r_recip"][:, :], t["r_esum"][:, :], incs=[("sDVE", 1)])
        ev_rc = g.mark("sDVE")
        g.wait("scalar", ev_rc)
        g.acopy(t["r_recipbf"][:, :], t["r_recip"][:, :], incs=[("sACT", 1)])
        ev_rcb = g.mark("sACT")
        g.wait("tensor", ev_rcb)
        if ev_epi_act is not None:
            g.wait("tensor", ev_epi_act)
        for qb in range(2):
            g.mm(PS[:, 3072 + qb * 512:3072 + qb * 512 + QB], ones_row,
                 t["r_recipbf"][0:1, qb * QB:(qb + 1) * QB], start=True, stop=True,
                 incs=([("sPE", 1)] if qb == 1 else []))
        ev_bcm = g.mark("sPE")
        g.wait("scalar", ev_bcm)
        for qb in range(2):
            g.acopy(t["recb"][:, qb * QB:(qb + 1) * QB],
                    PS[:, 3072 + qb * 512:3072 + qb * 512 + QB],
                    incs=([("sACT", 1)] if qb == 1 else []))
        ev_epi_act = g.mark("sACT")
        g.wait("vector", ev_epi_act)
        g.wait("vector", ev_last)
        for qb in range(2):
            g.tt(qav_tile(qh)[:, qb * QB:(qb + 1) * QB],
                 PS[:, 2048 + qb * 512:2048 + qb * 512 + QB],
                 t["recb"][:, qb * QB:(qb + 1) * QB], ALU.mult,
                 incs=([("sDVE", 1)] if qb == 1 else []))
        ev_epi_dve = g.mark("sDVE")

    if cfg.debug:
        g.wait("sync", ev_epi_dve)
        for qh in range(QH):
            g.dma("sync", t["dbg_av"][qh * 128:(qh + 1) * 128, :], qav_tile(qh))

    g.barrier()
    attn_out_evs = [None] * ND
    wo_last = {0: None, 1: None}
    wo_wr = None

    def evac_wo(oc, slot, ev_pe):
        nonlocal wo_wr
        par = oc % 2
        buf = fa_x0 if par == 0 else fa_x1
        if wo_last[par] is not None:
            g.wait("sync", wo_last[par])
        ev_x = g.dma("sync", buf, t["x_out"][oc * 128:(oc + 1) * 128, :])
        g.wait("vector", ev_x)
        g.wait("vector", ev_pe)
        if wo_wr is not None:
            g.wait("vector", wo_wr)
        g.tt(fa_xw, PS[:, slot * 1024:slot * 1024 + T], buf, ALU.add,
             incs=[("sDVE", 1)])
        ev_e = g.mark("sDVE")
        wo_last[par] = ev_e
        g.wait("sync", ev_e)
        attn_out_evs[oc] = g.dma(
            "sync", t["x_out"][oc * 128:(oc + 1) * 128, :], fa_xw)
        wo_wr = attn_out_evs[oc]
        return ev_e

    emit_proj(t["wo"], ND, QH, qav_tile, evac_wo, dual=True)
    if cfg.debug:
        for c in range(ND):
            g.wait("sync", attn_out_evs[c])
            g.dma("sync", t["dbg_x2"][c * 128:(c + 1) * 128, :],
                  t["x_out"][c * 128:(c + 1) * 128, :])

    g.barrier()
    ln3_guard = g.mark("sPE")
    xn3_evs = emit_ln(t["x_out"], resident=True, chunk_evs=attn_out_evs,
                      first_write_guard=ln3_guard)
    g.wait("tensor", xn3_evs[-1])
    g.wait("tensor", xn3_evs[-2])
    g.barrier(kinds=("sync", "gpsimd"))
    ffn2_evs = emit_ffn(t["x_out"], t["x_out"], x_src_evs=attn_out_evs)

    for ev in ffn2_evs:
        g.wait("sync", ev)
    g.barrier(kinds=("sync",))



def declare(nc, cfg):
    from contextlib import ExitStack
    ctx = ExitStack()
    t = {}
    D, T, ND, NI, KV, QH, QB, NC = (cfg.D, cfg.T, cfg.ND, cfg.NI, cfg.KV,
                                    cfg.QH, cfg.QB, cfg.NC)

    t["x_in"] = nc.dram_tensor("x_in", [D, T], F32, kind="ExternalInput")
    t["wq"] = nc.dram_tensor("wq", [QH * 128, D], F32, kind="ExternalInput")
    t["wo"] = nc.dram_tensor("wo", [D, QH * 128], F32, kind="ExternalInput")
    t["wkr"] = nc.dram_tensor("wkr", [cfg.CD, D], F32, kind="ExternalInput")
    t["wdk"] = nc.dram_tensor("wdk", [cfg.CD, D], F32, kind="ExternalInput")
    t["wupk"] = nc.dram_tensor("wupk", [cfg.CD, cfg.CD], F32, kind="ExternalInput")
    t["wupv"] = nc.dram_tensor("wupv", [cfg.CD, KV * 128], F32, kind="ExternalInput")
    t["wgu"] = nc.dram_tensor("wgu", [NI * 128, 2 * ND * 128], F32,
                              kind="ExternalInput")
    t["wd"] = nc.dram_tensor("wd", [cfg.n_og * 128, NI * cfg.OG * 128], F32,
                             kind="ExternalInput")
    for nm in ("cosq", "sinq", "cosk", "sink"):
        t[nm] = nc.dram_tensor(nm, [128, T], F32, kind="ExternalInput")
    t["masks_in"] = nc.dram_tensor("masks_in", [128, 4 * QB], BF16,
                                   kind="ExternalInput")
    t["rt_in"] = nc.dram_tensor("rt_in", [128, 128], BF16, kind="ExternalInput")
    t["ones_in"] = nc.dram_tensor("ones_in", [128, 130], BF16, kind="ExternalInput")
    t["zrow_in"] = nc.dram_tensor("zrow_in", [1, 128], BF16, kind="ExternalInput")
    t["valid_in"] = nc.dram_tensor("valid_in", [1, 10 * QH], F32,
                                   kind="ExternalInput")
    t["x_out"] = nc.dram_tensor("x_out", [D, T], F32, kind="ExternalOutput")

    t["xn2d"] = nc.dram_tensor("xn2d", [D, T], BF16)
    t["qtild"] = nc.dram_tensor("qtild", [QH * 128, T], BF16)
    t["contrib"] = nc.dram_tensor("contrib", [cfg.CROWS, T], BF16)
    t["gout"] = nc.dram_tensor("gout", [cfg.GR, T], BF16)

    if cfg.debug:
        t["dbg_xn1"] = nc.dram_tensor("dbg_xn1", [D, T], BF16, kind="ExternalOutput")
        t["dbg_x1"] = nc.dram_tensor("dbg_x1", [D, T], F32, kind="ExternalOutput")
        t["dbg_kt"] = nc.dram_tensor("dbg_kt", [KV * 128, T], BF16,
                                     kind="ExternalOutput")
        t["dbg_vt"] = nc.dram_tensor("dbg_vt", [KV * 128, T], BF16,
                                     kind="ExternalOutput")
        t["dbg_q"] = nc.dram_tensor("dbg_q", [QH * 128, T], BF16,
                                    kind="ExternalOutput")
        t["dbg_av"] = nc.dram_tensor("dbg_av", [QH * 128, T], BF16,
                                     kind="ExternalOutput")
        t["dbg_x2"] = nc.dram_tensor("dbg_x2", [D, T], F32, kind="ExternalOutput")

    def sb(name, shape, dtype):
        t[name] = ctx.enter_context(nc.sbuf_tensor(name, shape, dtype))

    max_g = max(len(gg) for gg in cfg.fgroups)
    sb("AB", [128, max(ND, QH) * T + max(max_g, 20) * T], BF16)
    wb_cols = max(3 * ND * 128,
                  2 * ((max_g + 1) // 2) * cfg.OG * 128,
                  2 * ND * 128, 2 * QH * 128, 2 * NC * 128)
    sb("WB", [128, wb_cols], BF16)
    sb("CK", [128, NC * T + 2 * T], BF16)
    sb("F32A", [128, 5 * T], F32)
    sb("masks", [128, 4 * QB], BF16)
    sb("rt", [128, 128], BF16)
    sb("ones", [128, 130], BF16)
    sb("zrow", [1, 128], BF16)
    sb("valid", [1, 10 * QH], F32)
    sb("eb", [128, 4 * QB], BF16)
    sb("recb", [128, 2 * QB], BF16)
    sb("negB", [128, 2 * QH], F32)
    sb("bias_x", [128, 10 * QH], F32)
    sb("silu", [128, 2 * T], BF16)
    sb("sq", [128, 2 * T], BF16)
    sb("ksg2", [4 * KV, T], BF16)
    sb("r_kpm", [4 * KV, 1], F32)
    for nm in ("r_sum", "r_sumsq", "r_rstd"):
        sb(nm, [1, T], F32)
    t["r_mean"] = t["r_sum"]
    t["r_var"] = t["r_sumsq"]
    t["r_qn2"] = t["r_rstd"]
    t["r_esum"] = t["r_sum"]
    t["r_recip"] = t["r_sumsq"]
    sb("r_meanbf", [1, T], BF16)
    sb("r_rstdbf", [1, T], BF16)
    sb("r_recipbf", [1, 2 * QB], BF16)
    sb("r_qmax2", [1, 2 * QH], F32)
    sb("r_lnq", [1, 2 * QH], F32)
    sb("r_nbbf", [1, 2 * QH], BF16)
    sb("r_bx", [1, 10 * QH], F32)
    sb("r_bx2", [1, 10 * QH], F32)
    sb("r_bxbf", [1, 10 * QH], BF16)
    sb("r_t1", [1, 96], F32)
    sb("r_Bv", [1, 2 * QH], F32)

    t["PS"] = ctx.enter_context(nc.psum_tensor("PS", [128, 4096], F32))

    sems = {}
    for nm in SEMS:
        sems[nm] = ctx.enter_context(nc.semaphore(nm))
    return t, sems, ctx


def build_program(cfg):
    from contextlib import ExitStack
    nc = bass.Bass("TRN2", target_bir_lowering=False)
    tn, sems, ctx = declare(nc, cfg)
    g = G(nc, cfg, tn, sems)
    with ctx:
        with nc.Block() as block:
            def run_walk(kind):
                def f(eng):
                    with ExitStack() as regctx:
                        g.start_walk(kind, eng, regctx)
                        emit_program(g)
                return f
            block.tensor(run_walk("tensor"))
            block.scalar(run_walk("scalar"))
            block.vector(run_walk("vector"))
            block.gpsimd(run_walk("gpsimd"))
            block.sync(run_walk("sync"))
    return nc


_CACHED = {}


def kernel(**inputs):
    cfg = REAL
    key = "real"
    if key not in _CACHED:
        _CACHED[key] = build_program(cfg)
    nc = _CACHED[key]
    in_maps = host_prepare(inputs, cfg)
    res = run_bass_kernel_spmd(nc, in_maps, core_ids=list(range(8)))
    return host_assemble(res.results, cfg)


# version 09
# speedup vs baseline: 1.0978x; 1.0343x over previous
import math
import numpy as np
import ml_dtypes

import concourse.bass as bass
import concourse.mybir as mybir
from concourse.bass_utils import run_bass_kernel_spmd

F32 = mybir.dt.float32
BF16 = mybir.dt.bfloat16
I32 = mybir.dt.int32
AF = mybir.ActivationFunctionType
ALU = mybir.AluOpType

LN_EPS = 1e-5
B_MARGIN = 2.0
MAIN_SLOTS = [(0, jj) for jj in range(3)] + [(1, jj) for jj in range(7)]


class Cfg:
    def __init__(self, D, I, KV, T, n_fgroups, debug=False):
        self.D = D
        self.I = I
        self.KV = KV
        self.QH = 4 * KV
        self.HD = 128
        self.T = T
        self.QB = T // 2
        self.NH = T // 2
        assert self.NH <= 512 and self.QB % 128 == 0
        self.ND = D // 128
        self.NI = I // 128
        self.CD = 64 * KV
        self.NC = self.CD // 128
        self.TT = T // 128
        self.KT_C = self.QB // 128
        self.n_fgroups = n_fgroups
        base, rem = self.NI // n_fgroups, self.NI % n_fgroups
        sizes = [base + (1 if g_ < rem else 0) for g_ in range(n_fgroups)]
        self.fgroups = []
        s = 0
        for sz in sizes:
            self.fgroups.append(list(range(s, s + sz)))
            s += sz
        self.OG = min(4, self.ND)
        self.n_og = self.ND // self.OG
        self.CROWS = 2 * KV * 128 + KV
        self.GR = 4 * self.CROWS
        self.debug = debug
        self.wgu_cols = 2 * self.ND * 128


REAL = Cfg(D=4096, I=11008, KV=8, T=1024, n_fgroups=5)
SMALL = Cfg(D=512, I=512, KV=2, T=256, n_fgroups=2, debug=True)
MID = Cfg(D=512, I=512, KV=2, T=1024, n_fgroups=2, debug=True)
MID2 = Cfg(D=512, I=512, KV=8, T=1024, n_fgroups=2, debug=True)
MID3 = Cfg(D=4096, I=512, KV=8, T=1024, n_fgroups=2, debug=True)



def sigma_perm():
    return np.concatenate([np.arange(0, 32), np.arange(64, 96),
                           np.arange(32, 64), np.arange(96, 128)])


def host_tile_lhsT(W):
    O, K = W.shape
    no, nk = O // 128, K // 128
    Wb = W.reshape(no, 128, nk, 128)
    Wt = Wb.transpose(0, 3, 2, 1)
    return np.ascontiguousarray(Wt.reshape(O, K))


def host_tile_wd(Wd, cfg):
    Wb = Wd.reshape(cfg.n_og, cfg.OG, 128, cfg.NI, 128)
    Wt = Wb.transpose(0, 4, 3, 1, 2)
    return np.ascontiguousarray(Wt.reshape(cfg.n_og * 128, cfg.NI * cfg.OG * 128))


def host_rope_tables(positions, cfg):
    HD = cfg.HD
    inv = 1.0 / (10000.0 ** (np.arange(0, HD, 2, dtype=np.float64) / HD))
    fr = np.outer(inv, positions.astype(np.float64))
    cos = np.cos(fr).astype(np.float32)
    sin = np.sin(fr).astype(np.float32)
    rows_q = np.concatenate([np.arange(32), np.arange(32),
                             np.arange(32, 64), np.arange(32, 64)])
    rows_k = np.concatenate([np.arange(32), np.arange(32)] * 2)
    s = 1.0 / math.sqrt(HD)
    return (cos[rows_q] * s, sin[rows_q] * s, cos[rows_k], sin[rows_k])


def host_R_matrix():
    R = np.zeros((128, 128), np.float32)
    for b in (0, 64):
        for i in range(32):
            R[b + i, b + i + 32] = -1.0
            R[b + 32 + i, b + i] = 1.0
    return np.ascontiguousarray(R.T)


def host_masks(cfg):
    QB = cfg.QB
    out = np.zeros((128, 4 * QB), np.float32)
    q = np.arange(QB)
    k = np.arange(128)
    for kt in range(4):
        kk = min(kt, cfg.KT_C - 1)
        out[:, kt * QB:(kt + 1) * QB] = (q[None, :] >= (kk * 128 + k)[:, None])
    return out.astype(ml_dtypes.bfloat16)


def host_valid(core, cfg):
    c = core % 4
    own = {0: c, 1: 7 - c}
    v = np.zeros((1, 10 * cfg.QH), np.float32)
    for s, (qb, jj) in enumerate(MAIN_SLOTS):
        if jj < own[qb]:
            v[0, s * cfg.QH:(s + 1) * cfg.QH] = 1.0
    return v


def host_prepare(inputs, cfg):
    x = np.asarray(inputs["L0_attn_hidden_states"], np.float32)
    Wq = np.asarray(inputs["L0_attn_Wq"], np.float32)
    Wkr = np.asarray(inputs["L0_attn_Wkr"], np.float32)
    Wdk = np.asarray(inputs["L0_attn_Wdk"], np.float32)
    Wupk = np.asarray(inputs["L0_attn_Wupk"], np.float32)
    Wupv = np.asarray(inputs["L0_attn_Wupv"], np.float32)
    Wo = np.asarray(inputs["L0_attn_Wo"], np.float32)
    Wg = np.asarray(inputs["L0_ffn_w_gate"], np.float32)
    Wu = np.asarray(inputs["L0_ffn_w_up"], np.float32)
    Wd = np.asarray(inputs["L0_ffn_w_down"], np.float32)

    QB = cfg.QB
    sp = sigma_perm()
    qrows = np.concatenate([qh * 128 + sp for qh in range(cfg.QH)])
    wq_t = host_tile_lhsT(Wq[qrows])
    wo_t = host_tile_lhsT(Wo)
    wkr_t = host_tile_lhsT(Wkr)
    wdk_t = host_tile_lhsT(Wdk)
    wupk_t = host_tile_lhsT(Wupk)
    wupv_t = np.ascontiguousarray(Wupv.T)

    wg_t = host_tile_lhsT(Wg).reshape(cfg.NI, 128, cfg.ND * 128)
    wu_t = host_tile_lhsT(Wu).reshape(cfg.NI, 128, cfg.ND * 128)
    wgu_t = np.ascontiguousarray(
        np.concatenate([wg_t, wu_t], axis=2)).reshape(cfg.NI * 128, -1)
    wd_t = host_tile_wd(Wd, cfg)

    masks = host_masks(cfg)
    RT = host_R_matrix().astype(ml_dtypes.bfloat16)
    ones_seg = np.ones((128, 130), ml_dtypes.bfloat16)
    zrow = np.zeros((1, 128), ml_dtypes.bfloat16)

    in_maps = []
    for core in range(8):
        b, c = core // 4, core % 4
        toks = np.concatenate([np.arange(c * QB, (c + 1) * QB),
                               np.arange((7 - c) * QB, (8 - c) * QB)])
        xc = np.ascontiguousarray(x[b][toks].T)
        cosq, sinq, cosk, sink = host_rope_tables(toks, cfg)
        in_maps.append({
            "x_in": xc,
            "wq": wq_t, "wo": wo_t, "wkr": wkr_t, "wdk": wdk_t,
            "wupk": wupk_t, "wupv": wupv_t, "wgu": wgu_t, "wd": wd_t,
            "cosq": np.ascontiguousarray(cosq), "sinq": np.ascontiguousarray(sinq),
            "cosk": np.ascontiguousarray(cosk), "sink": np.ascontiguousarray(sink),
            "masks_in": masks, "rt_in": RT, "ones_in": ones_seg, "zrow_in": zrow,
            "valid_in": host_valid(core, cfg),
        })
    return in_maps


def host_assemble(results, cfg):
    out = np.zeros((2, 8 * cfg.QB, cfg.D), np.float32)
    for core in range(8):
        b, c = core // 4, core % 4
        o = results[core]["x_out"]
        out[b, c * cfg.QB:(c + 1) * cfg.QB] = o[:, 0:cfg.QB].T
        out[b, (7 - c) * cfg.QB:(8 - c) * cfg.QB] = o[:, cfg.QB:].T
    return out



ENGS = ("tensor", "scalar", "vector", "gpsimd", "sync")
DMA_RINGS = {"sync": ("sSY", 8), "scalar": ("sSC", 4), "gpsimd": ("sWD", 8)}
SEMS = (["sPE", "sACT", "sDVE", "sPOOL", "sCC"]
        + [f"{nm}{i}" for nm, k in DMA_RINGS.values() for i in range(k)])


class G:
    def __init__(self, nc, cfg, tn, sems):
        self.nc, self.cfg, self.t, self.sems = nc, cfg, tn, sems

    def start_walk(self, kind, eng, regctx):
        self.kind, self.eng = kind, eng
        self.cnt = {k: 0 for k in self.sems}
        self.regs = {}
        self.regctx = regctx
        self.dma_rr = {q: 0 for q in DMA_RINGS}

    def bump(self, sem, n):
        self.cnt[sem] += n
        return self.cnt[sem]

    def mark(self, sem):
        return (sem, self.cnt[sem])

    def _ap(self, instr, incs):
        for sem, n in incs:
            self.bump(sem, n)
            if instr is not None:
                instr.then_inc(self.sems[sem], n)

    def wait(self, kind, ev):
        sem, val = ev
        if val > 0 and self.kind == kind:
            self.eng.wait_ge(self.sems[sem], val)

    def on(self, kind):
        return self.kind == kind

    def mm(self, out, lhsT, rhs, start, stop, incs=(), skip=False):
        i = self.eng.matmul(out, lhsT, rhs, start=start, stop=stop,
                            skip_group_check=skip) if self.on("tensor") else None
        self._ap(i, incs)

    def act(self, out, in_, func, bias=0.0, scale=1.0, incs=()):
        i = self.eng.activation(out, in_, func, bias=bias, scale=scale) \
            if self.on("scalar") else None
        self._ap(i, incs)

    def acopy(self, out, in_, incs=()):
        i = self.eng.copy(out, in_) if self.on("scalar") else None
        self._ap(i, incs)

    def tt(self, out, a, b, op, incs=()):
        i = self.eng.tensor_tensor(out, a, b, op) if self.on("vector") else None
        self._ap(i, incs)

    def stt(self, out, in0, scalar, in1, op0, op1, incs=()):
        i = self.eng.scalar_tensor_tensor(out, in0, scalar, in1, op0, op1) \
            if self.on("vector") else None
        self._ap(i, incs)

    def ts(self, out, in0, s1, s2, op0, op1=ALU.mult, incs=()):
        if self.on("vector"):
            if s2 is None:
                i = self.eng.tensor_single_scalar(out, in0, s1, op0)
            else:
                i = self.eng.tensor_scalar(out, in0, s1, s2, op0, op1)
        else:
            i = None
        self._ap(i, incs)

    def ptt(self, out, a, b, op, incs=()):
        i = self.eng.tensor_tensor(out, a, b, op) if self.on("gpsimd") else None
        self._ap(i, incs)

    def memset(self, ap, v, incs=()):
        i = self.eng.memset(ap, v) if self.on("gpsimd") else None
        self._ap(i, incs)

    def vcopy(self, out, in_, incs=()):
        i = self.eng.tensor_copy(out, in_) if self.on("vector") else None
        self._ap(i, incs)

    def vrecip(self, out, in_, incs=()):
        i = self.eng.reciprocal(out, in_) if self.on("vector") else None
        self._ap(i, incs)

    def vreduce(self, out, in_, op, incs=()):
        i = self.eng.tensor_reduce(out, in_, mybir.AxisListType.X, op) \
            if self.on("vector") else None
        self._ap(i, incs)

    def dma(self, q, out, in_):
        nm, k = DMA_RINGS[q]
        phys = f"{nm}{self.dma_rr[q] % k}"
        self.dma_rr[q] += 1
        if self.cnt[phys] > 0:
            self.wait(q, (phys, self.cnt[phys]))
        i = self.eng.dma_start(out=out, in_=in_) if self.on(q) else None
        self._ap(i, [(phys, 16)])
        return (phys, self.cnt[phys])

    def qmarks(self, q):
        nm, k = DMA_RINGS[q]
        return [(f"{nm}{i}", self.cnt[f"{nm}{i}"]) for i in range(k)]

    def collective(self, kind, op, groups, ins, outs, incs=()):
        i = self.eng.collective_compute(kind, op, replica_groups=groups,
                                        ins=ins, outs=outs) \
            if self.on("gpsimd") else None
        self._ap(i, incs)

    def reg(self, kind, name):
        if self.kind != kind:
            return None
        if name not in self.regs:
            self.regs[name] = self.regctx.enter_context(self.eng.register(name))
        return self.regs[name]

    def reg_load(self, kind, r, src_ap):
        if self.kind == kind:
            self.eng.reg_load(r, src_ap)

    def rap(self, kind, tensor, reg, pattern):
        if self.kind == kind:
            return bass.AP(tensor, reg, pattern)
        return None

    def barrier(self, kinds=ENGS):
        sems = [s for s in self.sems if s != "sCC"]
        evs = [self.mark(s) for s in sems]
        for kind in kinds:
            for ev in evs:
                self.wait(kind, ev)



def emit_program(g: G):
    cfg, t, nc = g.cfg, g.t, g.nc
    D, T, ND, NI, KV, QH, QB, NC, NH = (cfg.D, cfg.T, cfg.ND, cfg.NI, cfg.KV,
                                        cfg.QH, cfg.QB, cfg.NC, cfg.NH)
    PS, AB, WB, CK, FA = t["PS"], t["AB"], t["WB"], t["CK"], t["F32A"]
    ABW = AB.shape[1]
    TAIL0 = max(ND, QH) * T

    def acc_flags(first_it, last_it, hh, nhh=2, step=None):
        st = step if step is not None else NH
        start = bool(first_it and (hh * st) % 512 == 0)
        stop = bool(last_it and (((hh + 1) * st) % 512 == 0 or hh == nhh - 1))
        return start, stop

    def xn_tile(c):
        return AB[:, c * T:(c + 1) * T]

    def inter_tile(i):
        return AB[:, TAIL0 + i * T:TAIL0 + (i + 1) * T]

    def qav_tile(qh):
        return AB[:, qh * T:(qh + 1) * T]

    def kt_tile(h):
        return AB[:, TAIL0 + h * T:TAIL0 + (h + 1) * T]

    def vt_tile(h):
        return AB[:, TAIL0 + (KV + h) * T:TAIL0 + (KV + h + 1) * T]

    fa_x0 = FA[:, 0:T]
    fa_x1 = FA[:, T:2 * T]
    fa_tmp = FA[:, 2 * T:3 * T]
    fa_xw = FA[:, 3 * T:4 * T]
    cos_q, sin_q = FA[:, 0:T], FA[:, T:2 * T]
    cos_k, sin_k = FA[:, 2 * T:3 * T], FA[:, 3 * T:4 * T]
    fa_rope = FA[:, 4 * T:5 * T]
    mbrb = CK[:, 0:2 * T]
    knorm = t["masks"][0:1, 0:2 * T]

    ones_col = t["ones"][:, 0:1]
    ones_row = t["ones"][0:1, 0:128]
    zrow = t["zrow"][0:1, 0:128]
    zcol = t["zrow"][0:1, 0:1]
    ones_qb_row = t["masks"][0:1, 0:QB]

    for nm, dst in (("rt_in", t["rt"][:, :]),
                    ("ones_in", t["ones"][:, :]), ("zrow_in", t["zrow"][:, :]),
                    ("valid_in", t["valid"][:, :])):
        g.dma("sync", dst, t[nm][:, :])
    for k in ENGS:
        for ev in g.qmarks("sync"):
            g.wait(k, ev)

    def emit_ln(x_dram, resident, chunk_evs=None, first_write_guard=None):
        act_evs = []
        pe_evs = []
        for c in range(ND):
            buf = fa_x0 if (c % 2 == 0) else fa_x1
            sqv = t["sq"][:, (c % 2) * T:(c % 2) * T + T]
            xbf = xn_tile(c)
            lq = "sync" if (c % 2 == 0) else "scalar"
            if chunk_evs is not None:
                g.wait(lq, chunk_evs[c])
            if c >= 2:
                g.wait(lq, act_evs[c - 2])
                g.wait("scalar", pe_evs[c - 2])
            ev_x = g.dma(lq, buf, x_dram[c * 128:(c + 1) * 128, :])
            g.wait("scalar", ev_x)
            if c == 0 and first_write_guard is not None:
                g.wait("scalar", first_write_guard)
            g.act(sqv, buf, AF.Square)
            g.acopy(xbf, buf, incs=[("sACT", 1)])
            ev_sq = g.mark("sACT")
            act_evs.append(ev_sq)
            g.wait("tensor", ev_sq)
            for hh in range(2):
                sl = slice(hh * NH, (hh + 1) * NH)
                fst, lst = acc_flags(c == 0, c == ND - 1, hh)
                g.mm(PS[0:1, hh * NH:(hh + 1) * NH], ones_col, xbf[:, sl],
                     start=fst, stop=lst)
                g.mm(PS[0:1, 1024 + hh * NH:1024 + (hh + 1) * NH], ones_col,
                     sqv[:, sl], start=fst, stop=lst,
                     incs=([("sPE", 1)] if hh == 1 else []))
            pe_evs.append(g.mark("sPE"))
        ev_stats = g.mark("sPE")
        g.wait("scalar", ev_stats)
        g.acopy(t["r_sum"][:, :], PS[0:1, 0:T])
        g.acopy(t["r_sumsq"][:, :], PS[0:1, 1024:1024 + T], incs=[("sACT", 1)])
        ev_rows = g.mark("sACT")
        g.wait("vector", ev_rows)
        g.ts(t["r_mean"][:, :], t["r_sum"][:, :], 1.0 / D, None, ALU.mult,
             incs=[("sDVE", 1)])
        g.wait("vector", g.mark("sDVE"))
        g.tt(t["r_rstd"][:, :], t["r_mean"][:, :], t["r_mean"][:, :], ALU.mult,
             incs=[("sDVE", 1)])
        g.wait("vector", g.mark("sDVE"))
        g.stt(t["r_var"][:, :], t["r_sumsq"][:, :], 1.0 / D, t["r_rstd"][:, :],
              ALU.mult, ALU.subtract, incs=[("sDVE", 1)])
        g.wait("vector", g.mark("sDVE"))
        g.ts(t["r_var"][:, :], t["r_var"][:, :], LN_EPS, None, ALU.add,
             incs=[("sDVE", 1)])
        ev_var = g.mark("sDVE")
        g.wait("scalar", ev_var)
        g.act(t["r_rstd"][:, :], t["r_var"][:, :], AF.Ln,
              incs=[("sACT", 1)])
        g.wait("scalar", g.mark("sACT"))
        g.act(t["r_rstd"][:, :], t["r_rstd"][:, :], AF.Exp, scale=-0.5,
              incs=[("sACT", 1)])
        g.wait("scalar", g.mark("sACT"))
        g.acopy(t["r_meanbf"][:, :], t["r_mean"][:, :])
        g.acopy(t["r_rstdbf"][:, :], t["r_rstd"][:, :], incs=[("sACT", 1)])
        ev_rbf = g.mark("sACT")
        g.wait("tensor", ev_rbf)
        for hh in range(2):
            sl = slice(hh * NH, (hh + 1) * NH)
            fst, lst = acc_flags(True, True, hh)
            g.mm(PS[:, 2048 + hh * NH:2048 + (hh + 1) * NH], ones_row,
                 t["r_meanbf"][0:1, sl], start=fst, stop=lst)
            g.mm(PS[:, 3072 + hh * NH:3072 + (hh + 1) * NH], ones_row,
                 t["r_rstdbf"][0:1, sl], start=fst, stop=lst,
                 incs=([("sPE", 1)] if hh == 1 else []))
        ev_bc = g.mark("sPE")
        g.wait("scalar", ev_bc)
        g.acopy(mbrb[:, 0:T], PS[:, 2048:2048 + T])
        g.acopy(mbrb[:, T:2 * T], PS[:, 3072:3072 + T], incs=[("sACT", 1)])
        ev_mb = g.mark("sACT")
        assert resident
        xn_evs = [None] * ND
        g.wait("vector", ev_mb)
        g.wait("gpsimd", ev_mb)
        for c in range(ND):
            eng = "vector" if (c % 2 == 0) else "gpsimd"
            sem = "sDVE" if (c % 2 == 0) else "sPOOL"
            op = g.tt if (c % 2 == 0) else g.ptt
            scratch = fa_tmp if (c % 2 == 0) else fa_xw
            g.wait(eng, pe_evs[c])
            if c >= 2:
                g.wait(eng, xn_evs[c - 2])
            op(scratch, xn_tile(c), mbrb[:, 0:T], ALU.subtract,
               incs=[(sem, 1)])
            g.wait(eng, g.mark(sem))
            op(xn_tile(c), scratch, mbrb[:, T:2 * T], ALU.mult,
               incs=[(sem, 1)])
            xn_evs[c] = g.mark(sem)
        return xn_evs

    def emit_ffn(x_src, x_dst, x_src_evs=None):
        WSL = ND * 128
        silu_evs = {}
        up_evs = {}
        inter_done = {}
        chunk_write_evs = [None] * ND
        last_stt = {0: None, 1: None}
        last_wr2 = {0: None, 1: None}
        xw_bufs = (fa_xw, fa_tmp)
        prev_evac = None
        wrr = [0]
        slot_pe = {}

        def wload(dram_ap, cols):
            s = wrr[0] % 3
            wrr[0] += 1
            ev_free = slot_pe.get(s)
            if ev_free is not None:
                g.wait("gpsimd", ev_free)
            ev = g.dma("gpsimd", WB[:, s * WSL:s * WSL + cols], dram_ap)
            return s, ev

        for gi, group in enumerate(cfg.fgroups):
            ng = len(group)
            for idx, i in enumerate(group):
                sg, evg = wload(t["wgu"][i * 128:(i + 1) * 128, 0:WSL], WSL)
                su, evu = wload(t["wgu"][i * 128:(i + 1) * 128, WSL:2 * WSL], WSL)
                p = idx % 2
                g.wait("tensor", evg)
                g.wait("tensor", evu)
                if idx >= 2:
                    g.wait("tensor", silu_evs[(gi, idx - 2)])
                    g.wait("tensor", inter_done[(gi, idx - 2)])
                elif prev_evac is not None:
                    g.wait("tensor", prev_evac)
                gcol, ucol = p * 1024, 2048 + p * 1024
                gate_ev = None
                for (col, s) in ((gcol, sg), (ucol, su)):
                    for c in range(ND):
                        for hh in range(2):
                            sl = slice(hh * NH, (hh + 1) * NH)
                            last = (c == ND - 1)
                            fst, lst = acc_flags(c == 0, last, hh)
                            g.mm(PS[:, col + hh * NH:col + (hh + 1) * NH],
                                 WB[:, s * WSL + c * 128:s * WSL + (c + 1) * 128],
                                 xn_tile(c)[:, sl], start=fst, stop=lst,
                                 incs=([("sPE", 1)] if (last and hh == 1) else []))
                    if col == gcol:
                        gate_ev = g.mark("sPE")
                        slot_pe[sg] = gate_ev
                    else:
                        up_evs[(gi, idx)] = g.mark("sPE")
                        slot_pe[su] = up_evs[(gi, idx)]
                g.wait("scalar", gate_ev)
                sbuf = t["silu"][:, p * T:(p + 1) * T]
                g.act(sbuf, PS[:, gcol:gcol + T], AF.Silu, incs=[("sACT", 1)])
                silu_evs[(gi, idx)] = g.mark("sACT")
                g.wait("vector", silu_evs[(gi, idx)])
                g.wait("vector", up_evs[(gi, idx)])
                g.tt(inter_tile(idx), sbuf, PS[:, ucol:ucol + T], ALU.mult,
                     incs=[("sDVE", 1)])
                inter_done[(gi, idx)] = g.mark("sDVE")
            ev_ginter = g.mark("sDVE")

            nt = min(3, ng)
            bounds = []
            b0 = 0
            for tpart in range(nt):
                sz = (ng - b0 + (nt - tpart - 1)) // (nt - tpart)
                bounds.append((b0, b0 + sz))
                b0 += sz
            evac_by_o = {}
            xread_evs = {}
            for og in range(cfg.n_og):
                for o in range(cfg.OG):
                    ch = og * cfg.OG + o
                    par = ch % 2
                    buf = fa_x0 if par == 0 else fa_x1
                    src = x_dst if gi > 0 else x_src
                    if gi == 0 and x_src_evs is not None:
                        g.wait("sync", x_src_evs[ch])
                    if chunk_write_evs[ch] is not None:
                        g.wait("sync", chunk_write_evs[ch])
                    if last_stt[par] is not None:
                        g.wait("sync", last_stt[par])
                    if o < 2:
                        xread_evs[(og, o)] = g.dma(
                            "sync", buf, src[ch * 128:(ch + 1) * 128, :])
                third_ev = None
                for ti, (i0, i1) in enumerate(bounds):
                    w = (i1 - i0) * cfg.OG * 128
                    src0 = (group[0] + i0) * cfg.OG * 128
                    sd, evd = wload(t["wd"][og * 128:(og + 1) * 128,
                                            src0:src0 + w], w)
                    g.wait("tensor", evd)
                    if og == 0 and ti == 0:
                        g.wait("tensor", ev_ginter)
                        if prev_evac is not None:
                            g.wait("tensor", prev_evac)
                    nmm, tot = 0, (i1 - i0) * cfg.OG * 2
                    for ii in range(i0, i1):
                        for o in range(cfg.OG):
                            if ti == 0 and ii == i0 and og >= 1:
                                g.wait("tensor", evac_by_o[(og - 1, o)])
                            ocol = o * 1024
                            wcol = sd * WSL + ((ii - i0) * cfg.OG + o) * 128
                            for hh in range(2):
                                sl = slice(hh * NH, (hh + 1) * NH)
                                nmm += 1
                                fst, lst = acc_flags(ii == 0 and ti == 0,
                                                     ii == ng - 1, hh)
                                g.mm(PS[:, ocol + hh * NH:ocol + (hh + 1) * NH],
                                     WB[:, wcol:wcol + 128], inter_tile(ii)[:, sl],
                                     start=fst, stop=lst,
                                     incs=([("sPE", 1)] if nmm == tot else []))
                    third_ev = g.mark("sPE")
                    slot_pe[sd] = third_ev
                for o in range(cfg.OG):
                    ch = og * cfg.OG + o
                    par = ch % 2
                    buf = fa_x0 if par == 0 else fa_x1
                    src = x_dst if gi > 0 else x_src
                    if o >= 2:
                        if chunk_write_evs[ch] is not None:
                            g.wait("sync", chunk_write_evs[ch])
                        if last_stt[par] is not None:
                            g.wait("sync", last_stt[par])
                        xread_evs[(og, o)] = g.dma(
                            "sync", buf, src[ch * 128:(ch + 1) * 128, :])
                    wpar = o % 2
                    xw = xw_bufs[wpar]
                    g.wait("vector", xread_evs[(og, o)])
                    g.wait("vector", third_ev)
                    if last_wr2[wpar] is not None:
                        g.wait("vector", last_wr2[wpar])
                    g.stt(xw, PS[:, o * 1024:o * 1024 + T], 0.5, buf,
                          ALU.mult, ALU.add, incs=[("sDVE", 1)])
                    ev_e = g.mark("sDVE")
                    evac_by_o[(og, o)] = ev_e
                    last_stt[par] = ev_e
                    g.wait("sync", ev_e)
                    chunk_write_evs[ch] = g.dma(
                        "sync", x_dst[ch * 128:(ch + 1) * 128, :], xw)
                    last_wr2[wpar] = chunk_write_evs[ch]
            prev_evac = evac_by_o[(cfg.n_og - 1, cfg.OG - 1)]
        return chunk_write_evs

    def emit_proj(wdram, oc_n, ct_n, rhs_tile, evac, slots=4, wcols=None):
        wcols = wcols if wcols is not None else ct_n * 128
        pe_done = {}
        war = {}
        for oc in range(oc_n):
            p = oc % 2
            ev_free = pe_done.get(oc - 2)
            if ev_free is not None:
                g.wait("gpsimd", ev_free)
            ev_w = g.dma("gpsimd", WB[:, p * wcols:(p + 1) * wcols],
                         wdram[oc * 128:(oc + 1) * 128, 0:wcols])
            g.wait("tensor", ev_w)
            slot = oc % slots
            if war.get(oc - slots) is not None:
                g.wait("tensor", war[oc - slots])
            for c in range(ct_n):
                for hh in range(2):
                    sl = slice(hh * NH, (hh + 1) * NH)
                    last = (c == ct_n - 1)
                    fst, lst = acc_flags(c == 0, last, hh)
                    g.mm(PS[:, slot * 1024 + hh * NH:slot * 1024 + (hh + 1) * NH],
                         WB[:, p * wcols + c * 128:p * wcols + (c + 1) * 128],
                         rhs_tile(c)[:, sl], start=fst, stop=lst,
                         incs=([("sPE", 1)] if (last and hh == 1) else []))
            pe_done[oc] = g.mark("sPE")
            war[oc] = evac(oc, slot, pe_done[oc])
        return g.mark("sDVE"), g.mark("sACT"), g.mark("sPE")

    xn_evs = emit_ln(t["x_in"], resident=True)
    g.wait("tensor", xn_evs[-1])
    g.wait("tensor", xn_evs[-2])
    if cfg.debug:
        for c in range(ND):
            g.wait("sync", xn_evs[c])
            g.dma("sync", t["dbg_xn1"][c * 128:(c + 1) * 128, :], xn_tile(c))
    g.barrier()
    ffn1_evs = emit_ffn(t["x_in"], t["x_out"], x_src_evs=None)
    if cfg.debug:
        for c in range(ND):
            g.wait("sync", ffn1_evs[c])
            g.dma("sync", t["dbg_x1"][c * 128:(c + 1) * 128, :],
                  t["x_out"][c * 128:(c + 1) * 128, :])

    g.barrier()
    xn2_evs = emit_ln(t["x_out"], resident=True, chunk_evs=ffn1_evs)
    g.barrier()
    for nm, dst in (("cosq", cos_q), ("sinq", sin_q), ("cosk", cos_k),
                    ("sink", sin_k)):
        g.dma("sync", dst, t[nm][:, :])
    for ev in g.qmarks("sync"):
        g.wait("vector", ev)
        g.wait("tensor", ev)

    def ckv_tile(cc):
        return CK[:, cc * T:(cc + 1) * T]

    stage1 = CK[:, NC * T:NC * T + T]
    stage2 = CK[:, NC * T + T:NC * T + 2 * T]

    def evac_copy_to(dst_fn):
        def f(oc, slot, ev_pe):
            g.wait("scalar", ev_pe)
            g.acopy(dst_fn(oc), PS[:, slot * 1024:slot * 1024 + T],
                    incs=[("sACT", 1)])
            return g.mark("sACT")
        return f

    emit_proj(t["wdk"], NC, ND, xn_tile, evac_copy_to(ckv_tile))
    g.barrier()

    def make_rope_evac(cos_t, sin_t, post):
        st = {"rot_war": {}, "stage_pe": {}, "stage_dve": {}, "ro": {}}

        def evac(oc, slot, ev_pe):
            g.wait("scalar", ev_pe)
            if oc >= 1:
                g.wait("scalar", st["stage_pe"][oc - 1])
                g.wait("scalar", st["stage_dve"][oc - 1])
            g.acopy(stage1, PS[:, slot * 1024:slot * 1024 + T], incs=[("sACT", 1)])
            ev_k = g.mark("sACT")
            srot = 2 + (oc % 2)
            g.wait("tensor", ev_k)
            if oc >= 2:
                g.wait("tensor", st["rot_war"][oc - 2])
            for hh in range(2):
                sl = slice(hh * NH, (hh + 1) * NH)
                fst, lst = acc_flags(True, True, hh)
                g.mm(PS[:, srot * 1024 + hh * NH:srot * 1024 + (hh + 1) * NH],
                     t["rt"][:, :], stage1[:, sl], start=fst, stop=lst,
                     incs=([("sPE", 1)] if hh == 1 else []))
            ev_rot = g.mark("sPE")
            st["stage_pe"][oc] = ev_rot
            g.wait("vector", ev_k)
            if oc >= 1:
                for e in st["ro"][oc - 1]:
                    g.wait("vector", e)
            g.tt(fa_rope, stage1, cos_t, ALU.mult)
            g.wait("vector", ev_rot)
            g.tt(stage2, PS[:, srot * 1024:srot * 1024 + T], sin_t, ALU.mult,
                 incs=[("sDVE", 1)])
            g.wait("vector", g.mark("sDVE"))
            g.tt(stage2, fa_rope, stage2, ALU.add, incs=[("sDVE", 1)])
            ev_ro = g.mark("sDVE")
            st["stage_dve"][oc] = ev_ro
            ev_fins = post(oc, ev_ro)
            st["rot_war"][oc] = ev_fins[0]
            st["ro"][oc] = ev_fins
            return ev_k
        return evac

    def post_kr(oc, ev_ro):
        g.wait("vector", ev_ro)
        for hh in range(2):
            h = oc * 2 + hh
            g.vcopy(kt_tile(h)[0:64, :], stage2[hh * 64:(hh + 1) * 64, :],
                    incs=([("sDVE", 1)] if hh == 1 else []))
        return [g.mark("sDVE")]

    emit_proj(t["wkr"], NC, ND, xn_tile, make_rope_evac(cos_k, sin_k, post_kr),
              slots=2)
    g.barrier()

    kc_state = {}

    def evac_kc(oc, slot, ev_pe):
        g.wait("scalar", ev_pe)
        if oc >= 1:
            g.wait("scalar", kc_state[oc - 1])
        g.acopy(stage1, PS[:, slot * 1024:slot * 1024 + T], incs=[("sACT", 1)])
        ev_k = g.mark("sACT")
        g.wait("vector", ev_k)
        for hh in range(2):
            h = oc * 2 + hh
            g.vcopy(kt_tile(h)[64:128, :], stage1[hh * 64:(hh + 1) * 64, :],
                    incs=([("sDVE", 1)] if hh == 1 else []))
        ev = g.mark("sDVE")
        kc_state[oc] = ev
        return ev

    emit_proj(t["wupk"], NC, NC, ckv_tile, evac_kc, wcols=NC * 128)
    ev_kt_done = g.mark("sDVE")
    g.barrier()

    KVHD = KV * 128
    wupv_sb = WB[:, 0:NC * KVHD]
    ev_wupv = g.dma(
        "gpsimd", wupv_sb.rearrange("p (c d) -> p c d", c=NC),
        t["wupv"].ap().rearrange("(c p) d -> p c d", p=128))
    g.wait("tensor", ev_wupv)
    step = min(NH, KVHD)
    vt_evs = {}
    for tt in range(cfg.TT):
        slot = tt % 4
        if tt >= 4:
            g.wait("tensor", vt_evs[tt - 4])
        nmm, tot = 0, NC * (KVHD // step)
        for cc in range(NC):
            for hh in range(KVHD // step):
                sl = slice(hh * step, (hh + 1) * step)
                nmm += 1
                fst, lst = acc_flags(cc == 0, cc == NC - 1, hh,
                                     nhh=KVHD // step, step=step)
                g.mm(PS[:, slot * 1024 + hh * step:slot * 1024 + (hh + 1) * step],
                     ckv_tile(cc)[:, tt * 128:(tt + 1) * 128],
                     wupv_sb[:, cc * KVHD + hh * step:cc * KVHD + (hh + 1) * step],
                     start=fst, stop=lst,
                     incs=([("sPE", 1)] if nmm == tot else []))
        ev_pe = g.mark("sPE")
        g.wait("scalar", ev_pe)
        for h in range(KV):
            g.acopy(vt_tile(h)[:, tt * 128:(tt + 1) * 128],
                    PS[:, slot * 1024 + h * 128:slot * 1024 + (h + 1) * 128],
                    incs=([("sACT", 1)] if h == KV - 1 else []))
        vt_evs[tt] = g.mark("sACT")
    ev_vt_done = g.mark("sACT")

    ev_kn = {}
    knorm_ship = {}
    for h in range(KV):
        sqv = t["sq"][:, (h % 2) * T:(h % 2) * T + T]
        g.wait("scalar", ev_kt_done)
        if h >= 2:
            g.wait("scalar", ev_kn[h - 2])
        g.act(sqv, kt_tile(h), AF.Square, incs=[("sACT", 1)])
        ev_sq = g.mark("sACT")
        g.wait("tensor", ev_sq)
        if h == 0:
            g.wait("tensor", ev_vt_done)
        for hh in range(2):
            sl = slice(hh * NH, (hh + 1) * NH)
            fst, lst = acc_flags(True, True, hh)
            g.mm(PS[0:1, (h % 4) * 1024 + hh * NH:(h % 4) * 1024 + (hh + 1) * NH],
                 ones_col, sqv[:, sl], start=fst, stop=lst,
                 incs=([("sPE", 1)] if hh == 1 else []))
        ev_mm = g.mark("sPE")
        ev_kn[h] = ev_mm
        g.wait("scalar", ev_mm)
        if h >= 2:
            g.wait("scalar", knorm_ship[h - 2])
        g.acopy(knorm[0:1, (h % 2) * T:(h % 2) * T + T],
                PS[0:1, (h % 4) * 1024:(h % 4) * 1024 + T], incs=[("sACT", 1)])
        ev_k = g.mark("sACT")
        g.wait("sync", ev_k)
        knorm_ship[h] = g.dma(
            "sync", t["contrib"][2 * KV * 128 + h:2 * KV * 128 + h + 1, :],
            knorm[0:1, (h % 2) * T:(h % 2) * T + T])

    g.wait("sync", ev_kt_done)
    for h in range(KV):
        g.dma("sync", t["contrib"][h * 128:(h + 1) * 128, :], kt_tile(h))
    g.wait("sync", ev_vt_done)
    for h in range(KV):
        g.dma("sync",
              t["contrib"][KV * 128 + h * 128:KV * 128 + (h + 1) * 128, :],
              vt_tile(h))
    for ev in g.qmarks("sync"):
        g.wait("gpsimd", ev)
    g.collective("AllGather", ALU.bypass, [[0, 1, 2, 3], [4, 5, 6, 7]],
                 [t["contrib"].ap().opt()], [t["gout"].ap().opt()],
                 incs=[("sCC", 1)])
    ev_cc = g.mark("sCC")
    for ev in g.qmarks("sync"):
        g.wait("sync", ev)
    ev_masks = g.dma("sync", t["masks"][:, :], t["masks_in"][:, :])
    g.wait("tensor", ev_masks)
    g.wait("vector", ev_masks)

    if cfg.debug:
        g.wait("sync", ev_kt_done)
        g.wait("sync", ev_vt_done)
        for h in range(KV):
            g.dma("sync", t["dbg_kt"][h * 128:(h + 1) * 128, :], kt_tile(h))
            g.dma("sync", t["dbg_vt"][h * 128:(h + 1) * 128, :], vt_tile(h))

    q_state = {}

    def post_q(oc, ev_ro):
        g.wait("sync", ev_ro)
        ev_store = g.dma("sync", t["qtild"][oc * 128:(oc + 1) * 128, :], stage2)
        g.wait("scalar", ev_ro)
        if oc >= 2:
            g.wait("scalar", q_state[oc - 2])
        sqv = t["sq"][:, (oc % 2) * T:(oc % 2) * T + T]
        g.act(sqv, stage2, AF.Square, incs=[("sACT", 1)])
        ev_sq = g.mark("sACT")
        srot = 2 + (oc % 2)
        g.wait("tensor", ev_sq)
        for hh in range(2):
            sl = slice(hh * NH, (hh + 1) * NH)
            fst, lst = acc_flags(True, True, hh)
            g.mm(PS[0:1, srot * 1024 + hh * NH:srot * 1024 + (hh + 1) * NH],
                 ones_col, sqv[:, sl], start=fst, stop=lst,
                 incs=([("sPE", 1)] if hh == 1 else []))
        ev_qn = g.mark("sPE")
        q_state[oc] = ev_qn
        g.wait("scalar", ev_qn)
        g.acopy(t["r_qn2"][:, :], PS[0:1, srot * 1024:srot * 1024 + T],
                incs=[("sACT", 1)])
        ev_r = g.mark("sACT")
        g.wait("vector", ev_r)
        for qb in range(2):
            g.vreduce(t["r_qmax2"][0:1, oc * 2 + qb:oc * 2 + qb + 1],
                      t["r_qn2"][0:1, qb * QB:(qb + 1) * QB], ALU.max,
                      incs=([("sDVE", 1)] if qb == 1 else []))
        return [g.mark("sDVE"), ev_store]

    emit_proj(t["wq"], QH, ND, xn_tile,
              make_rope_evac(cos_q, sin_q, post_q), slots=2)
    ev_qproj = g.mark("sDVE")
    g.barrier()

    g.wait("sync", ev_cc)
    ev_ks = g.dma("sync", t["ksg2"][:, :],
                  bass.AP(t["gout"], 2 * KV * 128 * T,
                          [[cfg.CROWS * T, 4], [T, KV], [1, T]]))
    g.wait("vector", ev_ks)
    g.vreduce(t["r_kpm"][:, :], t["ksg2"][:, :], ALU.max, incs=[("sDVE", 1)])
    ev_kpm = g.mark("sDVE")
    g.wait("sync", ev_kpm)
    ev_kb = g.dma("sync", t["r_t1"][0:1, 24:24 + 4 * KV], t["r_kpm"][:, 0:1])
    g.wait("vector", ev_kb)
    g.tt(t["r_t1"][0:1, 56:56 + KV], t["r_t1"][0:1, 24:24 + KV],
         t["r_t1"][0:1, 24 + KV:24 + 2 * KV], ALU.max, incs=[("sDVE", 1)])
    g.wait("vector", g.mark("sDVE"))
    g.tt(t["r_t1"][0:1, 64:64 + KV], t["r_t1"][0:1, 24 + 2 * KV:24 + 3 * KV],
         t["r_t1"][0:1, 24 + 3 * KV:24 + 4 * KV], ALU.max, incs=[("sDVE", 1)])
    g.wait("vector", g.mark("sDVE"))
    g.tt(t["r_t1"][0:1, 8:8 + KV], t["r_t1"][0:1, 56:56 + KV],
         t["r_t1"][0:1, 64:64 + KV], ALU.max, incs=[("sDVE", 1)])
    ev_km = g.mark("sDVE")
    g.wait("scalar", ev_km)
    g.act(t["r_t1"][0:1, 0:KV], t["r_t1"][0:1, 8:8 + KV], AF.Ln,
          incs=[("sACT", 1)])
    ev_lnk0 = g.mark("sACT")
    g.wait("vector", ev_lnk0)
    g.ts(t["r_t1"][0:1, 16:16 + KV], t["r_t1"][0:1, 0:KV], 0.5, None, ALU.mult,
         incs=[("sDVE", 1)])
    ev_lnk = g.mark("sDVE")

    g.wait("scalar", ev_qproj)
    g.act(t["r_lnq"][:, :], t["r_qmax2"][:, :], AF.Ln, incs=[("sACT", 1)])
    ev_lnq = g.mark("sACT")
    g.wait("scalar", ev_lnq)
    g.wait("scalar", ev_lnk)
    for h in range(KV):
        g.act(t["r_Bv"][0:1, h * 8:(h + 1) * 8],
              t["r_lnq"][0:1, h * 8:(h + 1) * 8], AF.Exp, scale=0.5,
              bias=t["r_t1"][0:1, 16 + h:17 + h],
              incs=([("sACT", 1)] if h == KV - 1 else []))
    ev_b = g.mark("sACT")
    g.wait("vector", ev_b)
    g.ts(t["r_lnq"][:, :], t["r_Bv"][:, :], -1.0, -B_MARGIN, ALU.mult, ALU.add,
         incs=[("sDVE", 1)])
    ev_nb = g.mark("sDVE")
    g.wait("scalar", ev_nb)
    g.acopy(t["r_nbbf"][:, :], t["r_lnq"][:, :], incs=[("sACT", 1)])
    ev_nbf = g.mark("sACT")
    g.wait("tensor", ev_nbf)
    g.mm(PS[:, 2048:2048 + 2 * QH], ones_row, t["r_nbbf"][0:1, :],
         start=True, stop=True, incs=[("sPE", 1)])
    ev_nbc = g.mark("sPE")
    g.wait("scalar", ev_nbc)
    g.acopy(t["negB"][:, :], PS[:, 2048:2048 + 2 * QH], incs=[("sACT", 1)])
    ev_negB = g.mark("sACT")

    NSLOT = 3 + 7
    g.wait("vector", ev_nb)
    for s, (qb, jj) in enumerate(MAIN_SLOTS):
        g.vcopy(t["r_bx"][0:1, s * QH:(s + 1) * QH],
                bass.AP(t["r_lnq"], qb, [[2 * QH, 1], [2, QH]]),
                incs=([("sDVE", 1)] if s == NSLOT - 1 else []))
    g.wait("vector", g.mark("sDVE"))
    g.tt(t["r_bx"][:, :], t["r_bx"][:, :], t["valid"][:, :], ALU.mult,
         incs=[("sDVE", 1)])
    g.ts(t["r_bx2"][:, :], t["valid"][:, :], 1e9, -1e9, ALU.mult, ALU.add,
         incs=[("sDVE", 1)])
    g.wait("vector", g.mark("sDVE"))
    g.tt(t["r_bx"][:, :], t["r_bx"][:, :], t["r_bx2"][:, :], ALU.add,
         incs=[("sDVE", 1)])
    ev_bx = g.mark("sDVE")
    g.wait("scalar", ev_bx)
    g.acopy(t["r_bxbf"][:, :], t["r_bx"][:, :], incs=[("sACT", 1)])
    ev_bxbf = g.mark("sACT")
    g.wait("tensor", ev_bxbf)
    g.mm(PS[:, 2048:2048 + NSLOT * QH], ones_row, t["r_bxbf"][0:1, :],
         start=True, stop=True, incs=[("sPE", 1)])
    ev_bxm = g.mark("sPE")
    g.wait("scalar", ev_bxm)
    g.acopy(t["bias_x"][:, :], PS[:, 2048:2048 + NSLOT * QH], incs=[("sACT", 1)])
    ev_biasx = g.mark("sACT")

    g.barrier()
    for qh in range(QH):
        g.dma("sync", qav_tile(qh), t["qtild"][qh * 128:(qh + 1) * 128, :])
    for ev in g.qmarks("sync"):
        g.wait("tensor", ev)
    if cfg.debug:
        for qh in range(QH):
            g.dma("sync", t["dbg_q"][qh * 128:(qh + 1) * 128, :], qav_tile(qh))

    KT_C = cfg.KT_C
    NU = (3 + 7 + 2) * KT_C
    ev_unit_pe = {}
    ev_exp = {}
    ev_epi_dve = None
    ev_epi_act = None

    HSEG = 10 * T
    KTV = AB[:, TAIL0:TAIL0 + 2 * HSEG]

    def ktall(hp):
        return KTV[:, hp * HSEG: hp * HSEG + 4 * T]

    def vall(hp):
        return KTV[:, hp * HSEG + 4 * T: hp * HSEG + 8 * T]

    def ownkt(hp):
        return KTV[:, hp * HSEG + 8 * T: hp * HSEG + 9 * T]

    def ownvt(hp):
        return KTV[:, hp * HSEG + 9 * T: hp * HSEG + 10 * T]

    def chunk_col(jj, kt):
        rank = jj if jj < 4 else 7 - jj
        half = 0 if jj < 4 else 1
        return rank * T + half * QB + kt * 128

    ev_hload = {}
    ev_h_pe_done = {}

    def load_h(h):
        hp = h % 2
        if h >= 2:
            g.wait("sync", ev_h_pe_done[h - 2])
        else:
            g.wait("sync", ev_cc)
        evs = []
        for r in range(4):
            evs.append(g.dma("sync", ktall(hp)[:, r * T:(r + 1) * T],
                             t["gout"][r * cfg.CROWS + h * 128:
                                       r * cfg.CROWS + (h + 1) * 128, :]))
            evs.append(g.dma("sync", vall(hp)[:, r * T:(r + 1) * T],
                             t["gout"][r * cfg.CROWS + KV * 128 + h * 128:
                                       r * cfg.CROWS + KV * 128 + (h + 1) * 128, :]))
        evs.append(g.dma("sync", ownkt(hp),
                         t["contrib"][h * 128:(h + 1) * 128, :]))
        evs.append(g.dma("sync", ownvt(hp),
                         t["contrib"][KV * 128 + h * 128:
                                      KV * 128 + (h + 1) * 128, :]))
        ev_hload[h] = evs

    g.wait("scalar", ev_negB)
    g.wait("scalar", ev_biasx)
    load_h(0)
    load_h(1)

    esA = fa_rope
    esB = t["sq"][:, 0:2 * T].bitcast(F32)
    E = 2 if KT_C % 2 == 0 else 1
    NP = NU // E

    ev_exppair = {}
    ev_av = {}
    ev_esA = {}
    ev_esB = {}
    ev_epi_dve = None
    ev_epi_act = None
    ev_ese_prev = None

    for qh in range(QH):
        h = qh // 4
        hp = h % 2
        ev_ktls = ev_hload[h]
        if ev_epi_dve is not None:
            g.wait("tensor", ev_epi_dve)
            g.wait("gpsimd", ev_epi_dve)
        g.memset(esA[:, :], 0.0)
        g.memset(esB[:, :], 0.0, incs=[("sPOOL", 1)])
        ev_ms = g.mark("sPOOL")
        g.wait("vector", ev_ms)
        g.wait("gpsimd", ev_ms)
        for col in (2048, 2560):
            g.mm(PS[:, col:col + QB], zrow, ones_qb_row, start=True, stop=False,
                 skip=True, incs=([("sPE", 1)] if col == 2560 else []))
        ev_clear = g.mark("sPE")

        units = []
        for s, (qb, jj) in enumerate(MAIN_SLOTS):
            for kt in range(KT_C):
                units.append(("main", qb, s, chunk_col(jj, kt), kt))
        for qb in range(2):
            for kt in range(KT_C):
                units.append(("diag", qb, None, qb * QB + kt * 128, kt))

        def u_info(u):
            kind_u, qb, s, col, kt = units[u]
            lhs = (ktall(hp)[:, col:col + 128] if kind_u == "main"
                   else ownkt(hp)[:, col:col + 128])
            vlhs = (vall(hp)[:, col:col + 128] if kind_u == "main"
                    else ownvt(hp)[:, col:col + 128])
            if kind_u == "main":
                bias = t["bias_x"][:, s * QH + qh:s * QH + qh + 1]
            else:
                bias = t["negB"][:, qh * 2 + qb:qh * 2 + qb + 1]
            return kind_u, qb, col, kt, lhs, vlhs, bias

        def ebuf_of(u):
            p = u // E
            if E == 2:
                return t["eb"][:, (p % 2) * 2 * QB + (u % 2) * QB:
                               (p % 2) * 2 * QB + (u % 2) * QB + QB]
            return t["eb"][:, (u % 2) * QB:(u % 2) * QB + QB]

        ev_mask_done = {}

        def prev_pair(p, d):
            pp = p - d
            if pp >= 0:
                return (qh, pp)
            if qh > 0:
                return (qh - 1, NP + pp)
            return None

        def emit_av(u):
            kind_u, qb, col, kt, lhs, vlhs, bias = u_info(u)
            eb_u = ebuf_of(u)
            g.wait("tensor", ev_mask_done[(qh, u)])
            avcol = 2048 + qb * 512
            stop_acc = (u == 11 * KT_C - 1) or (u == NU - 1)
            g.mm(PS[:, avcol:avcol + QB], vlhs, eb_u,
                 start=False, stop=stop_acc, skip=True, incs=[("sPE", 1)])
            ev_av[(qh, u)] = g.mark("sPE")

        for p in range(NP):
            us = list(range(p * E, (p + 1) * E))
            for u in us:
                kind_u, qb, col, kt, lhs, vlhs, bias = u_info(u)
                if u == 0 and qh % 4 == 0:
                    for e_ in ev_ktls:
                        g.wait("tensor", e_)
                if u == us[0]:
                    pq = prev_pair(p, 2)
                    if pq is not None:
                        g.wait("tensor", ev_exppair[pq])
                    if p == 0:
                        g.wait("tensor", ev_clear)
                scol = (((p % 2) * 2 + (u % 2)) * 512 if E == 2
                        else (u % 2) * 512)
                g.mm(PS[:, scol:scol + QB], lhs,
                     qav_tile(qh)[:, qb * QB:(qb + 1) * QB],
                     start=True, stop=True,
                     incs=([("sPE", 1)] if u == us[-1] else []))
            ev_s = g.mark("sPE")
            if p > 0:
                for u in range((p - 1) * E, p * E):
                    emit_av(u)
            g.wait("scalar", ev_s)
            pq2 = prev_pair(p, 2)
            if pq2 is not None:
                qh2, pp2 = pq2
                for u2 in range(pp2 * E, (pp2 + 1) * E):
                    g.wait("scalar", ev_av[(qh2, u2)])
                    g.wait("scalar", ev_esA.get((qh2, u2), ("sDVE", 0)))
                    g.wait("scalar", ev_esB.get((qh2, u2), ("sPOOL", 0)))
            kind0, qb0_, col0, kt0, lhs0, vlhs0, bias0 = u_info(us[0])
            if E == 2:
                eb_grp = t["eb"][:, (p % 2) * 2 * QB:(p % 2) * 2 * QB + 2 * QB]
                base = (p % 2) * 1024
                g.act(eb_grp, PS[:, base:base + 512 + QB], AF.Exp, bias=bias0,
                      incs=[("sACT", 1)])
            else:
                scol = (us[0] % 2) * 512
                g.act(ebuf_of(us[0]), PS[:, scol:scol + QB], AF.Exp, bias=bias0,
                      incs=[("sACT", 1)])
            ev_e = g.mark("sACT")
            ev_exppair[(qh, p)] = ev_e
            for u in us:
                kind_u, qb, col, kt, lhs, vlhs, bias = u_info(u)
                eb_u = ebuf_of(u)
                if kind_u == "diag":
                    g.wait("vector", ev_e)
                    g.tt(eb_u, eb_u, t["masks"][:, kt * QB:(kt + 1) * QB],
                         ALU.mult, incs=[("sDVE", 1)])
                    ev_mask_done[(qh, u)] = g.mark("sDVE")
                else:
                    ev_mask_done[(qh, u)] = ev_e
                if u % 2 == 0:
                    g.wait("vector", ev_mask_done[(qh, u)])
                    g.wait("vector", g.mark("sDVE"))
                    g.tt(esA[:, qb * QB:(qb + 1) * QB],
                         esA[:, qb * QB:(qb + 1) * QB], eb_u, ALU.add,
                         incs=[("sDVE", 1)])
                    ev_esA[(qh, u)] = g.mark("sDVE")
                else:
                    g.wait("gpsimd", ev_mask_done[(qh, u)])
                    g.wait("gpsimd", g.mark("sPOOL"))
                    g.ptt(esB[:, qb * QB:(qb + 1) * QB],
                          esB[:, qb * QB:(qb + 1) * QB], eb_u, ALU.add,
                          incs=[("sPOOL", 1)])
                    ev_esB[(qh, u)] = g.mark("sPOOL")
        for u in range(NU - E, NU):
            emit_av(u)
        if qh % 4 == 3:
            ev_h_pe_done[h] = g.mark("sPE")
            if h + 2 < KV:
                load_h(h + 2)

        ev_last = ev_av[(qh, NU - 1)]
        g.wait("vector", ev_esB[(qh, NU - 1)])
        g.wait("vector", ev_esA[(qh, NU - 2)])
        g.tt(esA[:, :], esA[:, :], esB[:, :], ALU.add, incs=[("sDVE", 1)])
        ev_merge = g.mark("sDVE")
        g.wait("scalar", ev_merge)
        g.wait("scalar", ev_last)
        es_bf = t["eb"][:, 0:2 * QB]
        g.acopy(es_bf, esA[:, :], incs=[("sACT", 1)])
        ev_cast = g.mark("sACT")
        g.wait("tensor", ev_cast)
        for qb in range(2):
            g.mm(PS[0:1, 3072 + qb * 512:3072 + qb * 512 + QB], ones_col,
                 es_bf[:, qb * QB:(qb + 1) * QB], start=True, stop=True,
                 incs=([("sPE", 1)] if qb == 1 else []))
        ev_esmm = g.mark("sPE")
        g.wait("scalar", ev_esmm)
        g.acopy(t["r_esum"][0:1, 0:QB], PS[0:1, 3072:3072 + QB])
        g.acopy(t["r_esum"][0:1, QB:2 * QB], PS[0:1, 3584:3584 + QB],
                incs=[("sACT", 1)])
        ev_ese = g.mark("sACT")
        ev_ese_prev = ev_ese
        g.wait("vector", ev_ese)
        g.ts(t["r_esum"][:, :], t["r_esum"][:, :], 1e-30, None, ALU.add,
             incs=[("sDVE", 1)])
        g.wait("vector", g.mark("sDVE"))
        g.vrecip(t["r_recip"][:, :], t["r_esum"][:, :], incs=[("sDVE", 1)])
        ev_rc = g.mark("sDVE")
        g.wait("scalar", ev_rc)
        g.acopy(t["r_recipbf"][:, :], t["r_recip"][:, :], incs=[("sACT", 1)])
        ev_rcb = g.mark("sACT")
        g.wait("tensor", ev_rcb)
        if ev_epi_act is not None:
            g.wait("tensor", ev_epi_act)
        for qb in range(2):
            g.mm(PS[:, 3072 + qb * 512:3072 + qb * 512 + QB], ones_row,
                 t["r_recipbf"][0:1, qb * QB:(qb + 1) * QB], start=True, stop=True,
                 incs=([("sPE", 1)] if qb == 1 else []))
        ev_bcm = g.mark("sPE")
        g.wait("scalar", ev_bcm)
        for qb in range(2):
            g.acopy(t["recb"][:, qb * QB:(qb + 1) * QB],
                    PS[:, 3072 + qb * 512:3072 + qb * 512 + QB],
                    incs=([("sACT", 1)] if qb == 1 else []))
        ev_epi_act = g.mark("sACT")
        g.wait("vector", ev_epi_act)
        g.wait("vector", ev_last)
        for qb in range(2):
            g.tt(qav_tile(qh)[:, qb * QB:(qb + 1) * QB],
                 PS[:, 2048 + qb * 512:2048 + qb * 512 + QB],
                 t["recb"][:, qb * QB:(qb + 1) * QB], ALU.mult,
                 incs=([("sDVE", 1)] if qb == 1 else []))
        ev_epi_dve = g.mark("sDVE")

    if cfg.debug:
        g.wait("sync", ev_epi_dve)
        for qh in range(QH):
            g.dma("sync", t["dbg_av"][qh * 128:(qh + 1) * 128, :], qav_tile(qh))

    g.barrier()
    attn_out_evs = [None] * ND
    wo_last = {0: None, 1: None}
    wo_wr = None

    def evac_wo(oc, slot, ev_pe):
        nonlocal wo_wr
        par = oc % 2
        buf = fa_x0 if par == 0 else fa_x1
        if wo_last[par] is not None:
            g.wait("sync", wo_last[par])
        ev_x = g.dma("sync", buf, t["x_out"][oc * 128:(oc + 1) * 128, :])
        g.wait("vector", ev_x)
        g.wait("vector", ev_pe)
        if wo_wr is not None:
            g.wait("vector", wo_wr)
        g.tt(fa_xw, PS[:, slot * 1024:slot * 1024 + T], buf, ALU.add,
             incs=[("sDVE", 1)])
        ev_e = g.mark("sDVE")
        wo_last[par] = ev_e
        g.wait("sync", ev_e)
        attn_out_evs[oc] = g.dma(
            "sync", t["x_out"][oc * 128:(oc + 1) * 128, :], fa_xw)
        wo_wr = attn_out_evs[oc]
        return ev_e

    emit_proj(t["wo"], ND, QH, qav_tile, evac_wo)
    if cfg.debug:
        for c in range(ND):
            g.wait("sync", attn_out_evs[c])
            g.dma("sync", t["dbg_x2"][c * 128:(c + 1) * 128, :],
                  t["x_out"][c * 128:(c + 1) * 128, :])

    g.barrier()
    ln3_guard = g.mark("sPE")
    xn3_evs = emit_ln(t["x_out"], resident=True, chunk_evs=attn_out_evs,
                      first_write_guard=ln3_guard)
    g.wait("tensor", xn3_evs[-1])
    g.wait("tensor", xn3_evs[-2])
    g.barrier(kinds=("sync", "gpsimd"))
    ffn2_evs = emit_ffn(t["x_out"], t["x_out"], x_src_evs=attn_out_evs)

    for ev in ffn2_evs:
        g.wait("sync", ev)
    g.barrier(kinds=("sync",))



def declare(nc, cfg):
    from contextlib import ExitStack
    ctx = ExitStack()
    t = {}
    D, T, ND, NI, KV, QH, QB, NC = (cfg.D, cfg.T, cfg.ND, cfg.NI, cfg.KV,
                                    cfg.QH, cfg.QB, cfg.NC)

    t["x_in"] = nc.dram_tensor("x_in", [D, T], F32, kind="ExternalInput")
    t["wq"] = nc.dram_tensor("wq", [QH * 128, D], F32, kind="ExternalInput")
    t["wo"] = nc.dram_tensor("wo", [D, QH * 128], F32, kind="ExternalInput")
    t["wkr"] = nc.dram_tensor("wkr", [cfg.CD, D], F32, kind="ExternalInput")
    t["wdk"] = nc.dram_tensor("wdk", [cfg.CD, D], F32, kind="ExternalInput")
    t["wupk"] = nc.dram_tensor("wupk", [cfg.CD, cfg.CD], F32, kind="ExternalInput")
    t["wupv"] = nc.dram_tensor("wupv", [cfg.CD, KV * 128], F32, kind="ExternalInput")
    t["wgu"] = nc.dram_tensor("wgu", [NI * 128, 2 * ND * 128], F32,
                              kind="ExternalInput")
    t["wd"] = nc.dram_tensor("wd", [cfg.n_og * 128, NI * cfg.OG * 128], F32,
                             kind="ExternalInput")
    for nm in ("cosq", "sinq", "cosk", "sink"):
        t[nm] = nc.dram_tensor(nm, [128, T], F32, kind="ExternalInput")
    t["masks_in"] = nc.dram_tensor("masks_in", [128, 4 * QB], BF16,
                                   kind="ExternalInput")
    t["rt_in"] = nc.dram_tensor("rt_in", [128, 128], BF16, kind="ExternalInput")
    t["ones_in"] = nc.dram_tensor("ones_in", [128, 130], BF16, kind="ExternalInput")
    t["zrow_in"] = nc.dram_tensor("zrow_in", [1, 128], BF16, kind="ExternalInput")
    t["valid_in"] = nc.dram_tensor("valid_in", [1, 10 * QH], F32,
                                   kind="ExternalInput")
    t["x_out"] = nc.dram_tensor("x_out", [D, T], F32, kind="ExternalOutput")

    t["xn2d"] = nc.dram_tensor("xn2d", [D, T], BF16)
    t["qtild"] = nc.dram_tensor("qtild", [QH * 128, T], BF16)
    t["contrib"] = nc.dram_tensor("contrib", [cfg.CROWS, T], BF16)
    t["gout"] = nc.dram_tensor("gout", [cfg.GR, T], BF16)

    if cfg.debug:
        t["dbg_xn1"] = nc.dram_tensor("dbg_xn1", [D, T], BF16, kind="ExternalOutput")
        t["dbg_x1"] = nc.dram_tensor("dbg_x1", [D, T], F32, kind="ExternalOutput")
        t["dbg_kt"] = nc.dram_tensor("dbg_kt", [KV * 128, T], BF16,
                                     kind="ExternalOutput")
        t["dbg_vt"] = nc.dram_tensor("dbg_vt", [KV * 128, T], BF16,
                                     kind="ExternalOutput")
        t["dbg_q"] = nc.dram_tensor("dbg_q", [QH * 128, T], BF16,
                                    kind="ExternalOutput")
        t["dbg_av"] = nc.dram_tensor("dbg_av", [QH * 128, T], BF16,
                                     kind="ExternalOutput")
        t["dbg_x2"] = nc.dram_tensor("dbg_x2", [D, T], F32, kind="ExternalOutput")

    def sb(name, shape, dtype):
        t[name] = ctx.enter_context(nc.sbuf_tensor(name, shape, dtype))

    max_g = max(len(gg) for gg in cfg.fgroups)
    sb("AB", [128, max(ND, QH) * T + max(max_g, 20) * T], BF16)
    wb_cols = max(3 * ND * 128,
                  2 * ((max_g + 1) // 2) * cfg.OG * 128,
                  2 * ND * 128, 2 * QH * 128, 2 * NC * 128)
    sb("WB", [128, wb_cols], BF16)
    sb("CK", [128, NC * T + 2 * T], BF16)
    sb("F32A", [128, 5 * T], F32)
    sb("masks", [128, 4 * QB], BF16)
    sb("rt", [128, 128], BF16)
    sb("ones", [128, 130], BF16)
    sb("zrow", [1, 128], BF16)
    sb("valid", [1, 10 * QH], F32)
    sb("eb", [128, 4 * QB], BF16)
    sb("recb", [128, 2 * QB], BF16)
    sb("negB", [128, 2 * QH], F32)
    sb("bias_x", [128, 10 * QH], F32)
    sb("silu", [128, 2 * T], BF16)
    sb("sq", [128, 2 * T], BF16)
    sb("ksg2", [4 * KV, T], BF16)
    sb("r_kpm", [4 * KV, 1], F32)
    for nm in ("r_sum", "r_sumsq", "r_rstd"):
        sb(nm, [1, T], F32)
    t["r_mean"] = t["r_sum"]
    t["r_var"] = t["r_sumsq"]
    t["r_qn2"] = t["r_rstd"]
    t["r_esum"] = t["r_sum"]
    t["r_recip"] = t["r_sumsq"]
    sb("r_meanbf", [1, T], BF16)
    sb("r_rstdbf", [1, T], BF16)
    sb("r_recipbf", [1, 2 * QB], BF16)
    sb("r_qmax2", [1, 2 * QH], F32)
    sb("r_lnq", [1, 2 * QH], F32)
    sb("r_nbbf", [1, 2 * QH], BF16)
    sb("r_bx", [1, 10 * QH], F32)
    sb("r_bx2", [1, 10 * QH], F32)
    sb("r_bxbf", [1, 10 * QH], BF16)
    sb("r_t1", [1, 96], F32)
    sb("r_Bv", [1, 2 * QH], F32)

    t["PS"] = ctx.enter_context(nc.psum_tensor("PS", [128, 4096], F32))

    sems = {}
    for nm in SEMS:
        sems[nm] = ctx.enter_context(nc.semaphore(nm))
    return t, sems, ctx


def build_program(cfg):
    from contextlib import ExitStack
    nc = bass.Bass("TRN2", target_bir_lowering=False)
    tn, sems, ctx = declare(nc, cfg)
    g = G(nc, cfg, tn, sems)
    with ctx:
        with nc.Block() as block:
            def run_walk(kind):
                def f(eng):
                    with ExitStack() as regctx:
                        g.start_walk(kind, eng, regctx)
                        emit_program(g)
                return f
            block.tensor(run_walk("tensor"))
            block.scalar(run_walk("scalar"))
            block.vector(run_walk("vector"))
            block.gpsimd(run_walk("gpsimd"))
            block.sync(run_walk("sync"))
    return nc


_CACHED = {}


def kernel(**inputs):
    cfg = REAL
    key = "real"
    if key not in _CACHED:
        _CACHED[key] = build_program(cfg)
    nc = _CACHED[key]
    in_maps = host_prepare(inputs, cfg)
    res = run_bass_kernel_spmd(nc, in_maps, core_ids=list(range(8)))
    return host_assemble(res.results, cfg)


# version 10 (final)
# speedup vs baseline: 1.1040x; 1.0057x over previous
import math
import numpy as np
import ml_dtypes

import concourse.bass as bass
import concourse.mybir as mybir
from concourse.bass_utils import run_bass_kernel_spmd

F32 = mybir.dt.float32
BF16 = mybir.dt.bfloat16
I32 = mybir.dt.int32
AF = mybir.ActivationFunctionType
ALU = mybir.AluOpType

LN_EPS = 1e-5
B_MARGIN = 2.0
MAIN_SLOTS = [(0, jj) for jj in range(3)] + [(1, jj) for jj in range(7)]


class Cfg:
    def __init__(self, D, I, KV, T, n_fgroups, debug=False):
        self.D = D
        self.I = I
        self.KV = KV
        self.QH = 4 * KV
        self.HD = 128
        self.T = T
        self.QB = T // 2
        self.NH = T // 2
        assert self.NH <= 512 and self.QB % 128 == 0
        self.ND = D // 128
        self.NI = I // 128
        self.CD = 64 * KV
        self.NC = self.CD // 128
        self.TT = T // 128
        self.KT_C = self.QB // 128
        self.n_fgroups = n_fgroups
        base, rem = self.NI // n_fgroups, self.NI % n_fgroups
        sizes = [base + (1 if g_ < rem else 0) for g_ in range(n_fgroups)]
        self.fgroups = []
        s = 0
        for sz in sizes:
            self.fgroups.append(list(range(s, s + sz)))
            s += sz
        self.OG = min(4, self.ND)
        self.n_og = self.ND // self.OG
        self.CROWS = 2 * KV * 128 + KV
        self.GR = 4 * self.CROWS
        self.debug = debug
        self.wgu_cols = 2 * self.ND * 128


REAL = Cfg(D=4096, I=11008, KV=8, T=1024, n_fgroups=5)
SMALL = Cfg(D=512, I=512, KV=2, T=256, n_fgroups=2, debug=True)
MID = Cfg(D=512, I=512, KV=2, T=1024, n_fgroups=2, debug=True)
MID2 = Cfg(D=512, I=512, KV=8, T=1024, n_fgroups=2, debug=True)
MID3 = Cfg(D=4096, I=512, KV=8, T=1024, n_fgroups=2, debug=True)



def sigma_perm():
    return np.concatenate([np.arange(0, 32), np.arange(64, 96),
                           np.arange(32, 64), np.arange(96, 128)])


def host_tile_lhsT(W):
    O, K = W.shape
    no, nk = O // 128, K // 128
    Wb = W.reshape(no, 128, nk, 128)
    Wt = Wb.transpose(0, 3, 2, 1)
    return np.ascontiguousarray(Wt.reshape(O, K))


def host_tile_wd(Wd, cfg):
    Wb = Wd.reshape(cfg.n_og, cfg.OG, 128, cfg.NI, 128)
    Wt = Wb.transpose(0, 4, 3, 1, 2)
    return np.ascontiguousarray(Wt.reshape(cfg.n_og * 128, cfg.NI * cfg.OG * 128))


def host_rope_tables(positions, cfg):
    HD = cfg.HD
    inv = 1.0 / (10000.0 ** (np.arange(0, HD, 2, dtype=np.float64) / HD))
    fr = np.outer(inv, positions.astype(np.float64))
    cos = np.cos(fr).astype(np.float32)
    sin = np.sin(fr).astype(np.float32)
    rows_q = np.concatenate([np.arange(32), np.arange(32),
                             np.arange(32, 64), np.arange(32, 64)])
    rows_k = np.concatenate([np.arange(32), np.arange(32)] * 2)
    s = 1.0 / math.sqrt(HD)
    return (cos[rows_q] * s, sin[rows_q] * s, cos[rows_k], sin[rows_k])


def host_R_matrix():
    R = np.zeros((128, 128), np.float32)
    for b in (0, 64):
        for i in range(32):
            R[b + i, b + i + 32] = -1.0
            R[b + 32 + i, b + i] = 1.0
    return np.ascontiguousarray(R.T)


def host_masks(cfg):
    QB = cfg.QB
    out = np.zeros((128, 4 * QB), np.float32)
    q = np.arange(QB)
    k = np.arange(128)
    for kt in range(4):
        kk = min(kt, cfg.KT_C - 1)
        out[:, kt * QB:(kt + 1) * QB] = (q[None, :] >= (kk * 128 + k)[:, None])
    return out.astype(ml_dtypes.bfloat16)


def host_valid(core, cfg):
    c = core % 4
    own = {0: c, 1: 7 - c}
    v = np.zeros((1, 10 * cfg.QH), np.float32)
    for s, (qb, jj) in enumerate(MAIN_SLOTS):
        if jj < own[qb]:
            v[0, s * cfg.QH:(s + 1) * cfg.QH] = 1.0
    return v


def host_prepare(inputs, cfg):
    x = np.asarray(inputs["L0_attn_hidden_states"], np.float32)
    Wq = np.asarray(inputs["L0_attn_Wq"], np.float32)
    Wkr = np.asarray(inputs["L0_attn_Wkr"], np.float32)
    Wdk = np.asarray(inputs["L0_attn_Wdk"], np.float32)
    Wupk = np.asarray(inputs["L0_attn_Wupk"], np.float32)
    Wupv = np.asarray(inputs["L0_attn_Wupv"], np.float32)
    Wo = np.asarray(inputs["L0_attn_Wo"], np.float32)
    Wg = np.asarray(inputs["L0_ffn_w_gate"], np.float32)
    Wu = np.asarray(inputs["L0_ffn_w_up"], np.float32)
    Wd = np.asarray(inputs["L0_ffn_w_down"], np.float32)

    QB = cfg.QB
    sp = sigma_perm()
    qrows = np.concatenate([qh * 128 + sp for qh in range(cfg.QH)])
    wq_t = host_tile_lhsT(Wq[qrows])
    wo_t = host_tile_lhsT(Wo)
    wkr_t = host_tile_lhsT(Wkr)
    wdk_t = host_tile_lhsT(Wdk)
    wupk_t = host_tile_lhsT(Wupk)
    wupv_t = np.ascontiguousarray(Wupv.T)

    wg_t = host_tile_lhsT(Wg).reshape(cfg.NI, 128, cfg.ND * 128)
    wu_t = host_tile_lhsT(Wu).reshape(cfg.NI, 128, cfg.ND * 128)
    wgu_t = np.ascontiguousarray(
        np.concatenate([wg_t, wu_t], axis=2)).reshape(cfg.NI * 128, -1)
    wd_t = host_tile_wd(Wd, cfg)

    masks = host_masks(cfg)
    RT = host_R_matrix().astype(ml_dtypes.bfloat16)
    ones_seg = np.ones((128, 130), ml_dtypes.bfloat16)
    zrow = np.zeros((1, 128), ml_dtypes.bfloat16)

    in_maps = []
    for core in range(8):
        b, c = core // 4, core % 4
        toks = np.concatenate([np.arange(c * QB, (c + 1) * QB),
                               np.arange((7 - c) * QB, (8 - c) * QB)])
        xc = np.ascontiguousarray(x[b][toks].T)
        cosq, sinq, cosk, sink = host_rope_tables(toks, cfg)
        in_maps.append({
            "x_in": xc,
            "wq": wq_t, "wo": wo_t, "wkr": wkr_t, "wdk": wdk_t,
            "wupk": wupk_t, "wupv": wupv_t, "wgu": wgu_t, "wd": wd_t,
            "cosq": np.ascontiguousarray(cosq), "sinq": np.ascontiguousarray(sinq),
            "cosk": np.ascontiguousarray(cosk), "sink": np.ascontiguousarray(sink),
            "masks_in": masks, "rt_in": RT, "ones_in": ones_seg, "zrow_in": zrow,
            "valid_in": host_valid(core, cfg),
        })
    return in_maps


def host_assemble(results, cfg):
    out = np.zeros((2, 8 * cfg.QB, cfg.D), np.float32)
    for core in range(8):
        b, c = core // 4, core % 4
        o = results[core]["x_out"]
        out[b, c * cfg.QB:(c + 1) * cfg.QB] = o[:, 0:cfg.QB].T
        out[b, (7 - c) * cfg.QB:(8 - c) * cfg.QB] = o[:, cfg.QB:].T
    return out



ENGS = ("tensor", "scalar", "vector", "gpsimd", "sync")
DMA_RINGS = {"sync": ("sSY", 8), "scalar": ("sSC", 4), "gpsimd": ("sWD", 8)}
SEMS = (["sPE", "sACT", "sDVE", "sPOOL", "sCC"]
        + [f"{nm}{i}" for nm, k in DMA_RINGS.values() for i in range(k)])


class G:
    def __init__(self, nc, cfg, tn, sems):
        self.nc, self.cfg, self.t, self.sems = nc, cfg, tn, sems

    def start_walk(self, kind, eng, regctx):
        self.kind, self.eng = kind, eng
        self.cnt = {k: 0 for k in self.sems}
        self.regs = {}
        self.regctx = regctx
        self.dma_rr = {q: 0 for q in DMA_RINGS}

    def bump(self, sem, n):
        self.cnt[sem] += n
        return self.cnt[sem]

    def mark(self, sem):
        return (sem, self.cnt[sem])

    def _ap(self, instr, incs):
        for sem, n in incs:
            self.bump(sem, n)
            if instr is not None:
                instr.then_inc(self.sems[sem], n)

    def wait(self, kind, ev):
        sem, val = ev
        if val > 0 and self.kind == kind:
            self.eng.wait_ge(self.sems[sem], val)

    def on(self, kind):
        return self.kind == kind

    def mm(self, out, lhsT, rhs, start, stop, incs=(), skip=False):
        i = self.eng.matmul(out, lhsT, rhs, start=start, stop=stop,
                            skip_group_check=skip) if self.on("tensor") else None
        self._ap(i, incs)

    def act(self, out, in_, func, bias=0.0, scale=1.0, incs=()):
        i = self.eng.activation(out, in_, func, bias=bias, scale=scale) \
            if self.on("scalar") else None
        self._ap(i, incs)

    def acopy(self, out, in_, incs=()):
        i = self.eng.copy(out, in_) if self.on("scalar") else None
        self._ap(i, incs)

    def tt(self, out, a, b, op, incs=()):
        i = self.eng.tensor_tensor(out, a, b, op) if self.on("vector") else None
        self._ap(i, incs)

    def stt(self, out, in0, scalar, in1, op0, op1, incs=()):
        i = self.eng.scalar_tensor_tensor(out, in0, scalar, in1, op0, op1) \
            if self.on("vector") else None
        self._ap(i, incs)

    def ts(self, out, in0, s1, s2, op0, op1=ALU.mult, incs=()):
        if self.on("vector"):
            if s2 is None:
                i = self.eng.tensor_single_scalar(out, in0, s1, op0)
            else:
                i = self.eng.tensor_scalar(out, in0, s1, s2, op0, op1)
        else:
            i = None
        self._ap(i, incs)

    def ptt(self, out, a, b, op, incs=()):
        i = self.eng.tensor_tensor(out, a, b, op) if self.on("gpsimd") else None
        self._ap(i, incs)

    def memset(self, ap, v, incs=()):
        i = self.eng.memset(ap, v) if self.on("gpsimd") else None
        self._ap(i, incs)

    def vcopy(self, out, in_, incs=()):
        i = self.eng.tensor_copy(out, in_) if self.on("vector") else None
        self._ap(i, incs)

    def vrecip(self, out, in_, incs=()):
        i = self.eng.reciprocal(out, in_) if self.on("vector") else None
        self._ap(i, incs)

    def vreduce(self, out, in_, op, incs=()):
        i = self.eng.tensor_reduce(out, in_, mybir.AxisListType.X, op) \
            if self.on("vector") else None
        self._ap(i, incs)

    def dma(self, q, out, in_):
        nm, k = DMA_RINGS[q]
        phys = f"{nm}{self.dma_rr[q] % k}"
        self.dma_rr[q] += 1
        if self.cnt[phys] > 0:
            self.wait(q, (phys, self.cnt[phys]))
        i = self.eng.dma_start(out=out, in_=in_) if self.on(q) else None
        self._ap(i, [(phys, 16)])
        return (phys, self.cnt[phys])

    def qmarks(self, q):
        nm, k = DMA_RINGS[q]
        return [(f"{nm}{i}", self.cnt[f"{nm}{i}"]) for i in range(k)]

    def collective(self, kind, op, groups, ins, outs, incs=()):
        i = self.eng.collective_compute(kind, op, replica_groups=groups,
                                        ins=ins, outs=outs) \
            if self.on("gpsimd") else None
        self._ap(i, incs)

    def reg(self, kind, name):
        if self.kind != kind:
            return None
        if name not in self.regs:
            self.regs[name] = self.regctx.enter_context(self.eng.register(name))
        return self.regs[name]

    def reg_load(self, kind, r, src_ap):
        if self.kind == kind:
            self.eng.reg_load(r, src_ap)

    def rap(self, kind, tensor, reg, pattern):
        if self.kind == kind:
            return bass.AP(tensor, reg, pattern)
        return None

    def barrier(self, kinds=ENGS):
        sems = [s for s in self.sems if s != "sCC"]
        evs = [self.mark(s) for s in sems]
        for kind in kinds:
            for ev in evs:
                self.wait(kind, ev)



def emit_program(g: G):
    cfg, t, nc = g.cfg, g.t, g.nc
    D, T, ND, NI, KV, QH, QB, NC, NH = (cfg.D, cfg.T, cfg.ND, cfg.NI, cfg.KV,
                                        cfg.QH, cfg.QB, cfg.NC, cfg.NH)
    PS, AB, WB, CK, FA = t["PS"], t["AB"], t["WB"], t["CK"], t["F32A"]
    ABW = AB.shape[1]
    TAIL0 = max(ND, QH) * T

    def acc_flags(first_it, last_it, hh, nhh=2, step=None):
        st = step if step is not None else NH
        start = bool(first_it and (hh * st) % 512 == 0)
        stop = bool(last_it and (((hh + 1) * st) % 512 == 0 or hh == nhh - 1))
        return start, stop

    def xn_tile(c):
        return AB[:, c * T:(c + 1) * T]

    def inter_tile(i):
        return AB[:, TAIL0 + i * T:TAIL0 + (i + 1) * T]

    def qav_tile(qh):
        return AB[:, qh * T:(qh + 1) * T]

    def kt_tile(h):
        return AB[:, TAIL0 + h * T:TAIL0 + (h + 1) * T]

    def vt_tile(h):
        return AB[:, TAIL0 + (KV + h) * T:TAIL0 + (KV + h + 1) * T]

    fa_x0 = FA[:, 0:T]
    fa_x1 = FA[:, T:2 * T]
    fa_tmp = FA[:, 2 * T:3 * T]
    fa_xw = FA[:, 3 * T:4 * T]
    cos_q, sin_q = FA[:, 0:T], FA[:, T:2 * T]
    cos_k, sin_k = FA[:, 2 * T:3 * T], FA[:, 3 * T:4 * T]
    fa_rope = FA[:, 4 * T:5 * T]
    mbrb = CK[:, 0:2 * T]
    knorm = t["masks"][0:1, 0:2 * T]

    ones_col = t["ones"][:, 0:1]
    ones_row = t["ones"][0:1, 0:128]
    zrow = t["zrow"][0:1, 0:128]
    zcol = t["zrow"][0:1, 0:1]
    ones_qb_row = t["masks"][0:1, 0:QB]

    for nm, dst in (("rt_in", t["rt"][:, :]),
                    ("ones_in", t["ones"][:, :]), ("zrow_in", t["zrow"][:, :]),
                    ("valid_in", t["valid"][:, :])):
        g.dma("sync", dst, t[nm][:, :])
    for k in ENGS:
        for ev in g.qmarks("sync"):
            g.wait(k, ev)

    def emit_ln(x_dram, resident, chunk_evs=None, first_write_guard=None):
        act_evs = []
        pe_evs = []
        for c in range(ND):
            buf = fa_x0 if (c % 2 == 0) else fa_x1
            sqv = t["sq"][:, (c % 2) * T:(c % 2) * T + T]
            xbf = xn_tile(c)
            if chunk_evs is not None:
                g.wait("sync", chunk_evs[c])
            if c >= 2:
                g.wait("sync", act_evs[c - 2])
                g.wait("scalar", pe_evs[c - 2])
            ev_x = g.dma("sync", buf, x_dram[c * 128:(c + 1) * 128, :])
            g.wait("scalar", ev_x)
            if c == 0 and first_write_guard is not None:
                g.wait("scalar", first_write_guard)
            g.act(sqv, buf, AF.Square)
            g.acopy(xbf, buf, incs=[("sACT", 1)])
            ev_sq = g.mark("sACT")
            act_evs.append(ev_sq)
            g.wait("tensor", ev_sq)
            for hh in range(2):
                sl = slice(hh * NH, (hh + 1) * NH)
                fst, lst = acc_flags(c == 0, c == ND - 1, hh)
                g.mm(PS[0:1, hh * NH:(hh + 1) * NH], ones_col, xbf[:, sl],
                     start=fst, stop=lst)
                g.mm(PS[0:1, 1024 + hh * NH:1024 + (hh + 1) * NH], ones_col,
                     sqv[:, sl], start=fst, stop=lst,
                     incs=([("sPE", 1)] if hh == 1 else []))
            pe_evs.append(g.mark("sPE"))
        ev_stats = g.mark("sPE")
        g.wait("scalar", ev_stats)
        g.acopy(t["r_sum"][:, :], PS[0:1, 0:T])
        g.acopy(t["r_sumsq"][:, :], PS[0:1, 1024:1024 + T], incs=[("sACT", 1)])
        ev_rows = g.mark("sACT")
        g.wait("vector", ev_rows)
        g.ts(t["r_mean"][:, :], t["r_sum"][:, :], 1.0 / D, None, ALU.mult,
             incs=[("sDVE", 1)])
        g.wait("vector", g.mark("sDVE"))
        g.tt(t["r_rstd"][:, :], t["r_mean"][:, :], t["r_mean"][:, :], ALU.mult,
             incs=[("sDVE", 1)])
        g.wait("vector", g.mark("sDVE"))
        g.stt(t["r_var"][:, :], t["r_sumsq"][:, :], 1.0 / D, t["r_rstd"][:, :],
              ALU.mult, ALU.subtract, incs=[("sDVE", 1)])
        g.wait("vector", g.mark("sDVE"))
        g.ts(t["r_var"][:, :], t["r_var"][:, :], LN_EPS, None, ALU.add,
             incs=[("sDVE", 1)])
        ev_var = g.mark("sDVE")
        g.wait("scalar", ev_var)
        g.act(t["r_rstd"][:, :], t["r_var"][:, :], AF.Ln,
              incs=[("sACT", 1)])
        g.wait("scalar", g.mark("sACT"))
        g.act(t["r_rstd"][:, :], t["r_rstd"][:, :], AF.Exp, scale=-0.5,
              incs=[("sACT", 1)])
        g.wait("scalar", g.mark("sACT"))
        g.acopy(t["r_meanbf"][:, :], t["r_mean"][:, :])
        g.acopy(t["r_rstdbf"][:, :], t["r_rstd"][:, :], incs=[("sACT", 1)])
        ev_rbf = g.mark("sACT")
        g.wait("tensor", ev_rbf)
        for hh in range(2):
            sl = slice(hh * NH, (hh + 1) * NH)
            fst, lst = acc_flags(True, True, hh)
            g.mm(PS[:, 2048 + hh * NH:2048 + (hh + 1) * NH], ones_row,
                 t["r_meanbf"][0:1, sl], start=fst, stop=lst)
            g.mm(PS[:, 3072 + hh * NH:3072 + (hh + 1) * NH], ones_row,
                 t["r_rstdbf"][0:1, sl], start=fst, stop=lst,
                 incs=([("sPE", 1)] if hh == 1 else []))
        ev_bc = g.mark("sPE")
        g.wait("scalar", ev_bc)
        g.acopy(mbrb[:, 0:T], PS[:, 2048:2048 + T])
        g.acopy(mbrb[:, T:2 * T], PS[:, 3072:3072 + T], incs=[("sACT", 1)])
        ev_mb = g.mark("sACT")
        assert resident
        xn_evs = [None] * ND
        g.wait("vector", ev_mb)
        g.wait("gpsimd", ev_mb)
        for c in range(ND):
            eng = "vector" if (c % 2 == 0) else "gpsimd"
            sem = "sDVE" if (c % 2 == 0) else "sPOOL"
            op = g.tt if (c % 2 == 0) else g.ptt
            scratch = fa_tmp if (c % 2 == 0) else fa_xw
            g.wait(eng, pe_evs[c])
            if c >= 2:
                g.wait(eng, xn_evs[c - 2])
            op(scratch, xn_tile(c), mbrb[:, 0:T], ALU.subtract,
               incs=[(sem, 1)])
            g.wait(eng, g.mark(sem))
            op(xn_tile(c), scratch, mbrb[:, T:2 * T], ALU.mult,
               incs=[(sem, 1)])
            xn_evs[c] = g.mark(sem)
        return xn_evs

    def emit_ffn(x_src, x_dst, x_src_evs=None):
        WSL = ND * 128
        silu_evs = {}
        up_evs = {}
        inter_done = {}
        chunk_write_evs = [None] * ND
        last_stt = {0: None, 1: None}
        last_wr2 = {0: None, 1: None}
        xw_bufs = (fa_xw, fa_tmp)
        prev_evac = None
        wrr = [0]
        slot_pe = {}

        def wload(dram_ap, cols):
            s = wrr[0] % 3
            wrr[0] += 1
            ev_free = slot_pe.get(s)
            if ev_free is not None:
                g.wait("gpsimd", ev_free)
            ev = g.dma("gpsimd", WB[:, s * WSL:s * WSL + cols], dram_ap)
            return s, ev

        for gi, group in enumerate(cfg.fgroups):
            ng = len(group)
            for idx, i in enumerate(group):
                sg, evg = wload(t["wgu"][i * 128:(i + 1) * 128, 0:WSL], WSL)
                su, evu = wload(t["wgu"][i * 128:(i + 1) * 128, WSL:2 * WSL], WSL)
                p = idx % 2
                g.wait("tensor", evg)
                g.wait("tensor", evu)
                if idx >= 2:
                    g.wait("tensor", silu_evs[(gi, idx - 2)])
                    g.wait("tensor", inter_done[(gi, idx - 2)])
                elif prev_evac is not None:
                    g.wait("tensor", prev_evac)
                gcol, ucol = p * 1024, 2048 + p * 1024
                gate_ev = None
                for (col, s) in ((gcol, sg), (ucol, su)):
                    for c in range(ND):
                        for hh in range(2):
                            sl = slice(hh * NH, (hh + 1) * NH)
                            last = (c == ND - 1)
                            fst, lst = acc_flags(c == 0, last, hh)
                            g.mm(PS[:, col + hh * NH:col + (hh + 1) * NH],
                                 WB[:, s * WSL + c * 128:s * WSL + (c + 1) * 128],
                                 xn_tile(c)[:, sl], start=fst, stop=lst,
                                 incs=([("sPE", 1)] if (last and hh == 1) else []))
                    if col == gcol:
                        gate_ev = g.mark("sPE")
                        slot_pe[sg] = gate_ev
                    else:
                        up_evs[(gi, idx)] = g.mark("sPE")
                        slot_pe[su] = up_evs[(gi, idx)]
                g.wait("scalar", gate_ev)
                sbuf = t["silu"][:, p * T:(p + 1) * T]
                g.act(sbuf, PS[:, gcol:gcol + T], AF.Silu, incs=[("sACT", 1)])
                silu_evs[(gi, idx)] = g.mark("sACT")
                g.wait("vector", silu_evs[(gi, idx)])
                g.wait("vector", up_evs[(gi, idx)])
                g.tt(inter_tile(idx), sbuf, PS[:, ucol:ucol + T], ALU.mult,
                     incs=[("sDVE", 1)])
                inter_done[(gi, idx)] = g.mark("sDVE")
            ev_ginter = g.mark("sDVE")

            nt = min(3, ng)
            bounds = []
            b0 = 0
            for tpart in range(nt):
                sz = (ng - b0 + (nt - tpart - 1)) // (nt - tpart)
                bounds.append((b0, b0 + sz))
                b0 += sz
            evac_by_o = {}
            xread_evs = {}
            for og in range(cfg.n_og):
                for o in range(cfg.OG):
                    ch = og * cfg.OG + o
                    par = ch % 2
                    buf = fa_x0 if par == 0 else fa_x1
                    src = x_dst if gi > 0 else x_src
                    if gi == 0 and x_src_evs is not None:
                        g.wait("sync", x_src_evs[ch])
                    if chunk_write_evs[ch] is not None:
                        g.wait("sync", chunk_write_evs[ch])
                    if last_stt[par] is not None:
                        g.wait("sync", last_stt[par])
                    if o < 2:
                        xread_evs[(og, o)] = g.dma(
                            "sync", buf, src[ch * 128:(ch + 1) * 128, :])
                third_ev = None
                for ti, (i0, i1) in enumerate(bounds):
                    w = (i1 - i0) * cfg.OG * 128
                    src0 = (group[0] + i0) * cfg.OG * 128
                    sd, evd = wload(t["wd"][og * 128:(og + 1) * 128,
                                            src0:src0 + w], w)
                    g.wait("tensor", evd)
                    if og == 0 and ti == 0:
                        g.wait("tensor", ev_ginter)
                        if prev_evac is not None:
                            g.wait("tensor", prev_evac)
                    nmm, tot = 0, (i1 - i0) * cfg.OG * 2
                    for ii in range(i0, i1):
                        for o in range(cfg.OG):
                            if ti == 0 and ii == i0 and og >= 1:
                                g.wait("tensor", evac_by_o[(og - 1, o)])
                            ocol = o * 1024
                            wcol = sd * WSL + ((ii - i0) * cfg.OG + o) * 128
                            for hh in range(2):
                                sl = slice(hh * NH, (hh + 1) * NH)
                                nmm += 1
                                fst, lst = acc_flags(ii == 0 and ti == 0,
                                                     ii == ng - 1, hh)
                                g.mm(PS[:, ocol + hh * NH:ocol + (hh + 1) * NH],
                                     WB[:, wcol:wcol + 128], inter_tile(ii)[:, sl],
                                     start=fst, stop=lst,
                                     incs=([("sPE", 1)] if nmm == tot else []))
                    third_ev = g.mark("sPE")
                    slot_pe[sd] = third_ev
                for o in range(cfg.OG):
                    ch = og * cfg.OG + o
                    par = ch % 2
                    buf = fa_x0 if par == 0 else fa_x1
                    src = x_dst if gi > 0 else x_src
                    if o >= 2:
                        if chunk_write_evs[ch] is not None:
                            g.wait("sync", chunk_write_evs[ch])
                        if last_stt[par] is not None:
                            g.wait("sync", last_stt[par])
                        xread_evs[(og, o)] = g.dma(
                            "sync", buf, src[ch * 128:(ch + 1) * 128, :])
                    wpar = o % 2
                    xw = xw_bufs[wpar]
                    g.wait("vector", xread_evs[(og, o)])
                    g.wait("vector", third_ev)
                    if last_wr2[wpar] is not None:
                        g.wait("vector", last_wr2[wpar])
                    g.stt(xw, PS[:, o * 1024:o * 1024 + T], 0.5, buf,
                          ALU.mult, ALU.add, incs=[("sDVE", 1)])
                    ev_e = g.mark("sDVE")
                    evac_by_o[(og, o)] = ev_e
                    last_stt[par] = ev_e
                    g.wait("sync", ev_e)
                    chunk_write_evs[ch] = g.dma(
                        "sync", x_dst[ch * 128:(ch + 1) * 128, :], xw)
                    last_wr2[wpar] = chunk_write_evs[ch]
            prev_evac = evac_by_o[(cfg.n_og - 1, cfg.OG - 1)]
        return chunk_write_evs

    def emit_proj(wdram, oc_n, ct_n, rhs_tile, evac, slots=4, wcols=None):
        wcols = wcols if wcols is not None else ct_n * 128
        pe_done = {}
        war = {}
        for oc in range(oc_n):
            p = oc % 2
            ev_free = pe_done.get(oc - 2)
            if ev_free is not None:
                g.wait("gpsimd", ev_free)
            ev_w = g.dma("gpsimd", WB[:, p * wcols:(p + 1) * wcols],
                         wdram[oc * 128:(oc + 1) * 128, 0:wcols])
            g.wait("tensor", ev_w)
            slot = oc % slots
            if war.get(oc - slots) is not None:
                g.wait("tensor", war[oc - slots])
            for c in range(ct_n):
                for hh in range(2):
                    sl = slice(hh * NH, (hh + 1) * NH)
                    last = (c == ct_n - 1)
                    fst, lst = acc_flags(c == 0, last, hh)
                    g.mm(PS[:, slot * 1024 + hh * NH:slot * 1024 + (hh + 1) * NH],
                         WB[:, p * wcols + c * 128:p * wcols + (c + 1) * 128],
                         rhs_tile(c)[:, sl], start=fst, stop=lst,
                         incs=([("sPE", 1)] if (last and hh == 1) else []))
            pe_done[oc] = g.mark("sPE")
            war[oc] = evac(oc, slot, pe_done[oc])
        return g.mark("sDVE"), g.mark("sACT"), g.mark("sPE")

    xn_evs = emit_ln(t["x_in"], resident=True)
    g.wait("tensor", xn_evs[-1])
    g.wait("tensor", xn_evs[-2])
    if cfg.debug:
        for c in range(ND):
            g.wait("sync", xn_evs[c])
            g.dma("sync", t["dbg_xn1"][c * 128:(c + 1) * 128, :], xn_tile(c))
    g.barrier()
    ffn1_evs = emit_ffn(t["x_in"], t["x_out"], x_src_evs=None)
    if cfg.debug:
        for c in range(ND):
            g.wait("sync", ffn1_evs[c])
            g.dma("sync", t["dbg_x1"][c * 128:(c + 1) * 128, :],
                  t["x_out"][c * 128:(c + 1) * 128, :])

    g.barrier()
    xn2_evs = emit_ln(t["x_out"], resident=True, chunk_evs=ffn1_evs)
    g.barrier()
    for nm, dst in (("cosq", cos_q), ("sinq", sin_q), ("cosk", cos_k),
                    ("sink", sin_k)):
        g.dma("sync", dst, t[nm][:, :])
    for ev in g.qmarks("sync"):
        g.wait("vector", ev)
        g.wait("tensor", ev)

    def ckv_tile(cc):
        return CK[:, cc * T:(cc + 1) * T]

    stage1 = CK[:, NC * T:NC * T + T]
    stage2 = CK[:, NC * T + T:NC * T + 2 * T]

    def evac_copy_to(dst_fn):
        def f(oc, slot, ev_pe):
            g.wait("scalar", ev_pe)
            g.acopy(dst_fn(oc), PS[:, slot * 1024:slot * 1024 + T],
                    incs=[("sACT", 1)])
            return g.mark("sACT")
        return f

    emit_proj(t["wdk"], NC, ND, xn_tile, evac_copy_to(ckv_tile))
    g.barrier()

    def make_rope_evac(cos_t, sin_t, post):
        st = {"rot_war": {}, "stage_pe": {}, "stage_dve": {}, "ro": {}}

        def evac(oc, slot, ev_pe):
            g.wait("scalar", ev_pe)
            if oc >= 1:
                g.wait("scalar", st["stage_pe"][oc - 1])
                g.wait("scalar", st["stage_dve"][oc - 1])
            g.acopy(stage1, PS[:, slot * 1024:slot * 1024 + T], incs=[("sACT", 1)])
            ev_k = g.mark("sACT")
            srot = 2 + (oc % 2)
            g.wait("tensor", ev_k)
            if oc >= 2:
                g.wait("tensor", st["rot_war"][oc - 2])
            for hh in range(2):
                sl = slice(hh * NH, (hh + 1) * NH)
                fst, lst = acc_flags(True, True, hh)
                g.mm(PS[:, srot * 1024 + hh * NH:srot * 1024 + (hh + 1) * NH],
                     t["rt"][:, :], stage1[:, sl], start=fst, stop=lst,
                     incs=([("sPE", 1)] if hh == 1 else []))
            ev_rot = g.mark("sPE")
            st["stage_pe"][oc] = ev_rot
            g.wait("vector", ev_k)
            if oc >= 1:
                for e in st["ro"][oc - 1]:
                    g.wait("vector", e)
            g.tt(fa_rope, stage1, cos_t, ALU.mult)
            g.wait("vector", ev_rot)
            g.tt(stage2, PS[:, srot * 1024:srot * 1024 + T], sin_t, ALU.mult,
                 incs=[("sDVE", 1)])
            g.wait("vector", g.mark("sDVE"))
            g.tt(stage2, fa_rope, stage2, ALU.add, incs=[("sDVE", 1)])
            ev_ro = g.mark("sDVE")
            st["stage_dve"][oc] = ev_ro
            ev_fins = post(oc, ev_ro)
            st["rot_war"][oc] = ev_fins[0]
            st["ro"][oc] = ev_fins
            return ev_k
        return evac

    def post_kr(oc, ev_ro):
        g.wait("vector", ev_ro)
        for hh in range(2):
            h = oc * 2 + hh
            g.vcopy(kt_tile(h)[0:64, :], stage2[hh * 64:(hh + 1) * 64, :],
                    incs=([("sDVE", 1)] if hh == 1 else []))
        return [g.mark("sDVE")]

    emit_proj(t["wkr"], NC, ND, xn_tile, make_rope_evac(cos_k, sin_k, post_kr),
              slots=2)
    g.barrier()

    kc_state = {}

    def evac_kc(oc, slot, ev_pe):
        g.wait("scalar", ev_pe)
        if oc >= 1:
            g.wait("scalar", kc_state[oc - 1])
        g.acopy(stage1, PS[:, slot * 1024:slot * 1024 + T], incs=[("sACT", 1)])
        ev_k = g.mark("sACT")
        g.wait("vector", ev_k)
        for hh in range(2):
            h = oc * 2 + hh
            g.vcopy(kt_tile(h)[64:128, :], stage1[hh * 64:(hh + 1) * 64, :],
                    incs=([("sDVE", 1)] if hh == 1 else []))
        ev = g.mark("sDVE")
        kc_state[oc] = ev
        return ev

    emit_proj(t["wupk"], NC, NC, ckv_tile, evac_kc, wcols=NC * 128)
    ev_kt_done = g.mark("sDVE")
    g.barrier()

    KVHD = KV * 128
    wupv_sb = WB[:, 0:NC * KVHD]
    ev_wupv = g.dma(
        "gpsimd", wupv_sb.rearrange("p (c d) -> p c d", c=NC),
        t["wupv"].ap().rearrange("(c p) d -> p c d", p=128))
    g.wait("tensor", ev_wupv)
    step = min(NH, KVHD)
    vt_evs = {}
    for tt in range(cfg.TT):
        slot = tt % 4
        if tt >= 4:
            g.wait("tensor", vt_evs[tt - 4])
        nmm, tot = 0, NC * (KVHD // step)
        for cc in range(NC):
            for hh in range(KVHD // step):
                sl = slice(hh * step, (hh + 1) * step)
                nmm += 1
                fst, lst = acc_flags(cc == 0, cc == NC - 1, hh,
                                     nhh=KVHD // step, step=step)
                g.mm(PS[:, slot * 1024 + hh * step:slot * 1024 + (hh + 1) * step],
                     ckv_tile(cc)[:, tt * 128:(tt + 1) * 128],
                     wupv_sb[:, cc * KVHD + hh * step:cc * KVHD + (hh + 1) * step],
                     start=fst, stop=lst,
                     incs=([("sPE", 1)] if nmm == tot else []))
        ev_pe = g.mark("sPE")
        g.wait("scalar", ev_pe)
        for h in range(KV):
            g.acopy(vt_tile(h)[:, tt * 128:(tt + 1) * 128],
                    PS[:, slot * 1024 + h * 128:slot * 1024 + (h + 1) * 128],
                    incs=([("sACT", 1)] if h == KV - 1 else []))
        vt_evs[tt] = g.mark("sACT")
    ev_vt_done = g.mark("sACT")

    ev_kn = {}
    knorm_ship = {}
    for h in range(KV):
        sqv = t["sq"][:, (h % 2) * T:(h % 2) * T + T]
        g.wait("scalar", ev_kt_done)
        if h >= 2:
            g.wait("scalar", ev_kn[h - 2])
        g.act(sqv, kt_tile(h), AF.Square, incs=[("sACT", 1)])
        ev_sq = g.mark("sACT")
        g.wait("tensor", ev_sq)
        if h == 0:
            g.wait("tensor", ev_vt_done)
        for hh in range(2):
            sl = slice(hh * NH, (hh + 1) * NH)
            fst, lst = acc_flags(True, True, hh)
            g.mm(PS[0:1, (h % 4) * 1024 + hh * NH:(h % 4) * 1024 + (hh + 1) * NH],
                 ones_col, sqv[:, sl], start=fst, stop=lst,
                 incs=([("sPE", 1)] if hh == 1 else []))
        ev_mm = g.mark("sPE")
        ev_kn[h] = ev_mm
        g.wait("scalar", ev_mm)
        if h >= 2:
            g.wait("scalar", knorm_ship[h - 2])
        g.acopy(knorm[0:1, (h % 2) * T:(h % 2) * T + T],
                PS[0:1, (h % 4) * 1024:(h % 4) * 1024 + T], incs=[("sACT", 1)])
        ev_k = g.mark("sACT")
        g.wait("sync", ev_k)
        knorm_ship[h] = g.dma(
            "sync", t["contrib"][2 * KV * 128 + h:2 * KV * 128 + h + 1, :],
            knorm[0:1, (h % 2) * T:(h % 2) * T + T])

    g.wait("sync", ev_kt_done)
    for h in range(KV):
        g.dma("sync", t["contrib"][h * 128:(h + 1) * 128, :], kt_tile(h))
    g.wait("sync", ev_vt_done)
    for h in range(KV):
        g.dma("sync",
              t["contrib"][KV * 128 + h * 128:KV * 128 + (h + 1) * 128, :],
              vt_tile(h))
    for ev in g.qmarks("sync"):
        g.wait("gpsimd", ev)
    g.collective("AllGather", ALU.bypass, [[0, 1, 2, 3], [4, 5, 6, 7]],
                 [t["contrib"].ap().opt()], [t["gout"].ap().opt()],
                 incs=[("sCC", 1)])
    ev_cc = g.mark("sCC")
    for ev in g.qmarks("sync"):
        g.wait("sync", ev)
    ev_masks = g.dma("sync", t["masks"][:, :], t["masks_in"][:, :])
    g.wait("tensor", ev_masks)
    g.wait("vector", ev_masks)

    if cfg.debug:
        g.wait("sync", ev_kt_done)
        g.wait("sync", ev_vt_done)
        for h in range(KV):
            g.dma("sync", t["dbg_kt"][h * 128:(h + 1) * 128, :], kt_tile(h))
            g.dma("sync", t["dbg_vt"][h * 128:(h + 1) * 128, :], vt_tile(h))

    q_state = {}

    def post_q(oc, ev_ro):
        g.wait("sync", ev_ro)
        ev_store = g.dma("sync", t["qtild"][oc * 128:(oc + 1) * 128, :], stage2)
        g.wait("scalar", ev_ro)
        if oc >= 2:
            g.wait("scalar", q_state[oc - 2])
        sqv = t["sq"][:, (oc % 2) * T:(oc % 2) * T + T]
        g.act(sqv, stage2, AF.Square, incs=[("sACT", 1)])
        ev_sq = g.mark("sACT")
        srot = 2 + (oc % 2)
        g.wait("tensor", ev_sq)
        for hh in range(2):
            sl = slice(hh * NH, (hh + 1) * NH)
            fst, lst = acc_flags(True, True, hh)
            g.mm(PS[0:1, srot * 1024 + hh * NH:srot * 1024 + (hh + 1) * NH],
                 ones_col, sqv[:, sl], start=fst, stop=lst,
                 incs=([("sPE", 1)] if hh == 1 else []))
        ev_qn = g.mark("sPE")
        q_state[oc] = ev_qn
        g.wait("scalar", ev_qn)
        g.acopy(t["r_qn2"][:, :], PS[0:1, srot * 1024:srot * 1024 + T],
                incs=[("sACT", 1)])
        ev_r = g.mark("sACT")
        g.wait("vector", ev_r)
        for qb in range(2):
            g.vreduce(t["r_qmax2"][0:1, oc * 2 + qb:oc * 2 + qb + 1],
                      t["r_qn2"][0:1, qb * QB:(qb + 1) * QB], ALU.max,
                      incs=([("sDVE", 1)] if qb == 1 else []))
        return [g.mark("sDVE"), ev_store]

    emit_proj(t["wq"], QH, ND, xn_tile,
              make_rope_evac(cos_q, sin_q, post_q), slots=2)
    ev_qproj = g.mark("sDVE")
    g.barrier()

    g.wait("sync", ev_cc)
    ev_ks = g.dma("sync", t["ksg2"][:, :],
                  bass.AP(t["gout"], 2 * KV * 128 * T,
                          [[cfg.CROWS * T, 4], [T, KV], [1, T]]))
    g.wait("vector", ev_ks)
    g.vreduce(t["r_kpm"][:, :], t["ksg2"][:, :], ALU.max, incs=[("sDVE", 1)])
    ev_kpm = g.mark("sDVE")
    g.wait("sync", ev_kpm)
    ev_kb = g.dma("sync", t["r_t1"][0:1, 24:24 + 4 * KV], t["r_kpm"][:, 0:1])
    g.wait("vector", ev_kb)
    g.tt(t["r_t1"][0:1, 56:56 + KV], t["r_t1"][0:1, 24:24 + KV],
         t["r_t1"][0:1, 24 + KV:24 + 2 * KV], ALU.max, incs=[("sDVE", 1)])
    g.wait("vector", g.mark("sDVE"))
    g.tt(t["r_t1"][0:1, 64:64 + KV], t["r_t1"][0:1, 24 + 2 * KV:24 + 3 * KV],
         t["r_t1"][0:1, 24 + 3 * KV:24 + 4 * KV], ALU.max, incs=[("sDVE", 1)])
    g.wait("vector", g.mark("sDVE"))
    g.tt(t["r_t1"][0:1, 8:8 + KV], t["r_t1"][0:1, 56:56 + KV],
         t["r_t1"][0:1, 64:64 + KV], ALU.max, incs=[("sDVE", 1)])
    ev_km = g.mark("sDVE")
    g.wait("scalar", ev_km)
    g.act(t["r_t1"][0:1, 0:KV], t["r_t1"][0:1, 8:8 + KV], AF.Ln,
          incs=[("sACT", 1)])
    ev_lnk0 = g.mark("sACT")
    g.wait("vector", ev_lnk0)
    g.ts(t["r_t1"][0:1, 16:16 + KV], t["r_t1"][0:1, 0:KV], 0.5, None, ALU.mult,
         incs=[("sDVE", 1)])
    ev_lnk = g.mark("sDVE")

    g.wait("scalar", ev_qproj)
    g.act(t["r_lnq"][:, :], t["r_qmax2"][:, :], AF.Ln, incs=[("sACT", 1)])
    ev_lnq = g.mark("sACT")
    g.wait("scalar", ev_lnq)
    g.wait("scalar", ev_lnk)
    for h in range(KV):
        g.act(t["r_Bv"][0:1, h * 8:(h + 1) * 8],
              t["r_lnq"][0:1, h * 8:(h + 1) * 8], AF.Exp, scale=0.5,
              bias=t["r_t1"][0:1, 16 + h:17 + h],
              incs=([("sACT", 1)] if h == KV - 1 else []))
    ev_b = g.mark("sACT")
    g.wait("vector", ev_b)
    g.ts(t["r_lnq"][:, :], t["r_Bv"][:, :], -1.0, -B_MARGIN, ALU.mult, ALU.add,
         incs=[("sDVE", 1)])
    ev_nb = g.mark("sDVE")
    g.wait("scalar", ev_nb)
    g.acopy(t["r_nbbf"][:, :], t["r_lnq"][:, :], incs=[("sACT", 1)])
    ev_nbf = g.mark("sACT")
    g.wait("tensor", ev_nbf)
    g.mm(PS[:, 2048:2048 + 2 * QH], ones_row, t["r_nbbf"][0:1, :],
         start=True, stop=True, incs=[("sPE", 1)])
    ev_nbc = g.mark("sPE")
    g.wait("scalar", ev_nbc)
    g.acopy(t["negB"][:, :], PS[:, 2048:2048 + 2 * QH], incs=[("sACT", 1)])
    ev_negB = g.mark("sACT")

    NSLOT = 3 + 7
    g.wait("vector", ev_nb)
    for s, (qb, jj) in enumerate(MAIN_SLOTS):
        g.vcopy(t["r_bx"][0:1, s * QH:(s + 1) * QH],
                bass.AP(t["r_lnq"], qb, [[2 * QH, 1], [2, QH]]),
                incs=([("sDVE", 1)] if s == NSLOT - 1 else []))
    g.wait("vector", g.mark("sDVE"))
    g.tt(t["r_bx"][:, :], t["r_bx"][:, :], t["valid"][:, :], ALU.mult,
         incs=[("sDVE", 1)])
    g.ts(t["r_bx2"][:, :], t["valid"][:, :], 1e9, -1e9, ALU.mult, ALU.add,
         incs=[("sDVE", 1)])
    g.wait("vector", g.mark("sDVE"))
    g.tt(t["r_bx"][:, :], t["r_bx"][:, :], t["r_bx2"][:, :], ALU.add,
         incs=[("sDVE", 1)])
    ev_bx = g.mark("sDVE")
    g.wait("scalar", ev_bx)
    g.acopy(t["r_bxbf"][:, :], t["r_bx"][:, :], incs=[("sACT", 1)])
    ev_bxbf = g.mark("sACT")
    g.wait("tensor", ev_bxbf)
    g.mm(PS[:, 2048:2048 + NSLOT * QH], ones_row, t["r_bxbf"][0:1, :],
         start=True, stop=True, incs=[("sPE", 1)])
    ev_bxm = g.mark("sPE")
    g.wait("scalar", ev_bxm)
    g.acopy(t["bias_x"][:, :], PS[:, 2048:2048 + NSLOT * QH], incs=[("sACT", 1)])
    ev_biasx = g.mark("sACT")

    g.barrier()
    for qh in range(QH):
        g.dma("sync", qav_tile(qh), t["qtild"][qh * 128:(qh + 1) * 128, :])
    for ev in g.qmarks("sync"):
        g.wait("tensor", ev)
    if cfg.debug:
        for qh in range(QH):
            g.dma("sync", t["dbg_q"][qh * 128:(qh + 1) * 128, :], qav_tile(qh))

    KT_C = cfg.KT_C
    NU = (3 + 7 + 2) * KT_C
    ev_unit_pe = {}
    ev_exp = {}
    ev_epi_dve = None
    ev_epi_act = None

    HSEG = 10 * T
    KTV = AB[:, TAIL0:TAIL0 + 2 * HSEG]

    def ktall(hp):
        return KTV[:, hp * HSEG: hp * HSEG + 4 * T]

    def vall(hp):
        return KTV[:, hp * HSEG + 4 * T: hp * HSEG + 8 * T]

    def ownkt(hp):
        return KTV[:, hp * HSEG + 8 * T: hp * HSEG + 9 * T]

    def ownvt(hp):
        return KTV[:, hp * HSEG + 9 * T: hp * HSEG + 10 * T]

    def chunk_col(jj, kt):
        rank = jj if jj < 4 else 7 - jj
        half = 0 if jj < 4 else 1
        return rank * T + half * QB + kt * 128

    ev_hload = {}
    ev_h_pe_done = {}

    def load_h(h):
        hp = h % 2
        if h >= 2:
            g.wait("sync", ev_h_pe_done[h - 2])
        else:
            g.wait("sync", ev_cc)
        evs = []
        for r in range(4):
            evs.append(g.dma("sync", ktall(hp)[:, r * T:(r + 1) * T],
                             t["gout"][r * cfg.CROWS + h * 128:
                                       r * cfg.CROWS + (h + 1) * 128, :]))
            evs.append(g.dma("sync", vall(hp)[:, r * T:(r + 1) * T],
                             t["gout"][r * cfg.CROWS + KV * 128 + h * 128:
                                       r * cfg.CROWS + KV * 128 + (h + 1) * 128, :]))
        evs.append(g.dma("sync", ownkt(hp),
                         t["contrib"][h * 128:(h + 1) * 128, :]))
        evs.append(g.dma("sync", ownvt(hp),
                         t["contrib"][KV * 128 + h * 128:
                                      KV * 128 + (h + 1) * 128, :]))
        ev_hload[h] = evs

    g.wait("scalar", ev_negB)
    g.wait("scalar", ev_biasx)
    load_h(0)
    load_h(1)

    E = 2 if KT_C % 2 == 0 else 1
    NP = NU // E

    ev_exppair = {}
    ev_av = {}
    ev_epi_dve = None
    ev_epi_act = None
    ev_ese_prev = None

    for qh in range(QH):
        h = qh // 4
        hp = h % 2
        ev_ktls = ev_hload[h]
        if ev_epi_dve is not None:
            g.wait("tensor", ev_epi_dve)
        for col in (2048, 2560):
            g.mm(PS[:, col:col + QB], zrow, ones_qb_row, start=True, stop=False,
                 skip=True, incs=([("sPE", 1)] if col == 2560 else []))
        ev_clear = g.mark("sPE")

        units = []
        for s, (qb, jj) in enumerate(MAIN_SLOTS):
            for kt in range(KT_C):
                units.append(("main", qb, s, chunk_col(jj, kt), kt))
        for qb in range(2):
            for kt in range(KT_C):
                units.append(("diag", qb, None, qb * QB + kt * 128, kt))

        def u_info(u):
            kind_u, qb, s, col, kt = units[u]
            lhs = (ktall(hp)[:, col:col + 128] if kind_u == "main"
                   else ownkt(hp)[:, col:col + 128])
            vlhs = (vall(hp)[:, col:col + 128] if kind_u == "main"
                    else ownvt(hp)[:, col:col + 128])
            if kind_u == "main":
                bias = t["bias_x"][:, s * QH + qh:s * QH + qh + 1]
            else:
                bias = t["negB"][:, qh * 2 + qb:qh * 2 + qb + 1]
            return kind_u, qb, col, kt, lhs, vlhs, bias

        def ebuf_of(u):
            p = u // E
            if E == 2:
                return t["eb"][:, (p % 2) * 2 * QB + (u % 2) * QB:
                               (p % 2) * 2 * QB + (u % 2) * QB + QB]
            return t["eb"][:, (u % 2) * QB:(u % 2) * QB + QB]

        ev_mask_done = {}

        def prev_pair(p, d):
            pp = p - d
            if pp >= 0:
                return (qh, pp)
            if qh > 0:
                return (qh - 1, NP + pp)
            return None

        es_count = {0: 0, 1: 0}
        es_total = {0: 4 * KT_C, 1: (len(MAIN_SLOTS) - 3 + 1) * KT_C}

        def emit_av(u):
            kind_u, qb, col, kt, lhs, vlhs, bias = u_info(u)
            eb_u = ebuf_of(u)
            g.wait("tensor", ev_mask_done[(qh, u)])
            avcol = 2048 + qb * 512
            stop_acc = (u == 11 * KT_C - 1) or (u == NU - 1)
            g.mm(PS[:, avcol:avcol + QB], vlhs, eb_u,
                 start=False, stop=stop_acc, skip=True)
            es_fst = (es_count[qb] == 0)
            es_count[qb] += 1
            es_lst = (es_count[qb] == es_total[qb])
            g.mm(PS[0:1, 3072 + qb * 512:3072 + qb * 512 + QB], ones_col, eb_u,
                 start=es_fst, stop=es_lst, incs=[("sPE", 1)])
            ev_av[(qh, u)] = g.mark("sPE")

        for p in range(NP):
            us = list(range(p * E, (p + 1) * E))
            for u in us:
                kind_u, qb, col, kt, lhs, vlhs, bias = u_info(u)
                if u == 0 and qh % 4 == 0:
                    for e_ in ev_ktls:
                        g.wait("tensor", e_)
                if u == us[0]:
                    pq = prev_pair(p, 2)
                    if pq is not None:
                        g.wait("tensor", ev_exppair[pq])
                    if p == 0:
                        g.wait("tensor", ev_clear)
                scol = (((p % 2) * 2 + (u % 2)) * 512 if E == 2
                        else (u % 2) * 512)
                g.mm(PS[:, scol:scol + QB], lhs,
                     qav_tile(qh)[:, qb * QB:(qb + 1) * QB],
                     start=True, stop=True,
                     incs=([("sPE", 1)] if u == us[-1] else []))
            ev_s = g.mark("sPE")
            if p > 0:
                for u in range((p - 1) * E, p * E):
                    emit_av(u)
            g.wait("scalar", ev_s)
            pq2 = prev_pair(p, 2)
            if pq2 is not None:
                qh2, pp2 = pq2
                for u2 in range(pp2 * E, (pp2 + 1) * E):
                    g.wait("scalar", ev_av[(qh2, u2)])
            kind0, qb0_, col0, kt0, lhs0, vlhs0, bias0 = u_info(us[0])
            if E == 2:
                eb_grp = t["eb"][:, (p % 2) * 2 * QB:(p % 2) * 2 * QB + 2 * QB]
                base = (p % 2) * 1024
                g.act(eb_grp, PS[:, base:base + 512 + QB], AF.Exp, bias=bias0,
                      incs=[("sACT", 1)])
            else:
                scol = (us[0] % 2) * 512
                g.act(ebuf_of(us[0]), PS[:, scol:scol + QB], AF.Exp, bias=bias0,
                      incs=[("sACT", 1)])
            ev_e = g.mark("sACT")
            ev_exppair[(qh, p)] = ev_e
            for u in us:
                kind_u, qb, col, kt, lhs, vlhs, bias = u_info(u)
                eb_u = ebuf_of(u)
                if kind_u == "diag":
                    g.wait("vector", ev_e)
                    g.tt(eb_u, eb_u, t["masks"][:, kt * QB:(kt + 1) * QB],
                         ALU.mult, incs=[("sDVE", 1)])
                    ev_mask_done[(qh, u)] = g.mark("sDVE")
                else:
                    ev_mask_done[(qh, u)] = ev_e
        for u in range(NU - E, NU):
            emit_av(u)
        if qh % 4 == 3:
            ev_h_pe_done[h] = g.mark("sPE")
            if h + 2 < KV:
                load_h(h + 2)

        ev_last = ev_av[(qh, NU - 1)]
        assert es_count[0] == es_total[0] and es_count[1] == es_total[1], \
            (es_count, es_total)
        g.wait("scalar", ev_last)
        g.acopy(t["r_esum"][0:1, 0:QB], PS[0:1, 3072:3072 + QB])
        g.acopy(t["r_esum"][0:1, QB:2 * QB], PS[0:1, 3584:3584 + QB],
                incs=[("sACT", 1)])
        ev_ese = g.mark("sACT")
        ev_ese_prev = ev_ese
        g.wait("vector", ev_ese)
        g.ts(t["r_esum"][:, :], t["r_esum"][:, :], 1e-30, None, ALU.add,
             incs=[("sDVE", 1)])
        g.wait("vector", g.mark("sDVE"))
        g.vrecip(t["r_recip"][:, :], t["r_esum"][:, :], incs=[("sDVE", 1)])
        ev_rc = g.mark("sDVE")
        g.wait("scalar", ev_rc)
        g.acopy(t["r_recipbf"][:, :], t["r_recip"][:, :], incs=[("sACT", 1)])
        ev_rcb = g.mark("sACT")
        g.wait("tensor", ev_rcb)
        if ev_epi_act is not None:
            g.wait("tensor", ev_epi_act)
        for qb in range(2):
            g.mm(PS[:, 3072 + qb * 512:3072 + qb * 512 + QB], ones_row,
                 t["r_recipbf"][0:1, qb * QB:(qb + 1) * QB], start=True, stop=True,
                 incs=([("sPE", 1)] if qb == 1 else []))
        ev_bcm = g.mark("sPE")
        g.wait("scalar", ev_bcm)
        for qb in range(2):
            g.acopy(t["recb"][:, qb * QB:(qb + 1) * QB],
                    PS[:, 3072 + qb * 512:3072 + qb * 512 + QB],
                    incs=([("sACT", 1)] if qb == 1 else []))
        ev_epi_act = g.mark("sACT")
        g.wait("vector", ev_epi_act)
        g.wait("vector", ev_last)
        for qb in range(2):
            g.tt(qav_tile(qh)[:, qb * QB:(qb + 1) * QB],
                 PS[:, 2048 + qb * 512:2048 + qb * 512 + QB],
                 t["recb"][:, qb * QB:(qb + 1) * QB], ALU.mult,
                 incs=([("sDVE", 1)] if qb == 1 else []))
        ev_epi_dve = g.mark("sDVE")

    if cfg.debug:
        g.wait("sync", ev_epi_dve)
        for qh in range(QH):
            g.dma("sync", t["dbg_av"][qh * 128:(qh + 1) * 128, :], qav_tile(qh))

    g.barrier()
    attn_out_evs = [None] * ND
    wo_last = {0: None, 1: None}
    wo_wr = None

    def evac_wo(oc, slot, ev_pe):
        nonlocal wo_wr
        par = oc % 2
        buf = fa_x0 if par == 0 else fa_x1
        if wo_last[par] is not None:
            g.wait("sync", wo_last[par])
        ev_x = g.dma("sync", buf, t["x_out"][oc * 128:(oc + 1) * 128, :])
        g.wait("vector", ev_x)
        g.wait("vector", ev_pe)
        if wo_wr is not None:
            g.wait("vector", wo_wr)
        g.tt(fa_xw, PS[:, slot * 1024:slot * 1024 + T], buf, ALU.add,
             incs=[("sDVE", 1)])
        ev_e = g.mark("sDVE")
        wo_last[par] = ev_e
        g.wait("sync", ev_e)
        attn_out_evs[oc] = g.dma(
            "sync", t["x_out"][oc * 128:(oc + 1) * 128, :], fa_xw)
        wo_wr = attn_out_evs[oc]
        return ev_e

    emit_proj(t["wo"], ND, QH, qav_tile, evac_wo)
    if cfg.debug:
        for c in range(ND):
            g.wait("sync", attn_out_evs[c])
            g.dma("sync", t["dbg_x2"][c * 128:(c + 1) * 128, :],
                  t["x_out"][c * 128:(c + 1) * 128, :])

    g.barrier()
    ln3_guard = g.mark("sPE")
    xn3_evs = emit_ln(t["x_out"], resident=True, chunk_evs=attn_out_evs,
                      first_write_guard=ln3_guard)
    g.wait("tensor", xn3_evs[-1])
    g.wait("tensor", xn3_evs[-2])
    g.barrier(kinds=("sync", "gpsimd"))
    ffn2_evs = emit_ffn(t["x_out"], t["x_out"], x_src_evs=attn_out_evs)

    for ev in ffn2_evs:
        g.wait("sync", ev)
    g.barrier(kinds=("sync",))



def declare(nc, cfg):
    from contextlib import ExitStack
    ctx = ExitStack()
    t = {}
    D, T, ND, NI, KV, QH, QB, NC = (cfg.D, cfg.T, cfg.ND, cfg.NI, cfg.KV,
                                    cfg.QH, cfg.QB, cfg.NC)

    t["x_in"] = nc.dram_tensor("x_in", [D, T], F32, kind="ExternalInput")
    t["wq"] = nc.dram_tensor("wq", [QH * 128, D], F32, kind="ExternalInput")
    t["wo"] = nc.dram_tensor("wo", [D, QH * 128], F32, kind="ExternalInput")
    t["wkr"] = nc.dram_tensor("wkr", [cfg.CD, D], F32, kind="ExternalInput")
    t["wdk"] = nc.dram_tensor("wdk", [cfg.CD, D], F32, kind="ExternalInput")
    t["wupk"] = nc.dram_tensor("wupk", [cfg.CD, cfg.CD], F32, kind="ExternalInput")
    t["wupv"] = nc.dram_tensor("wupv", [cfg.CD, KV * 128], F32, kind="ExternalInput")
    t["wgu"] = nc.dram_tensor("wgu", [NI * 128, 2 * ND * 128], F32,
                              kind="ExternalInput")
    t["wd"] = nc.dram_tensor("wd", [cfg.n_og * 128, NI * cfg.OG * 128], F32,
                             kind="ExternalInput")
    for nm in ("cosq", "sinq", "cosk", "sink"):
        t[nm] = nc.dram_tensor(nm, [128, T], F32, kind="ExternalInput")
    t["masks_in"] = nc.dram_tensor("masks_in", [128, 4 * QB], BF16,
                                   kind="ExternalInput")
    t["rt_in"] = nc.dram_tensor("rt_in", [128, 128], BF16, kind="ExternalInput")
    t["ones_in"] = nc.dram_tensor("ones_in", [128, 130], BF16, kind="ExternalInput")
    t["zrow_in"] = nc.dram_tensor("zrow_in", [1, 128], BF16, kind="ExternalInput")
    t["valid_in"] = nc.dram_tensor("valid_in", [1, 10 * QH], F32,
                                   kind="ExternalInput")
    t["x_out"] = nc.dram_tensor("x_out", [D, T], F32, kind="ExternalOutput")

    t["xn2d"] = nc.dram_tensor("xn2d", [D, T], BF16)
    t["qtild"] = nc.dram_tensor("qtild", [QH * 128, T], BF16)
    t["contrib"] = nc.dram_tensor("contrib", [cfg.CROWS, T], BF16)
    t["gout"] = nc.dram_tensor("gout", [cfg.GR, T], BF16)

    if cfg.debug:
        t["dbg_xn1"] = nc.dram_tensor("dbg_xn1", [D, T], BF16, kind="ExternalOutput")
        t["dbg_x1"] = nc.dram_tensor("dbg_x1", [D, T], F32, kind="ExternalOutput")
        t["dbg_kt"] = nc.dram_tensor("dbg_kt", [KV * 128, T], BF16,
                                     kind="ExternalOutput")
        t["dbg_vt"] = nc.dram_tensor("dbg_vt", [KV * 128, T], BF16,
                                     kind="ExternalOutput")
        t["dbg_q"] = nc.dram_tensor("dbg_q", [QH * 128, T], BF16,
                                    kind="ExternalOutput")
        t["dbg_av"] = nc.dram_tensor("dbg_av", [QH * 128, T], BF16,
                                     kind="ExternalOutput")
        t["dbg_x2"] = nc.dram_tensor("dbg_x2", [D, T], F32, kind="ExternalOutput")

    def sb(name, shape, dtype):
        t[name] = ctx.enter_context(nc.sbuf_tensor(name, shape, dtype))

    max_g = max(len(gg) for gg in cfg.fgroups)
    sb("AB", [128, max(ND, QH) * T + max(max_g, 20) * T], BF16)
    wb_cols = max(3 * ND * 128,
                  2 * ((max_g + 1) // 2) * cfg.OG * 128,
                  2 * ND * 128, 2 * QH * 128, 2 * NC * 128)
    sb("WB", [128, wb_cols], BF16)
    sb("CK", [128, NC * T + 2 * T], BF16)
    sb("F32A", [128, 5 * T], F32)
    sb("masks", [128, 4 * QB], BF16)
    sb("rt", [128, 128], BF16)
    sb("ones", [128, 130], BF16)
    sb("zrow", [1, 128], BF16)
    sb("valid", [1, 10 * QH], F32)
    sb("eb", [128, 4 * QB], BF16)
    sb("recb", [128, 2 * QB], BF16)
    sb("negB", [128, 2 * QH], F32)
    sb("bias_x", [128, 10 * QH], F32)
    sb("silu", [128, 2 * T], BF16)
    sb("sq", [128, 2 * T], BF16)
    sb("ksg2", [4 * KV, T], BF16)
    sb("r_kpm", [4 * KV, 1], F32)
    for nm in ("r_sum", "r_sumsq", "r_rstd"):
        sb(nm, [1, T], F32)
    t["r_mean"] = t["r_sum"]
    t["r_var"] = t["r_sumsq"]
    t["r_qn2"] = t["r_rstd"]
    t["r_esum"] = t["r_sum"]
    t["r_recip"] = t["r_sumsq"]
    sb("r_meanbf", [1, T], BF16)
    sb("r_rstdbf", [1, T], BF16)
    sb("r_recipbf", [1, 2 * QB], BF16)
    sb("r_qmax2", [1, 2 * QH], F32)
    sb("r_lnq", [1, 2 * QH], F32)
    sb("r_nbbf", [1, 2 * QH], BF16)
    sb("r_bx", [1, 10 * QH], F32)
    sb("r_bx2", [1, 10 * QH], F32)
    sb("r_bxbf", [1, 10 * QH], BF16)
    sb("r_t1", [1, 96], F32)
    sb("r_Bv", [1, 2 * QH], F32)

    t["PS"] = ctx.enter_context(nc.psum_tensor("PS", [128, 4096], F32))

    sems = {}
    for nm in SEMS:
        sems[nm] = ctx.enter_context(nc.semaphore(nm))
    return t, sems, ctx


def build_program(cfg):
    from contextlib import ExitStack
    nc = bass.Bass("TRN2", target_bir_lowering=False)
    tn, sems, ctx = declare(nc, cfg)
    g = G(nc, cfg, tn, sems)
    with ctx:
        with nc.Block() as block:
            def run_walk(kind):
                def f(eng):
                    with ExitStack() as regctx:
                        g.start_walk(kind, eng, regctx)
                        emit_program(g)
                return f
            block.tensor(run_walk("tensor"))
            block.scalar(run_walk("scalar"))
            block.vector(run_walk("vector"))
            block.gpsimd(run_walk("gpsimd"))
            block.sync(run_walk("sync"))
    return nc


_CACHED = {}


def kernel(**inputs):
    cfg = REAL
    key = "real"
    if key not in _CACHED:
        _CACHED[key] = build_program(cfg)
    nc = _CACHED[key]
    in_maps = host_prepare(inputs, cfg)
    res = run_bass_kernel_spmd(nc, in_maps, core_ids=list(range(8)))
    return host_assemble(res.results, cfg)
